# version 11
# baseline (speedup 1.0000x reference)
import math
import numpy as np
import concourse.bass as bass
import concourse.mybir as mybir
from concourse.bass_utils import run_bass_kernel_spmd

F32 = mybir.dt.float32
BF16 = mybir.dt.bfloat16
AX = mybir.AxisListType.X
OP = mybir.AluOpType
AF = mybir.ActivationFunctionType

LN_EPS = 1e-5
CLIP = 8.0
MASKMUL = -1e27
ARENA_COLS = 45056


def alibi_slopes(n_heads, bias_max=8.0):
    next_pow2 = 2 ** int(np.ceil(np.log2(n_heads)))
    m = np.arange(1, next_pow2 + 1, dtype=np.float32) * (bias_max / next_pow2)
    slopes = 1.0 / np.power(2.0, m)
    if next_pow2 != n_heads:
        slopes = np.concatenate([slopes[1::2], slopes[::2]])[:n_heads]
    return slopes.astype(np.float32)


class Cfg:
    def __init__(self, D=4096, S=4096, B=2, H=32, NE=16, NR=64):
        self.D, self.S, self.B, self.H = D, S, B, H
        self.DH = 128
        assert H * self.DH == D
        self.NC = 8
        self.R4 = self.NC // B
        self.TOK = B * S // self.NC
        self.NT = self.TOK // 128
        self.KD = D // 128
        self.CH = 512
        self.CHT = min(512, self.TOK)
        self.NTH = self.TOK // self.CHT
        self.TPT = self.CHT // 128
        self.NQC = 3 * D // self.CH
        self.NDC = D // self.CH
        self.NBC = D // self.CH
        self.NE, self.NR = NE, NR
        self.ER = NE * NR
        self.NERC = self.ER // 128
        self.SC = S // self.CH
        self.KB = S // 128
        self.HQ = min(2, H // self.R4)
        self.NHG = H // self.HQ


class Sem:
    def __init__(self, nc, name):
        self.h = nc.alloc_semaphore(name=name)
        self.n = 0


def build_program(c):
    nc = bass.Bass(trn_type="TRN2", target_bir_lowering=False)

    xsh = nc.dram_tensor("xsh", [c.TOK, c.D], F32, kind="ExternalInput").ap()
    qpos = nc.dram_tensor("qpos", [c.TOK, 1], F32, kind="ExternalInput").ap()
    slv = nc.dram_tensor("slv", [128, c.H // c.R4], F32, kind="ExternalInput").ap()
    wqkvT = nc.dram_tensor("wqkvT", [c.D, 3 * c.D], F32, kind="ExternalInput").ap()
    woutT = nc.dram_tensor("woutT", [c.D, c.D], F32, kind="ExternalInput").ap()
    basewT = nc.dram_tensor("basewT", [c.D, c.D], F32, kind="ExternalInput").ap()
    baseb = nc.dram_tensor("baseb", [1, c.D], F32, kind="ExternalInput").ap()
    aT = nc.dram_tensor("aT", [c.D, c.NR], F32, kind="ExternalInput").ap()
    routerT = nc.dram_tensor("routerT", [c.D, c.NE], F32, kind="ExternalInput").ap()
    bflat = nc.dram_tensor("bflat", [c.ER, c.D], F32, kind="ExternalInput").ap()
    qlnw = nc.dram_tensor("qlnw", [1, c.D], F32, kind="ExternalInput").ap()
    qlnb = nc.dram_tensor("qlnb", [1, c.D], F32, kind="ExternalInput").ap()
    klnw = nc.dram_tensor("klnw", [1, c.D], F32, kind="ExternalInput").ap()
    klnb = nc.dram_tensor("klnb", [1, c.D], F32, kind="ExternalInput").ap()
    y = nc.dram_tensor("y", [c.TOK, c.D], F32, kind="ExternalOutput").ap()

    xnf_s = nc.dram_tensor("xnf_s", [c.D, c.TOK], F32).ap()
    qs_s = nc.dram_tensor("qs_s", [c.TOK, c.D], F32).ap()
    ks_s = nc.dram_tensor("ks_s", [c.TOK, c.D], F32).ap()
    HPN = c.H // c.NC
    QKR = 2 * HPN * 128
    a2aqk_i = nc.dram_tensor("a2aqk_i", [c.NC * QKR, c.TOK], F32).ap()
    a2aqk_o = nc.dram_tensor("a2aqk_o", [c.NC * QKR, c.TOK], F32).ap()
    a2av_i = nc.dram_tensor("a2av_i", [c.NC * HPN * c.TOK, c.DH], F32).ap()
    a2av_o = nc.dram_tensor("a2av_o", [c.NC * HPN * c.TOK, c.DH], F32).ap()
    a2aao_i = nc.dram_tensor("a2aao_i", [c.NC * HPN * 128, c.TOK], F32).ap()
    a2aao_o = nc.dram_tensor("a2aao_o", [c.NC * HPN * 128, c.TOK], F32).ap()
    base_s = nc.dram_tensor("base_s", [c.TOK, c.D], F32).ap()
    g_s = nc.dram_tensor("g_s", [c.TOK, c.D], F32).ap()
    wqkvT_h = nc.dram_tensor("wqkvT_h", [c.D, 3 * c.D], BF16).ap()
    wqkvT_l = nc.dram_tensor("wqkvT_l", [c.D, 3 * c.D], BF16).ap()
    basewT_h = nc.dram_tensor("basewT_h", [c.D, c.D], BF16).ap()
    basewT_l = nc.dram_tensor("basewT_l", [c.D, c.D], BF16).ap()
    woutT_h = nc.dram_tensor("woutT_h", [c.D, c.D], BF16).ap()
    woutT_l = nc.dram_tensor("woutT_l", [c.D, c.D], BF16).ap()
    bflat_h = nc.dram_tensor("bflat_h", [c.ER, c.D], BF16).ap()
    bflat_l = nc.dram_tensor("bflat_l", [c.ER, c.D], BF16).ap()

    ident_d = nc.inline_tensor(np.eye(128, dtype=np.float32), name="identc").ap()
    kpos_d = nc.inline_tensor(
        np.broadcast_to(np.arange(c.S, dtype=np.float32), (128, c.S)).copy(),
        name="kposc").ap()
    sel_np = np.zeros((c.NE, c.NERC * 128), np.float32)
    for _ecc in range(c.NERC):
        for _m in range(128):
            sel_np[(_ecc * 128 + _m) // c.NR, _ecc * 128 + _m] = 1.0
    sel2_d = nc.inline_tensor(sel_np, name="sel2c").ap()
    ones1_d = nc.inline_tensor(np.ones((1, 128), np.float32), name="ones1c").ap()
    negq_np = -(np.arange(c.KB, dtype=np.float32)[None, :] * 128
                + np.arange(128, dtype=np.float32)[:, None])
    negq_d = nc.inline_tensor(np.ascontiguousarray(negq_np), name="negqc").ap()
    dg_np = (np.arange(128, dtype=np.float32)[None, :]
             - np.arange(128, dtype=np.float32)[:, None])
    dg_np = np.where(dg_np <= 0, dg_np, np.float32(-1e30))
    dg_d = nc.inline_tensor(np.ascontiguousarray(dg_np), name="dgmask").ap()
    slopes = alibi_slopes(c.H)

    psall = nc.alloc_psum_tensor("psall", [128, 8 * 512], F32).ap()

    def bank(i):
        return psall[:, i * 512:(i + 1) * 512]

    arena = nc.alloc_sbuf_tensor("arena", [128, ARENA_COLS], F32).ap()
    amark = [0]

    def alloc(cols):
        ap = arena[:, amark[0]:amark[0] + cols]
        amark[0] += cols
        assert amark[0] <= ARENA_COLS, f"arena overflow {amark[0]}"
        return ap

    identt = alloc(128)
    onest = alloc(128)
    dgt = alloc(128)
    selt = alloc(c.NERC * 128)
    basebt = alloc(c.D)
    gqs, gqb, gks, gkb = alloc(c.H), alloc(c.H), alloc(c.H), alloc(c.H)
    qpost = alloc(c.NT)
    negqp = alloc(c.NT)
    negqc = alloc(c.KB)
    slvt = alloc(c.H // c.R4)
    sm = alloc(16 * 36)
    epst = alloc(16)


    def smx(i):
        return sm[:, i * 16:i * 16 + 1]
    PERSIST0 = amark[0]
    logits = alloc(c.NT * c.NE)
    rh2 = alloc(c.TOK)
    combf = alloc(c.TOK)
    atc = alloc(c.KD * c.NR)
    rtc = alloc(c.KD * c.NE)

    def S_(nm):
        return Sem(nc, nm)

    def slots(nm, n):
        return [S_(f"{nm}{i}") for i in range(n)]

    s_sq, s_vst, s_sct, s_xn = S_("sq"), S_("vst"), S_("sct"), S_("xn")
    s_tr0, s_ev0, s_wfr, s_mm1, s_ev1 = (S_(n) for n in ("tr0", "ev0", "wfr", "mm1", "ev1"))
    s_sq2, s_vst2, s_sct2, s_qn, s_tr2, s_ev2 = s_sq, s_vst, s_sct, s_xn, s_tr0, s_ev0
    s_cc, s_ffv, s_ffs, s_trc, s_cbm, s_cbv = (S_(n) for n in
        ("cc", "ffv", "ffs", "trc", "cbm", "cbv"))
    s_bfr, s_dev, s_sil = (S_(n) for n in ("bfr", "dev", "sil"))
    s_dbs, s_dbv, s_smm, s_sev, s_mx, s_e1, s_rs, s_e2 = (S_(n) for n in
        ("dbs", "dbv", "smm", "sev", "mx", "e1", "rs", "e2"))
    s_ptm, s_ptv, s_pv, s_aev, s_wv1 = (S_(n) for n in
        ("ptm", "ptv", "pv", "aev", "wv1"))
    s_ini = S_("ini")
    ldx = slots("ldx", 2)
    st0 = slots("st0", 2)
    ldxn = slots("ldxn", 2)
    s_ldpre = S_("ldpre")
    ldw = slots("ldw", 6)
    st1 = slots("st1", 6)
    lqk = slots("lqk", 2)
    st2 = slots("st2", 8)
    ldb = ldw
    bst = slots("bst", 2)
    st3 = slots("st3", 2)
    s_dup = S_("dup")
    s_loc = S_("loc")
    s_kv = S_("kv")
    qfl = slots("qfl", 2)
    ast = slots("ast", 2)
    lat = slots("lat", 2)
    gx = slots("gx", 2)
    yst = slots("yst", 2)
    spl = slots("spl", 4)
    sps = slots("sps", 4)
    s_sphi = S_("sphi")
    s_splo = S_("splo")
    s_splc = S_("splc")
    ldwl = slots("ldwl", 6)
    s_xsp = S_("xsp")
    s_csp = s_xsp
    s_asp = s_xsp

    ENGS = ("sync", "gpsimd", "tensor", "vector", "scalar")
    ops = {e: [] for e in ENGS}

    def em(eng, fn):
        ops[eng].append(fn)

    def run_phase():
        nonlocal ops
        with nc.Block() as b:
            for e in ENGS:
                if ops[e]:
                    getattr(b, e)(lambda eng, fns=tuple(ops[e]): [f(eng) for f in fns])
        nc.all_engine_barrier()
        ops = {e: [] for e in ENGS}

    def W(*pairs):
        frozen = tuple((s.h, v) for (s, v) in pairs if v > 0)

        def apply(e):
            for h, v in frozen:
                e.wait_ge(h, v)
        return apply

    def DMA(eng, out, in_, sem, *waits, slow=False):
        w = W(*waits)

        def f(e, out=out, in_=in_):
            w(e)
            e.dma_start(out=out, in_=in_,
                        allow_slow_non_contiguous=slow).then_inc(sem.h, 16)
        em(eng, f)
        sem.n += 16
        return sem.n

    def quiesce(eng, sems):
        pairs = tuple((sx.h, sx.n) for sx in sems if sx.n > 0)

        def f(e):
            for h, v in pairs:
                e.wait_ge(h, v)
        em(eng, f)

    DMA("sync", identt, ident_d, s_ini)
    DMA("sync", dgt, dg_d, s_ini)
    DMA("sync", selt[0:c.NE, :], sel2_d, s_ini)
    DMA("sync", onest[0:1, :], ones1_d, s_ini)
    DMA("sync", basebt[0:1, :], baseb, s_ini)
    for dst, src in ((gqs, qlnw), (gqb, qlnb), (gks, klnw), (gkb, klnb)):
        DMA("sync", dst, src[0, :].rearrange("(h d) -> d h", d=128), s_ini, slow=True)
    DMA("sync", qpost, qpos[:, 0].rearrange("(t p) -> p t", p=128), s_ini, slow=True)
    DMA("sync", negqc, negq_d, s_ini)
    DMA("sync", slvt, slv, s_ini)

    def f_initv(e, w=W((s_ini, s_ini.n))):
        w(e)
        isq = 1.0 / math.sqrt(c.DH)
        e.tensor_scalar(gqs, gqs, isq, None, OP.mult)
        e.drain()
        e.tensor_scalar(gqb, gqb, isq, None, OP.mult)
        e.drain()
        e.tensor_scalar(negqp, qpost, -1.0, None, OP.mult)
        e.drain()
        e.memset(epst[:, 0:1], LN_EPS)
    em("vector", f_initv)
    quiesce("sync", [s_ini])
    run_phase()

    am0 = amark[0]
    xa = [alloc(c.D), alloc(c.D)]
    xnb = [alloc(c.D), alloc(c.D)]
    sqs = alloc(c.D)
    stg0 = [alloc(512), alloc(512)]
    NG0 = c.KD // 4

    for i in range(c.NT):
        sl = i % 2
        DMA("sync", xa[sl], xsh[i * 128:(i + 1) * 128, :], ldx[sl],
            (s_xn, max(0, s_xn.n - 1)), (s_sq, max(0, s_sq.n - 1)))

        def f_sc(e, sl=sl, w=W((ldx[sl], ldx[sl].n))):
            w(e)
            e.activation(sqs, xa[sl], AF.Square,
                         accum_out=smx(2 + sl)).then_inc(s_sq.h, 1)
        em("scalar", f_sc)
        s_sq.n += 1

        def f_v1(e, sl=sl, w=W((ldx[sl], ldx[sl].n), (s_sq, s_sq.n))):
            w(e)
            e.reduce_sum(smx(0 + sl), xa[sl], axis=AX)
            e.drain()
            e.tensor_scalar(smx(4 + sl), smx(0 + sl), 1.0 / c.D, None, OP.mult)
            e.tensor_scalar(smx(6 + sl), smx(2 + sl), 1.0 / c.D, None, OP.mult)
            e.drain()
            e.tensor_tensor(smx(8 + sl), smx(4 + sl), smx(4 + sl), OP.mult)
            e.drain()
            e.tensor_tensor(smx(6 + sl), smx(6 + sl), smx(8 + sl),
                            OP.subtract).then_inc(s_vst.h, 1)
        em("vector", f_v1)
        s_vst.n += 1

        def f_sc2(e, sl=sl, w=W((s_vst, s_vst.n))):
            w(e)
            e.activation(smx(10 + sl), smx(6 + sl), AF.Sqrt,
                         bias=epst[:, 0:1]).then_inc(s_sct.h, 1)
        em("scalar", f_sc2)
        s_sct.n += 1

        def f_v2(e, sl=sl, w=W((s_sct, s_sct.n), (s_tr0, max(0, (i - 1) * NG0)))):
            w(e)
            e.reciprocal(smx(12 + sl), smx(10 + sl))
            e.drain()
            e.tensor_scalar(xnb[sl], xa[sl], smx(4 + sl), smx(12 + sl),
                            OP.subtract, OP.mult).then_inc(s_xn.h, 1)
        em("vector", f_v2)
        s_xn.n += 1

        for g in range(NG0):
            def f_tr(e, sl=sl, g=g, w=W((s_xn, s_xn.n), (s_ev0, max(0, s_ev0.n - 1)))):
                w(e)
                for q in range(4):
                    kd = g * 4 + q
                    ins = e.transpose(bank(g % 2)[:, q * 128:(q + 1) * 128],
                                      xnb[sl][:, kd * 128:(kd + 1) * 128], identt)
                ins.then_inc(s_tr0.h, 1)
            em("tensor", f_tr)
            s_tr0.n += 1

            def f_ev(e, g=g, w=W((s_tr0, s_tr0.n), (st0[g % 2], st0[g % 2].n))):
                w(e)
                e.tensor_copy(stg0[g % 2], bank(g % 2)).then_inc(s_ev0.h, 1)
            em("vector", f_ev)
            s_ev0.n += 1

            for q in range(4):
                kd = g * 4 + q
                DMA("gpsimd", xnf_s[kd * 128:(kd + 1) * 128, i * 128:(i + 1) * 128],
                    stg0[g % 2][:, q * 128:(q + 1) * 128], st0[g % 2],
                    (s_ev0, s_ev0.n) if q == 0 else (s_ev0, 0))
    SPW = 512
    wtmp = [None] * 4
    lo_f = [None] * 4
    hi_b = [None] * 4
    lo_b = [None] * 4

    def alloc_split_bufs(n):
        for i in range(n):
            wtmp[i] = alloc(SPW)
            lo_f[i] = alloc(SPW)
            hi_b[i] = alloc(SPW // 2)[:, :].bitcast(BF16)
            lo_b[i] = alloc(SPW // 2)[:, :].bitcast(BF16)

    def emit_split(jobs, nsl):
        spc = [0]
        for srcT, hiT, loT, RR, CC in jobs:
            for r0 in range(RR // 128):
                for c0 in range(CC // SPW):
                    sl = spc[0] % nsl
                    spc[0] += 1
                    DMA("sync", wtmp[sl], srcT[r0 * 128:(r0 + 1) * 128,
                                               c0 * SPW:(c0 + 1) * SPW],
                        spl[sl], (s_splo, max(0, s_splo.n - (nsl - 1))))

                    def f_hi(e, sl=sl, w=W((spl[sl], spl[sl].n),
                                           (s_splo, max(0, s_splo.n - (nsl - 1))),
                                           (sps[sl], sps[sl].n))):
                        w(e)
                        e.tensor_copy(hi_b[sl], wtmp[sl])
                        e.drain()
                        e.scalar_tensor_tensor(lo_f[sl], hi_b[sl], -1.0, wtmp[sl],
                                               OP.mult, OP.add)
                        e.drain()
                        e.tensor_copy(lo_b[sl], lo_f[sl]).then_inc(s_splo.h, 1)
                    em("vector", f_hi)
                    s_splo.n += 1

                    DMA("gpsimd", hiT[r0 * 128:(r0 + 1) * 128, c0 * SPW:(c0 + 1) * SPW],
                        hi_b[sl], sps[sl], (s_splo, s_splo.n))
                    DMA("gpsimd", loT[r0 * 128:(r0 + 1) * 128, c0 * SPW:(c0 + 1) * SPW],
                        lo_b[sl], sps[sl], (s_splo, 0))

    sps_finals = [(x, x.n) for x in sps]
    quiesce("sync", ldx + spl)
    quiesce("gpsimd", st0 + sps)
    run_phase()
    amark[0] = am0

    am1 = amark[0]
    xnfh = alloc(c.KD * c.CHT // 2)[:, :].bitcast(BF16)
    xnfl = alloc(c.KD * c.CHT // 2)[:, :].bitcast(BF16)
    xtmp = [alloc(c.CHT), alloc(c.CHT)]
    xlo = [alloc(c.CHT)]
    xlo = [xlo[0], xlo[0]]
    atch = alloc(c.KD * c.NR // 2)[:, :].bitcast(BF16)
    atcl = alloc(c.KD * c.NR // 2)[:, :].bitcast(BF16)
    rtch = alloc(c.KD * c.NE // 2)[:, :].bitcast(BF16)
    rtcl = alloc(c.KD * c.NE // 2)[:, :].bitcast(BF16)
    wsbh = [alloc(256)[:, :].bitcast(BF16) for _ in range(6)]
    wsbl = [alloc(256)[:, :].bitcast(BF16) for _ in range(6)]
    stg1 = [alloc(512) for _ in range(6)]
    NSTG = 6
    TPT = c.TPT
    wfp = [alloc(512), alloc(512)]
    wlo_t = alloc(512)

    chunks = ([("qkv", i) for i in range(c.NQC)] + [("base", i) for i in range(c.NBC)]
              + [("arh", 0), ("rtr", 0)])
    NCHK = len(chunks)
    wtile = 0
    mm1_at_th_end = []
    wfr_at_th_end = []
    st0_finals = [(st0[0], st0[0].n), (st0[1], st0[1].n)]

    for kd in range(c.KD):
        DMA("sync", atc[:, kd * c.NR:(kd + 1) * c.NR], aT[kd * 128:(kd + 1) * 128, :], s_ldpre)
        DMA("sync", rtc[:, kd * c.NE:(kd + 1) * c.NE], routerT[kd * 128:(kd + 1) * 128, :],
            s_ldpre)
    ldpre_fin = s_ldpre.n

    def _chunks(src, hi, lo, total):
        cw = min(512, c.CHT, total)
        return [(src[:, i * cw:(i + 1) * cw], hi[:, i * cw:(i + 1) * cw],
                 lo[:, i * cw:(i + 1) * cw]) for i in range(total // cw)]
    spre_pairs = (_chunks(atc, atch, atcl, c.KD * c.NR)
                  + _chunks(rtc, rtch, rtcl, c.KD * c.NE))
    for fsrc, fhi, flo in spre_pairs:
        def f_spre(e, fsrc=fsrc, fhi=fhi, w=W((s_ldpre, ldpre_fin),
                                              (s_splo, max(0, s_splo.n)))):
            w(e)
            e.activation(fhi, fsrc, AF.Identity).then_inc(s_sphi.h, 1)
        em("scalar", f_spre)
        s_sphi.n += 1

        cw_ = fsrc.shape[1]

        def f_spre2(e, fsrc=fsrc, fhi=fhi, cw_=cw_, w=W((s_sphi, s_sphi.n),
                                                        (s_splc, max(0, s_splc.n)))):
            w(e)
            e.scalar_tensor_tensor(xlo[0][:, 0:cw_], fhi, -1.0, fsrc,
                                   OP.mult, OP.add).then_inc(s_splo.h, 1)
        em("vector", f_spre2)
        s_splo.n += 1

        def f_spre3(e, flo=flo, cw_=cw_, w=W((s_splo, s_splo.n))):
            w(e)
            e.copy(flo, xlo[0][:, 0:cw_]).then_inc(s_splc.h, 1)
        em("scalar", f_spre3)
        s_splc.n += 1
    spre_done = s_splc.n

    suse = 0
    xspc = 0
    wtile_th1_first = [True]
    for th in range(c.NTH):
        for kd in range(c.KD):
            sx = kd % 2
            extra = (st0_finals + [(s_mm1, mm1_at_th_end[th - 1] if th > 0 else 0),
                                   (s_wfr, wfr_at_th_end[th - 1] if th > 0 else 0),
                                   (s_splc, spre_done)]
                     if kd == 0 else [(s_xsp, max(0, s_xsp.n - 1))])
            DMA("gpsimd", xtmp[sx], xnf_s[kd * 128:(kd + 1) * 128, th * c.CHT:(th + 1) * c.CHT],
                ldxn[sx], *extra)

            def f_xh(e, sx=sx, kd=kd, w=W((ldxn[sx], ldxn[sx].n), (s_xsp, max(0, s_xsp.n - 1)))):
                w(e)
                e.activation(xnfh[:, kd * c.CHT:(kd + 1) * c.CHT], xtmp[sx],
                             AF.Identity).then_inc(s_sphi.h, 1)
            em("scalar", f_xh)
            s_sphi.n += 1

            def f_xl(e, sx=sx, kd=kd, w=W((s_sphi, s_sphi.n), (s_xsp, s_xsp.n))):
                w(e)
                e.scalar_tensor_tensor(xlo[sx], xnfh[:, kd * c.CHT:(kd + 1) * c.CHT],
                                       -1.0, xtmp[sx], OP.mult, OP.add).then_inc(s_splo.h, 1)
            em("vector", f_xl)
            s_splo.n += 1

            def f_xc(e, sx=sx, kd=kd, w=W((s_splo, s_splo.n))):
                w(e)
                e.copy(xnfl[:, kd * c.CHT:(kd + 1) * c.CHT], xlo[sx]).then_inc(s_xsp.h, 1)
            em("scalar", f_xc)
            s_xsp.n += 1
        ldxn_here = s_xsp.n

        for ci, (kind, ec) in enumerate(chunks):
            cgi = th * NCHK + ci
            par = (cgi % 2) * 4
            ev_guard = (cgi - 2 + 1) * TPT if cgi >= 2 else 0

            if kind in ("qkv", "base"):
                src0 = wqkvT if kind == "qkv" else basewT
                srch = wqkvT_h if kind == "qkv" else basewT_h
                srcl = wqkvT_l if kind == "qkv" else basewT_l
                for kd in range(c.KD):
                    sl_w = wtile % 6
                    if th == 0:
                        sf = wtile % 2
                        DMA("sync", wfp[sf],
                            src0[kd * 128:(kd + 1) * 128, ec * 512:ec * 512 + 512],
                            spl[sf], (s_splo, max(0, s_splo.n - 1)))

                        def f_ws(e, sf=sf, sl_w=sl_w,
                                 w=W((spl[sf], spl[sf].n), (s_wfr, max(0, wtile - 5)),
                                     (sps[sl_w % 4], sps[sl_w % 4].n))):
                            w(e)
                            e.tensor_copy(wsbh[sl_w], wfp[sf])
                            e.drain()
                            e.scalar_tensor_tensor(wlo_t, wsbh[sl_w], -1.0, wfp[sf],
                                                   OP.mult, OP.add)
                            e.drain()
                            e.tensor_copy(wsbl[sl_w], wlo_t).then_inc(s_splo.h, 1)
                        em("vector", f_ws)
                        s_splo.n += 1
                        DMA("gpsimd",
                            srch[kd * 128:(kd + 1) * 128, ec * 512:ec * 512 + 512],
                            wsbh[sl_w], sps[sl_w % 4], (s_splo, s_splo.n))
                        DMA("gpsimd",
                            srcl[kd * 128:(kd + 1) * 128, ec * 512:ec * 512 + 512],
                            wsbl[sl_w], sps[sl_w % 4], (s_splo, 0))
                        wt_w = ((s_splo, s_splo.n), (ldw[sl_w], 0))
                    else:
                        extra_w = [(x, x.n) for x in sps] if wtile_th1_first[0] else []
                        wtile_th1_first[0] = False
                        DMA("sync", wsbh[sl_w],
                            srch[kd * 128:(kd + 1) * 128, ec * 512:ec * 512 + 512],
                            ldw[sl_w], (s_wfr, max(0, wtile - 5)), *extra_w)
                        DMA("sync", wsbl[sl_w],
                            srcl[kd * 128:(kd + 1) * 128, ec * 512:ec * 512 + 512],
                            ldwl[sl_w])
                        wt_w = ((ldw[sl_w], ldw[sl_w].n), (ldwl[sl_w], ldwl[sl_w].n))
                    wtile += 1

                    def f_mm(e, kd=kd, par=par, sl_w=sl_w, kind=kind,
                             w=W(*wt_w,
                                 (s_xsp, ldxn_here if (kd == 0 and ci == 0) else 0),
                                 (s_ev1, ev_guard if kd == 0 else 0))):
                        w(e)
                        stop = (kd == c.KD - 1) and kind == "qkv"
                        for tt in range(TPT):
                            xh = xnfh[:, kd * c.CHT + tt * 128:kd * c.CHT + tt * 128 + 128]
                            xl = xnfl[:, kd * c.CHT + tt * 128:kd * c.CHT + tt * 128 + 128]
                            e.matmul(bank(par + tt), xh, wsbh[sl_w],
                                     start=(kd == 0 and tt == 0) if False else (kd == 0),
                                     stop=False)
                            e.matmul(bank(par + tt), xl, wsbh[sl_w], start=False, stop=False)
                            ins = e.matmul(bank(par + tt), xh, wsbl[sl_w], start=False,
                                           stop=stop)
                        ins.then_inc(s_wfr.h, 1)
                    em("tensor", f_mm)
                    s_wfr.n += 1

                if kind == "base":
                    def f_bias(e, par=par, ec=ec):
                        for tt in range(TPT):
                            ins = e.matmul(bank(par + tt), onest[0:1, 0:128],
                                           basebt[0:1, ec * 512:ec * 512 + 512],
                                           start=False, stop=True)
                        ins.then_inc(s_mm1.h, 1)
                    em("tensor", f_bias)
                    s_mm1.n += 1
                done_w = (s_mm1, s_mm1.n) if kind == "base" else (s_wfr, s_wfr.n)

                for tt in range(TPT):
                    sl1 = suse % NSTG

                    def f_ev1(e, par=par, tt=tt, kind=kind, sl1=sl1,
                              w=W(done_w, (st1[sl1], st1[sl1].n))):
                        w(e)
                        if kind == "qkv":
                            e.tensor_scalar(stg1[sl1], bank(par + tt), CLIP, -CLIP,
                                            OP.min, OP.max).then_inc(s_ev1.h, 1)
                        else:
                            e.tensor_copy(stg1[sl1], bank(par + tt)).then_inc(s_ev1.h, 1)
                    em("vector", f_ev1)
                    s_ev1.n += 1

                    trow = th * c.CHT + tt * 128
                    if kind == "qkv":
                        if ec < c.NDC:
                            DMA("gpsimd", qs_s[trow:trow + 128, ec * 512:(ec + 1) * 512],
                                stg1[sl1], st1[sl1], (s_ev1, s_ev1.n))
                        elif ec < 2 * c.NDC:
                            DMA("gpsimd",
                                ks_s[trow:trow + 128, (ec - c.NDC) * 512:(ec - c.NDC + 1) * 512],
                                stg1[sl1], st1[sl1], (s_ev1, s_ev1.n))
                        else:
                            for hb in range(4):
                                h = (ec - 2 * c.NDC) * 4 + hb
                                vrow = ((h // HPN) * HPN + (h % HPN)) * c.TOK + trow
                                DMA("gpsimd", a2av_i[vrow:vrow + 128, :],
                                    stg1[sl1][:, hb * 128:(hb + 1) * 128], st1[sl1],
                                    (s_ev1, s_ev1.n) if hb == 0 else (s_ev1, 0))
                    else:
                        DMA("gpsimd", base_s[trow:trow + 128, ec * 512:(ec + 1) * 512],
                            stg1[sl1], st1[sl1], (s_ev1, s_ev1.n))
                    suse += 1

            elif kind == "arh":
                def f_arh(e, par=par,
                          w=W((s_splc, spre_done), (s_xsp, ldxn_here), (s_ev1, ev_guard))):
                    w(e)
                    for kd in range(c.KD):
                        ah = atch[:, kd * c.NR:(kd + 1) * c.NR]
                        al = atcl[:, kd * c.NR:(kd + 1) * c.NR]
                        xh = xnfh[:, kd * c.CHT:(kd + 1) * c.CHT]
                        xl = xnfl[:, kd * c.CHT:(kd + 1) * c.CHT]
                        e.matmul(bank(par)[0:c.NR, 0:c.CHT], ah, xh,
                                 start=(kd == 0), stop=False)
                        e.matmul(bank(par)[0:c.NR, 0:c.CHT], al, xh, start=False, stop=False)
                        ins = e.matmul(bank(par)[0:c.NR, 0:c.CHT], ah, xl, start=False,
                                       stop=(kd == c.KD - 1))
                    ins.then_inc(s_mm1.h, 1)
                em("tensor", f_arh)
                s_mm1.n += 1

                def f_evarh(e, th=th, par=par, w=W((s_mm1, s_mm1.n))):
                    w(e)
                    e.tensor_copy(rh2[0:c.NR, th * c.CHT:(th + 1) * c.CHT],
                                  bank(par)[0:c.NR, 0:c.CHT]).then_inc(s_ev1.h, TPT)
                em("vector", f_evarh)
                s_ev1.n += TPT
                suse += TPT

            else:
                def f_rtr(e, par=par,
                          w=W((s_splc, spre_done), (s_xsp, ldxn_here), (s_ev1, ev_guard))):
                    w(e)
                    for tt in range(TPT):
                        for kd in range(c.KD):
                            xh = xnfh[:, kd * c.CHT + tt * 128:kd * c.CHT + tt * 128 + 128]
                            xl = xnfl[:, kd * c.CHT + tt * 128:kd * c.CHT + tt * 128 + 128]
                            rh_ = rtch[:, kd * c.NE:(kd + 1) * c.NE]
                            rl_ = rtcl[:, kd * c.NE:(kd + 1) * c.NE]
                            e.matmul(bank(par + tt)[:, 0:c.NE], xh, rh_,
                                     start=(kd == 0), stop=False)
                            e.matmul(bank(par + tt)[:, 0:c.NE], xl, rh_,
                                     start=False, stop=False)
                            ins = e.matmul(bank(par + tt)[:, 0:c.NE], xh, rl_,
                                           start=False, stop=(kd == c.KD - 1))
                    ins.then_inc(s_mm1.h, 1)
                em("tensor", f_rtr)
                s_mm1.n += 1

                def f_evrtr(e, th=th, par=par, w=W((s_mm1, s_mm1.n))):
                    w(e)
                    for tt in range(TPT):
                        g = th * TPT + tt
                        ins = e.tensor_copy(logits[:, g * c.NE:(g + 1) * c.NE],
                                            bank(par + tt)[:, 0:c.NE])
                    ins.then_inc(s_ev1.h, TPT)
                em("vector", f_evrtr)
                s_ev1.n += TPT
                suse += TPT
        mm1_at_th_end.append(s_mm1.n)
        wfr_at_th_end.append(s_wfr.n)
    quiesce("sync", [s_ldpre] + ldw + ldwl + spl)
    quiesce("gpsimd", ldxn + st1 + sps)
    run_phase()
    amark[0] = am1
    st1_finals = [(s, s.n) for s in st1]

    am2 = amark[0]
    qa = [alloc(c.D), alloc(c.D)]
    ka = [alloc(c.D), alloc(c.D)]
    qn = qa
    kn = ka
    sq2 = alloc(c.D)
    stg2 = [alloc(128) for _ in range(8)]
    suse2 = 0

    def f_agv(e, w=W(*st1_finals)):
        w(e)
        from concourse.aot_env import allow_ici_all_to_all
        with allow_ici_all_to_all():
            e.collective_compute("AllToAll", OP.bypass,
                                 replica_groups=[list(range(c.NC))],
                                 ins=[a2av_i.opt()],
                                 outs=[a2av_o.opt()]).then_inc(s_cc.h, 1)
    em("gpsimd", f_agv)
    s_cc.n += 1

    for i in range(c.NT):
        sl = i % 2
        extra = (st1_finals if i == 0 else []) + [(s_qn, max(0, s_qn.n - 2)),
                                                   (s_tr2, max(0, (i - 1) * 2 * c.H))]
        DMA("gpsimd", qa[sl], qs_s[i * 128:(i + 1) * 128, :], lqk[sl], *extra)
        DMA("gpsimd", ka[sl], ks_s[i * 128:(i + 1) * 128, :], lqk[sl])
        for which, buf, nbuf, gw, gb in (("q", qa, qn, gqs, gqb), ("k", ka, kn, gks, gkb)):
            o = 0 if which == "q" else 16

            def f_sc(e, sl=sl, buf=buf, o=o, w=W((lqk[sl], lqk[sl].n))):
                w(e)
                e.activation(sq2, buf[sl], AF.Square,
                             accum_out=smx(o + 2 + sl)).then_inc(s_sq2.h, 1)
            em("scalar", f_sc)
            s_sq2.n += 1

            def f_v1(e, sl=sl, buf=buf, o=o, w=W((lqk[sl], lqk[sl].n), (s_sq2, s_sq2.n))):
                w(e)
                e.reduce_sum(smx(o + 0 + sl), buf[sl], axis=AX)
                e.drain()
                e.tensor_scalar(smx(o + 4 + sl), smx(o + 0 + sl),
                                1.0 / c.D, None, OP.mult)
                e.tensor_scalar(smx(o + 6 + sl), smx(o + 2 + sl),
                                1.0 / c.D, None, OP.mult)
                e.drain()
                e.tensor_tensor(smx(o + 8 + sl), smx(o + 4 + sl),
                                smx(o + 4 + sl), OP.mult)
                e.drain()
                e.tensor_tensor(smx(o + 6 + sl), smx(o + 6 + sl),
                                smx(o + 8 + sl), OP.subtract).then_inc(s_vst2.h, 1)
            em("vector", f_v1)
            s_vst2.n += 1

            def f_sc2(e, sl=sl, o=o, w=W((s_vst2, s_vst2.n))):
                w(e)
                e.activation(smx(o + 10 + sl), smx(o + 6 + sl),
                             AF.Sqrt, bias=epst[:, 0:1]).then_inc(s_sct2.h, 1)
            em("scalar", f_sc2)
            s_sct2.n += 1

            def f_v2(e, sl=sl, buf=buf, nbuf=nbuf, o=o,
                     w=W((s_sct2, s_sct2.n), (s_tr2, max(0, (i - 1) * 2 * c.H)))):
                w(e)
                e.reciprocal(smx(o + 12 + sl), smx(o + 10 + sl))
                e.drain()
                e.tensor_scalar(nbuf[sl], buf[sl], smx(o + 4 + sl),
                                smx(o + 12 + sl),
                                OP.subtract, OP.mult).then_inc(s_qn.h, 1)
            em("vector", f_v2)
            s_qn.n += 1

            for h in range(c.H):
                bnk = (h % 4) + (0 if sl == 0 else 4)

                def f_tr(e, sl=sl, nbuf=nbuf, h=h, bnk=bnk,
                         w=W((s_qn, s_qn.n), (s_ev2, max(0, s_ev2.n - 3)))):
                    w(e)
                    e.transpose(bank(bnk)[:, 0:128], nbuf[sl][:, h * 128:(h + 1) * 128],
                                identt).then_inc(s_tr2.h, 1)
                em("tensor", f_tr)
                s_tr2.n += 1

                s2 = suse2 % 8

                def f_ev(e, h=h, bnk=bnk, gw=gw, gb=gb, s2=s2,
                         w=W((s_tr2, s_tr2.n), (st2[s2], st2[s2].n))):
                    w(e)
                    e.tensor_scalar(stg2[s2], bank(bnk)[:, 0:128],
                                    gw[:, h:h + 1], gb[:, h:h + 1],
                                    OP.mult, OP.add).then_inc(s_ev2.h, 1)
                em("vector", f_ev)
                s_ev2.n += 1
                roff = ((h // HPN) * QKR + (0 if which == "q" else HPN * 128)
                        + (h % HPN) * 128)
                DMA("gpsimd", a2aqk_i[roff:roff + 128, i * 128:(i + 1) * 128],
                    stg2[s2], st2[s2], (s_ev2, s_ev2.n))
                suse2 += 1

    st2_finals = [(s, s.n) for s in st2]

    def f_ag(e, w=W(*st2_finals)):
        w(e)
        from concourse.aot_env import allow_ici_all_to_all
        with allow_ici_all_to_all():
            e.collective_compute("AllToAll", OP.bypass,
                                 replica_groups=[list(range(c.NC))],
                                 ins=[a2aqk_i.opt()],
                                 outs=[a2aqk_o.opt()]).then_inc(s_cc.h, 1)
    em("gpsimd", f_ag)
    s_cc.n += 1
    cc_qkv = s_cc.n
    quiesce("gpsimd", lqk + st2)
    run_phase()
    amark[0] = am2

    csb = alloc(c.NERC * c.TOK)
    csbh = alloc(c.NERC * c.TOK // 2)[:, :].bitcast(BF16)
    csbl = alloc(c.NERC * c.TOK // 2)[:, :].bitcast(BF16)
    clo = alloc(c.CHT)
    wsb3h = [alloc(256)[:, :].bitcast(BF16) for _ in range(6)]
    wsb3l = [alloc(256)[:, :].bitcast(BF16) for _ in range(6)]
    bstg = [alloc(512), alloc(512)]
    t1st = [alloc(512), alloc(512)]
    gstg = [alloc(512), alloc(512)]
    prbf = alloc(64)
    prb2f = alloc(64)
    prb = prbf[:, 0:c.NE]
    prb2 = prb2f[:, 0:c.NE]

    for g in range(c.NT):
        lg = logits[:, g * c.NE:(g + 1) * c.NE]

        def f_v(e, lg=lg, w=W((s_ffs, s_ffs.n), (s_trc, s_trc.n))):
            w(e)
            e.reduce_max(smx(0), lg, axis=AX)
            e.drain()
            e.tensor_scalar(smx(1), smx(0), -1.0, None, OP.mult).then_inc(s_ffv.h, 1)
        em("vector", f_v)
        s_ffv.n += 1

        def f_s(e, lg=lg, w=W((s_ffv, s_ffv.n))):
            w(e)
            e.activation(prb, lg, AF.Exp, bias=smx(1),
                         accum_out=smx(2)).then_inc(s_ffs.h, 1)
        em("scalar", f_s)
        s_ffs.n += 1

        def f_v2(e, w=W((s_ffs, s_ffs.n))):
            w(e)
            e.reciprocal(smx(3), smx(2))
            e.drain()
            e.tensor_scalar(prb, prb, smx(3), None, OP.mult)
            e.drain()
            e.reduce_max(smx(4), prb, axis=AX)
            e.drain()
            e.tensor_scalar(prb2, prb, smx(4), None, OP.is_ge)
            e.drain()
            e.scalar_tensor_tensor(prb2, prb2, -2.0, prb, OP.mult, OP.add)
            e.drain()
            e.reduce_max(smx(5), prb2, axis=AX)
            e.drain()
            e.tensor_tensor(smx(6), smx(4), smx(5), OP.add)
            e.drain()
            e.tensor_scalar(smx(6), smx(6), 1e-6, None, OP.add)
            e.drain()
            e.reciprocal(smx(7), smx(6))
            e.drain()
            e.tensor_tensor(smx(8), smx(4), smx(7), OP.mult)
            e.tensor_tensor(smx(9), smx(5), smx(7), OP.mult)
            e.tensor_scalar(prb2, prb2, smx(5), None, OP.is_ge)
            e.tensor_scalar(prb, prb, smx(4), None, OP.is_ge)
            e.drain()
            e.tensor_scalar(prb, prb, smx(8), None, OP.mult)
            e.drain()
            e.scalar_tensor_tensor(prb, prb2, smx(9), prb, OP.mult,
                                   OP.add).then_inc(s_ffv.h, 1)
        em("vector", f_v2)
        s_ffv.n += 1

        def f_tr(e, g=g, w=W((s_ffv, s_ffv.n), (s_cbv, max(0, s_cbv.n - 1)))):
            w(e)
            e.transpose(bank(g % 2)[0:c.NE, 0:128], prb, identt).then_inc(s_trc.h, 1)
        em("tensor", f_tr)
        s_trc.n += 1

        def f_ev(e, g=g, w=W((s_trc, s_trc.n))):
            w(e)
            e.tensor_copy(combf[0:c.NE, g * 128:(g + 1) * 128],
                          bank(g % 2)[0:c.NE, 0:128]).then_inc(s_cbv.h, 1)
        em("vector", f_ev)
        s_cbv.n += 1

    def f_dup(e, w=W((s_ev1, s_ev1.n))):
        w(e)
        e.dma_start(out=rh2[c.NR:2 * c.NR, :], in_=rh2[0:c.NR, :]).then_inc(s_dup.h, 16)
    em("gpsimd", f_dup)
    s_dup.n += 16

    cbv_base = s_cbv.n
    cb = 0
    for ecc in range(c.NERC):
        for tch in range(c.NTH):
            def f_cb(e, ecc=ecc, tch=tch, cb=cb,
                     w=W((s_cbv, cbv_base + max(0, cb - 1)), (s_dup, 16))):
                w(e)
                e.matmul(bank(2 + cb % 2)[:, 0:c.CHT],
                         selt[0:c.NE, ecc * 128:(ecc + 1) * 128],
                         combf[0:c.NE, tch * c.CHT:(tch + 1) * c.CHT],
                         start=True, stop=True).then_inc(s_cbm.h, 1)
            em("tensor", f_cb)
            s_cbm.n += 1

            def f_cv(e, ecc=ecc, tch=tch, cb=cb, w=W((s_cbm, s_cbm.n))):
                w(e)
                e.tensor_tensor(
                    csb[:, ecc * c.TOK + tch * c.CHT:ecc * c.TOK + (tch + 1) * c.CHT],
                    rh2[:, tch * c.CHT:(tch + 1) * c.CHT],
                    bank(2 + cb % 2)[:, 0:c.CHT], OP.mult).then_inc(s_cbv.h, 1)
            em("vector", f_cv)
            s_cbv.n += 1

            def f_ch(e, ecc=ecc, tch=tch, w=W((s_cbv, s_cbv.n), (s_csp, max(0, s_csp.n - 1)))):
                w(e)
                e.activation(csbh[:, ecc * c.TOK + tch * c.CHT:ecc * c.TOK + (tch + 1) * c.CHT],
                             csb[:, ecc * c.TOK + tch * c.CHT:ecc * c.TOK + (tch + 1) * c.CHT],
                             AF.Identity).then_inc(s_sphi.h, 1)
            em("scalar", f_ch)
            s_sphi.n += 1

            def f_cl(e, ecc=ecc, tch=tch, w=W((s_sphi, s_sphi.n))):
                w(e)
                e.scalar_tensor_tensor(
                    clo, csbh[:, ecc * c.TOK + tch * c.CHT:ecc * c.TOK + (tch + 1) * c.CHT],
                    -1.0, csb[:, ecc * c.TOK + tch * c.CHT:ecc * c.TOK + (tch + 1) * c.CHT],
                    OP.mult, OP.add).then_inc(s_splo.h, 1)
            em("vector", f_cl)
            s_splo.n += 1

            def f_cc2(e, ecc=ecc, tch=tch, w=W((s_splo, s_splo.n))):
                w(e)
                e.copy(csbl[:, ecc * c.TOK + tch * c.CHT:ecc * c.TOK + (tch + 1) * c.CHT],
                       clo).then_inc(s_csp.h, 1)
            em("scalar", f_cc2)
            s_csp.n += 1
            cb += 1
    csp_done = s_csp.n

    btile = 0
    bfp = [alloc(512), alloc(512)]
    blo_t = alloc(512)
    for oc in range(c.NBC):
        for ecc in range(c.NERC):
            sl_b = btile % 6
            sf = btile % 2
            DMA("sync", bfp[sf], bflat[ecc * 128:(ecc + 1) * 128, oc * 512:(oc + 1) * 512],
                spl[sf], (s_splo, max(0, s_splo.n - 1)))

            def f_bs(e, sf=sf, sl_b=sl_b,
                     w=W((spl[sf], spl[sf].n), (s_bfr, max(0, btile - 5)))):
                w(e)
                e.tensor_copy(wsb3h[sl_b], bfp[sf])
                e.drain()
                e.scalar_tensor_tensor(blo_t, wsb3h[sl_b], -1.0, bfp[sf], OP.mult, OP.add)
                e.drain()
                e.tensor_copy(wsb3l[sl_b], blo_t).then_inc(s_splo.h, 1)
            em("vector", f_bs)
            s_splo.n += 1
            bt_w = (s_splo, s_splo.n)
            btile += 1

            def f_dm(e, ecc=ecc, sl_b=sl_b, oc=oc,
                     w=W(bt_w,
                         (s_csp, csp_done if ecc == 0 else 0),
                         (s_dev, oc * c.NT if ecc == 0 else 0))):
                w(e)
                for tt in range(c.NT):
                    chs = csbh[:, ecc * c.TOK + tt * 128:ecc * c.TOK + (tt + 1) * 128]
                    cls = csbl[:, ecc * c.TOK + tt * 128:ecc * c.TOK + (tt + 1) * 128]
                    e.matmul(bank(tt), chs, wsb3h[sl_b], start=(ecc == 0), stop=False)
                    e.matmul(bank(tt), cls, wsb3h[sl_b], start=False, stop=False)
                    ins = e.matmul(bank(tt), chs, wsb3l[sl_b], start=False,
                                   stop=(ecc == c.NERC - 1))
                ins.then_inc(s_bfr.h, 1)
            em("tensor", f_dm)
            s_bfr.n += 1

        for tt in range(c.NT):
            dl = (oc * c.NT + tt) % 2
            extra = st1_finals if (oc == 0 and tt == 0) else []
            DMA("gpsimd", bstg[dl], base_s[tt * 128:(tt + 1) * 128, oc * 512:(oc + 1) * 512],
                bst[dl], (s_dev, max(0, s_dev.n - 1)), *extra)

            def f_dv(e, tt=tt, dl=dl, w=W((s_bfr, s_bfr.n), (bst[dl], bst[dl].n),
                                          (s_sil, max(0, s_sil.n - 1)))):
                w(e)
                e.scalar_tensor_tensor(t1st[dl], bank(tt), 0.5, bstg[dl],
                                       OP.mult, OP.add).then_inc(s_dev.h, 1)
            em("vector", f_dv)
            s_dev.n += 1

            def f_sil(e, dl=dl, w=W((s_dev, s_dev.n), (st3[dl], st3[dl].n))):
                w(e)
                e.activation(gstg[dl], t1st[dl], AF.Silu).then_inc(s_sil.h, 1)
            em("scalar", f_sil)
            s_sil.n += 1
            DMA("gpsimd", g_s[tt * 128:(tt + 1) * 128, oc * 512:(oc + 1) * 512],
                gstg[dl], st3[dl], (s_sil, s_sil.n))

    quiesce("sync", ldb)
    quiesce("gpsimd", bst + st3 + [s_dup])
    run_phase()

    amark[0] = PERSIST0
    kf2 = [alloc(c.S), alloc(c.S)]
    vt2 = [alloc(c.S), alloc(c.S)]
    scb = [alloc(c.S), alloc(c.S)]
    pT = alloc(c.KB * 256)
    pT3 = pT.rearrange("p (b w) -> p b w", w=256)
    slopek = alloc(c.S)
    negqs = alloc(c.KB)
    qtl = [alloc(128), alloc(128)]
    astg = [alloc(256), alloc(256)]
    NTASK = c.B * HPN
    NTB = c.TOK // 128
    NCH = c.KB // 2
    scg = [0]
    sev_bank = [0, 0]
    tb_ctr = [0]
    ptv_bank = [0, 0]
    last_smm = {0: 0, 1: 0}
    ptm_scb = {0: 0, 1: 0}
    e1_after = {0: 0, 1: 0}
    rs_after = {0: 0, 1: 0}
    e2_after = {0: 0, 1: 0}
    aev_after = {0: 0, 1: 0}
    kfree = {0: 0, 1: 0}
    vfree = {0: 0, 1: 0}
    pv_chunk = [0]

    def emit_pt(t, m, ql, e2n):
        qt = 2 * m + ql
        qp = qt % 2
        P = qt + 1
        NG = (P + 3) // 4
        for gk in range(NG):
            nb = min(4, P - gk * 4)
            tb = tb_ctr[0] % 2
            tb_ctr[0] += 1

            def f_pt(e, qp=qp, gk=gk, nb=nb, tb=tb, e2n=e2n,
                     w=W((s_e2, e2n if gk == 0 else 0),
                         (s_ptv, ptv_bank[tb]))):
                w(e)
                for q4 in range(nb):
                    kb = gk * 4 + q4
                    ins = e.transpose(bank(2 + tb)[:, q4 * 128:(q4 + 1) * 128],
                                      scb[qp][:, kb * 128:(kb + 1) * 128], identt)
                ins.then_inc(s_ptm.h, 1)
            em("tensor", f_pt)
            s_ptm.n += 1
            last_g = (ql == 1 and gk == NG - 1)

            def f_ptv(e, ql=ql, gk=gk, nb=nb, tb=tb, m=m, last_g=last_g,
                      w=W((s_ptm, s_ptm.n),
                          (s_pv, pv_chunk[0] if (ql == 0 and gk == 0) else 0))):
                w(e)
                ins = e.tensor_copy(
                    pT3[:, gk * 4:gk * 4 + nb, ql * 128:ql * 128 + 128],
                    bank(2 + tb)[:, 0:nb * 128].rearrange("p (b w) -> p b w",
                                                          w=128))
                if last_g:
                    ins = e.memset(pT[:, (2 * m + 1) * 256:(2 * m + 1) * 256
                                      + 128], 0.0)
                ins.then_inc(s_ptv.h, 1)
            em("vector", f_ptv)
            s_ptv.n += 1
            ptv_bank[tb] = s_ptv.n
        ptm_scb[qp] = s_ptm.n

    def emit_pvtail(t, tp, m, cp):
        NKB = 2 * m + 2

        def f_pv(e, tp=tp, cp=cp, NKB=NKB,
                 w=W((s_ptv, s_ptv.n), (s_aev, aev_after[cp]))):
            w(e)
            for kb in range(NKB):
                ins = e.matmul(bank(4 + cp)[:, 0:256],
                               vt2[tp][:, kb * 128:(kb + 1) * 128],
                               pT[:, kb * 256:(kb + 1) * 256],
                               start=(kb == 0), stop=(kb == NKB - 1))
            ins.then_inc(s_pv.h, 1)
        em("tensor", f_pv)
        s_pv.n += 1
        pv_chunk[0] = s_pv.n

        def f_aev(e, cp=cp, w=W((s_pv, s_pv.n), (ast[cp], ast[cp].n))):
            w(e)
            e.tensor_copy(astg[cp], bank(4 + cp)[:, 0:256]).then_inc(s_aev.h, 1)
        em("vector", f_aev)
        s_aev.n += 1
        aev_after[cp] = s_aev.n
        rpq = (2 * m * 128) // c.TOK
        qc2 = 2 * m * 128 - rpq * c.TOK
        aor = ((t // HPN) * c.R4 + rpq) * HPN * 128 + (t % HPN) * 128
        DMA("gpsimd", a2aao_i[aor:aor + 128, qc2:qc2 + 256], astg[cp],
            ast[cp], (s_aev, s_aev.n))

    sev_task = 0
    for t in range(NTASK):
        tp = t % 2
        hd = t % HPN
        bsel = t // HPN
        first = True
        for rp in range(c.R4):
            snd = bsel * c.R4 + rp
            gw_ = (((s_smm, kfree[tp]), (s_cc, cc_qkv)) if first else ())
            first = False
            krow = snd * QKR + HPN * 128 + hd * 128
            DMA("sync", kf2[tp][:, rp * c.TOK:(rp + 1) * c.TOK],
                a2aqk_o[krow:krow + 128, :], s_kv, *gw_)
        vw_ = ((s_pv, vfree[tp]),)
        for rp in range(c.R4):
            snd = bsel * c.R4 + rp
            for lb in range(NTB):
                vrow = snd * HPN * c.TOK + hd * c.TOK + lb * 128
                vcol = rp * c.TOK + lb * 128
                DMA("sync", vt2[tp][:, vcol:vcol + 128],
                    a2av_o[vrow:vrow + 128, :], s_kv, *vw_)
                vw_ = ()
        kv_here = s_kv.n

        DMA("sync", slopek, kpos_d, s_dbs, (s_sev, sev_task))

        def f_nqs(e, t=t, w=W((s_dbs, s_dbs.n))):
            w(e)
            e.tensor_scalar(slopek, slopek, slvt[:, t:t + 1], None, OP.mult)
            e.tensor_scalar(negqs, negqc, slvt[:, t:t + 1], None,
                            OP.mult).then_inc(s_dbv.h, 1)
        em("vector", f_nqs)
        s_dbv.n += 1
        dbv_task = s_dbv.n

        def emit_qt(m, ql, kv_first, t=t, tp=tp, hd=hd, bsel=bsel,
                    dbv_task=dbv_task, kv_here=kv_here):
            qt = 2 * m + ql
            qp = qt % 2
            PW = (qt + 1) * 128
            SCq = (PW + 511) // 512
            rp = (qt * 128) // c.TOK
            qcol = qt * 128 - rp * c.TOK
            qrow = (bsel * c.R4 + rp) * QKR + hd * 128
            DMA("sync", qtl[qp], a2aqk_o[qrow:qrow + 128, qcol:qcol + 128],
                qfl[qp], (s_smm, last_smm[qp]))
            dstart = qt * 128
            for sc in range(SCq):
                c0 = sc * 512
                wdt = min(512, PW - c0)
                bp = scg[0] % 2
                scg[0] += 1

                def f_smm(e, qp=qp, sc=sc, wdt=wdt, bp=bp, tp=tp,
                          kvw=(kv_here if (kv_first and sc == 0) else 0),
                          w=W((qfl[qp], qfl[qp].n if sc == 0 else 0),
                              (s_sev, sev_bank[bp]))):
                    if kvw:
                        e.wait_ge(s_kv.h, kvw)
                    w(e)
                    e.matmul(bank(bp)[:, 0:wdt], qtl[qp],
                             kf2[tp][:, sc * 512:sc * 512 + wdt],
                             start=True, stop=True).then_inc(s_smm.h, 1)
                em("tensor", f_smm)
                s_smm.n += 1
                w1 = wdt if c0 + wdt <= dstart else max(0, dstart - c0)

                def f_sev(e, qt=qt, qp=qp, t=t, c0=c0, wdt=wdt, w1=w1, bp=bp,
                          w=W((s_smm, s_smm.n), (s_dbv, dbv_task),
                              (s_ptm, ptm_scb[qp] if sc == 0 else 0))):
                    w(e)
                    ins = None
                    if w1 > 0:
                        ins = e.scalar_tensor_tensor(
                            scb[qp][:, c0:c0 + w1], slopek[:, c0:c0 + w1],
                            negqs[:, qt:qt + 1], bank(bp)[:, 0:w1],
                            OP.add, OP.add)
                    if w1 < wdt:
                        ins = e.scalar_tensor_tensor(
                            scb[qp][:, c0 + w1:c0 + wdt], dgt,
                            slvt[:, t:t + 1], bank(bp)[:, w1:wdt],
                            OP.mult, OP.add)
                    ins.then_inc(s_sev.h, 1)
                em("vector", f_sev)
                s_sev.n += 1
                sev_bank[bp] = s_sev.n
            last_smm[qp] = s_smm.n

            def f_mx(e, qp=qp, PW=PW, w=W((s_sev, s_sev.n),
                                          (s_e1, e1_after[qp]))):
                w(e)
                e.reduce_max(smx(20 + qp), scb[qp][:, 0:PW],
                             axis=AX).then_inc(s_mx.h, 1)
            em("vector", f_mx)
            s_mx.n += 1

            def f_e1(e, qp=qp, PW=PW, w=W((s_mx, s_mx.n),
                                          (s_rs, rs_after[qp]))):
                w(e)
                e.activation(smx(22 + qp), smx(20 + qp), AF.Identity,
                             scale=-1.0)
                e.drain()
                e.activation(scb[qp][:, 0:PW], scb[qp][:, 0:PW], AF.Exp,
                             bias=smx(22 + qp),
                             accum_out=smx(24 + qp)).then_inc(s_e1.h, 1)
            em("scalar", f_e1)
            s_e1.n += 1
            e1_after[qp] = s_e1.n

            def f_rs(e, qp=qp, w=W((s_e1, s_e1.n), (s_e2, e2_after[qp]))):
                w(e)
                e.reciprocal(smx(26 + qp), smx(24 + qp)).then_inc(s_rs.h, 1)
            em("vector", f_rs)
            s_rs.n += 1
            rs_after[qp] = s_rs.n

            def f_e2(e, qp=qp, PW=PW, w=W((s_rs, s_rs.n))):
                w(e)
                e.activation(scb[qp][:, 0:PW], scb[qp][:, 0:PW], AF.Identity,
                             scale=smx(26 + qp)).then_inc(s_e2.h, 1)
            em("scalar", f_e2)
            s_e2.n += 1
            e2_after[qp] = s_e2.n
            return s_e2.n

        pend = None
        for m in range(NCH):
            e2q0 = emit_qt(m, 0, kv_first=(m == 0))
            if pend is not None:
                emit_pt(t, pend[0], 1, pend[1])
                emit_pvtail(t, tp, pend[0], pend[0] % 2)
            e2q1 = emit_qt(m, 1, kv_first=False)
            emit_pt(t, m, 0, e2q0)
            pend = (m, e2q1)
        emit_pt(t, pend[0], 1, pend[1])
        emit_pvtail(t, tp, pend[0], pend[0] % 2)
        kfree[tp] = s_smm.n
        vfree[tp] = s_pv.n
        sev_task = s_sev.n

    def f_ao(e, w=W((ast[0], ast[0].n), (ast[1], ast[1].n))):
        w(e)
        from concourse.aot_env import allow_ici_all_to_all
        with allow_ici_all_to_all():
            e.collective_compute("AllToAll", OP.bypass,
                                 replica_groups=[list(range(c.NC))],
                                 ins=[a2aao_i.opt()],
                                 outs=[a2aao_o.opt()]).then_inc(s_cc.h, 1)
    em("gpsimd", f_ao)
    s_cc.n += 1
    cc_ao = s_cc.n
    quiesce("sync", [s_kv, s_dbs] + qfl)
    run_phase()
    amark[0] = PERSIST0

    atnh = alloc(c.KD * c.CHT // 2)[:, :].bitcast(BF16)
    atnl = alloc(c.KD * c.CHT // 2)[:, :].bitcast(BF16)
    atmp = [alloc(c.CHT), alloc(c.CHT)]
    alo = [alloc(c.CHT), alloc(c.CHT)]
    wsb5h = [alloc(256)[:, :].bitcast(BF16) for _ in range(6)]
    wsb5l = [alloc(256)[:, :].bitcast(BF16) for _ in range(6)]
    gld5 = [alloc(512), alloc(512)]
    xld5 = [alloc(512), alloc(512)]
    t1s = [alloc(512), alloc(512)]
    ost = [alloc(512), alloc(512)]
    ast_finals = [(s, s.n) for s in ast]
    st3_finals = [(s, s.n) for s in st3]
    TPT = c.TPT
    wfr5_after = []

    wfp5 = [alloc(512), alloc(512)]
    wlo5 = alloc(512)
    w5_first = [True]
    for th in range(c.NTH):
        for fc in range(c.KD):
            sx = fc % 2
            extra = ([(s_wfr, wfr5_after[th * c.NBC - 1] if th > 0 else 0),
                      (s_cc, cc_ao)]
                     if fc == 0 else [(s_asp, max(0, s_asp.n - 1))])
            DMA("gpsimd", atmp[sx],
                a2aao_o[fc * 128:(fc + 1) * 128, th * c.CHT:(th + 1) * c.CHT],
                lat[sx], *extra)

            def f_ah(e, sx=sx, fc=fc, w=W((lat[sx], lat[sx].n), (s_asp, max(0, s_asp.n - 1)))):
                w(e)
                e.activation(atnh[:, fc * c.CHT:(fc + 1) * c.CHT], atmp[sx],
                             AF.Identity).then_inc(s_sphi.h, 1)
            em("scalar", f_ah)
            s_sphi.n += 1

            def f_al(e, sx=sx, fc=fc, w=W((s_sphi, s_sphi.n))):
                w(e)
                e.scalar_tensor_tensor(alo[sx], atnh[:, fc * c.CHT:(fc + 1) * c.CHT],
                                       -1.0, atmp[sx], OP.mult, OP.add).then_inc(s_splo.h, 1)
            em("vector", f_al)
            s_splo.n += 1

            def f_ac(e, sx=sx, fc=fc, w=W((s_splo, s_splo.n))):
                w(e)
                e.copy(atnl[:, fc * c.CHT:(fc + 1) * c.CHT], alo[sx]).then_inc(s_asp.h, 1)
            em("scalar", f_ac)
            s_asp.n += 1
        lat_here = s_asp.n
        for oc in range(c.NBC):
            cgi5 = th * c.NBC + oc
            par = (cgi5 % 2) * 4
            for fc in range(c.KD):
                sl_w = (cgi5 * c.KD + fc) % 6
                if th == 0:
                    sf = fc % 2
                    DMA("sync", wfp5[sf],
                        woutT[fc * 128:(fc + 1) * 128, oc * 512:(oc + 1) * 512],
                        spl[sf], (s_splo, max(0, s_splo.n - 1)))

                    def f_w5(e, sf=sf, sl_w=sl_w,
                             w=W((spl[sf], spl[sf].n), (s_wfr, max(0, s_wfr.n - 5)),
                                 (sps[sl_w % 4], sps[sl_w % 4].n))):
                        w(e)
                        e.tensor_copy(wsb5h[sl_w], wfp5[sf])
                        e.drain()
                        e.scalar_tensor_tensor(wlo5, wsb5h[sl_w], -1.0, wfp5[sf],
                                               OP.mult, OP.add)
                        e.drain()
                        e.tensor_copy(wsb5l[sl_w], wlo5).then_inc(s_splo.h, 1)
                    em("vector", f_w5)
                    s_splo.n += 1
                    DMA("gpsimd", woutT_h[fc * 128:(fc + 1) * 128, oc * 512:(oc + 1) * 512],
                        wsb5h[sl_w], sps[sl_w % 4], (s_splo, s_splo.n))
                    DMA("gpsimd", woutT_l[fc * 128:(fc + 1) * 128, oc * 512:(oc + 1) * 512],
                        wsb5l[sl_w], sps[sl_w % 4], (s_splo, 0))
                    w5_w = ((s_splo, s_splo.n),)
                else:
                    extra5 = [(x, x.n) for x in sps] if w5_first[0] else []
                    w5_first[0] = False
                    DMA("sync", wsb5h[sl_w],
                        woutT_h[fc * 128:(fc + 1) * 128, oc * 512:(oc + 1) * 512], ldw[sl_w],
                        (s_wfr, max(0, s_wfr.n - 5)), *extra5)
                    DMA("sync", wsb5l[sl_w],
                        woutT_l[fc * 128:(fc + 1) * 128, oc * 512:(oc + 1) * 512],
                        ldwl[sl_w])
                    w5_w = ((ldw[sl_w], ldw[sl_w].n), (ldwl[sl_w], ldwl[sl_w].n))

                def f_wm(e, fc=fc, par=par, sl_w=sl_w,
                         w=W(*w5_w,
                             (s_asp, lat_here if (oc == 0 and fc == 0) else 0),
                             (s_wv1, (cgi5 - 2 + 1) * TPT if (cgi5 >= 2 and fc == 0) else 0))):
                    w(e)
                    for tt in range(TPT):
                        ah_ = atnh[:, fc * c.CHT + tt * 128:fc * c.CHT + tt * 128 + 128]
                        al_ = atnl[:, fc * c.CHT + tt * 128:fc * c.CHT + tt * 128 + 128]
                        e.matmul(bank(par + tt), ah_, wsb5h[sl_w], start=(fc == 0), stop=False)
                        e.matmul(bank(par + tt), al_, wsb5h[sl_w], start=False, stop=False)
                        ins = e.matmul(bank(par + tt), ah_, wsb5l[sl_w], start=False,
                                       stop=(fc == c.KD - 1))
                    ins.then_inc(s_wfr.h, 1)
                em("tensor", f_wm)
                s_wfr.n += 1
            wfr5_after.append(s_wfr.n)

            for tt in range(TPT):
                dl = (cgi5 * TPT + tt) % 2
                trow = th * c.CHT + tt * 128
                extra = st3_finals if (cgi5 == 0 and tt == 0) else []
                DMA("gpsimd", gld5[dl], g_s[trow:trow + 128, oc * 512:(oc + 1) * 512],
                    gx[dl], (s_wv1, max(0, s_wv1.n - 1)), *extra)
                DMA("gpsimd", xld5[dl], xsh[trow:trow + 128, oc * 512:(oc + 1) * 512], gx[dl])

                def f_wv(e, par=par, tt=tt, dl=dl,
                         w=W((s_wfr, s_wfr.n), (gx[dl], gx[dl].n),
                             (yst[dl], yst[dl].n))):
                    w(e)
                    e.tensor_tensor(t1s[dl], bank(par + tt), gld5[dl], OP.mult)
                    e.drain()
                    e.tensor_tensor(ost[dl], t1s[dl], xld5[dl], OP.add).then_inc(s_wv1.h, 1)
                em("vector", f_wv)
                s_wv1.n += 1
                DMA("gpsimd", y[trow:trow + 128, oc * 512:(oc + 1) * 512], ost[dl],
                    yst[dl], (s_wv1, s_wv1.n))

    def f_fin(e, w=W((yst[0], yst[0].n), (yst[1], yst[1].n))):
        w(e)
    em("gpsimd", f_fin)
    quiesce("sync", ldw + ldwl)
    quiesce("gpsimd", gx + lat)
    run_phase()
    return nc


def shard_inputs(c, L0_attn_hidden_states, L0_attn_Wqkv, L0_attn_q_ln_w, L0_attn_q_ln_b,
                 L0_attn_k_ln_w, L0_attn_k_ln_b, L0_attn_Wout, L0_ffn_base_w,
                 L0_ffn_base_b, L0_ffn_A, L0_ffn_B, L0_ffn_router_w):
    x = np.asarray(L0_attn_hidden_states, dtype=np.float32)
    com = {
        "wqkvT": np.ascontiguousarray(np.asarray(L0_attn_Wqkv, np.float32).T),
        "woutT": np.ascontiguousarray(np.asarray(L0_attn_Wout, np.float32).T),
        "basewT": np.ascontiguousarray(np.asarray(L0_ffn_base_w, np.float32).T),
        "baseb": np.asarray(L0_ffn_base_b, np.float32).reshape(1, c.D),
        "aT": np.ascontiguousarray(np.asarray(L0_ffn_A, np.float32).T),
        "routerT": np.ascontiguousarray(np.asarray(L0_ffn_router_w, np.float32).T),
        "bflat": np.ascontiguousarray(
            np.asarray(L0_ffn_B, np.float32).transpose(0, 2, 1).reshape(c.ER, c.D)),
        "qlnw": np.asarray(L0_attn_q_ln_w, np.float32).reshape(1, c.D),
        "qlnb": np.asarray(L0_attn_q_ln_b, np.float32).reshape(1, c.D),
        "klnw": np.asarray(L0_attn_k_ln_w, np.float32).reshape(1, c.D),
        "klnb": np.asarray(L0_attn_k_ln_b, np.float32).reshape(1, c.D),
    }
    in_maps = []
    for core in range(c.NC):
        b, r = core // c.R4, core % c.R4
        sh = dict(com)
        sh["xsh"] = np.ascontiguousarray(x[b, r * c.TOK:(r + 1) * c.TOK, :])
        sh["qpos"] = np.arange(r * c.TOK, (r + 1) * c.TOK, dtype=np.float32).reshape(c.TOK, 1)
        hpn = c.H // c.NC
        sl = alibi_slopes(c.H)[core * hpn:(core + 1) * hpn]
        sh["slv"] = np.ascontiguousarray(np.broadcast_to(
            np.tile(sl, c.B).reshape(1, c.B * hpn), (128, c.B * hpn)))
        in_maps.append(sh)
    return in_maps


def gather_out(c, results):
    out = np.empty((c.B, c.S, c.D), np.float32)
    for core in range(c.NC):
        b, r = core // c.R4, core % c.R4
        out[b, r * c.TOK:(r + 1) * c.TOK, :] = results[core]["y"]
    return out


def kernel(**inputs):
    c = Cfg()
    in_maps = shard_inputs(c, **inputs)
    nc = build_program(c)
    res = run_bass_kernel_spmd(nc, in_maps, core_ids=list(range(c.NC)))
    return gather_out(c, res.results)


# version 12
# speedup vs baseline: 1.0205x; 1.0205x over previous
import math
import numpy as np
import concourse.bass as bass
import concourse.mybir as mybir
from concourse.bass_utils import run_bass_kernel_spmd

F32 = mybir.dt.float32
BF16 = mybir.dt.bfloat16
AX = mybir.AxisListType.X
OP = mybir.AluOpType
AF = mybir.ActivationFunctionType

LN_EPS = 1e-5
CLIP = 8.0
MASKMUL = -1e27
ARENA_COLS = 45056


def alibi_slopes(n_heads, bias_max=8.0):
    next_pow2 = 2 ** int(np.ceil(np.log2(n_heads)))
    m = np.arange(1, next_pow2 + 1, dtype=np.float32) * (bias_max / next_pow2)
    slopes = 1.0 / np.power(2.0, m)
    if next_pow2 != n_heads:
        slopes = np.concatenate([slopes[1::2], slopes[::2]])[:n_heads]
    return slopes.astype(np.float32)


class Cfg:
    def __init__(self, D=4096, S=4096, B=2, H=32, NE=16, NR=64):
        self.D, self.S, self.B, self.H = D, S, B, H
        self.DH = 128
        assert H * self.DH == D
        self.NC = 8
        self.R4 = self.NC // B
        self.TOK = B * S // self.NC
        self.NT = self.TOK // 128
        self.KD = D // 128
        self.CH = 512
        self.CHT = min(512, self.TOK)
        self.NTH = self.TOK // self.CHT
        self.TPT = self.CHT // 128
        self.NQC = 3 * D // self.CH
        self.NDC = D // self.CH
        self.NBC = D // self.CH
        self.NE, self.NR = NE, NR
        self.ER = NE * NR
        self.NERC = self.ER // 128
        self.SC = S // self.CH
        self.KB = S // 128
        self.HQ = min(2, H // self.R4)
        self.NHG = H // self.HQ


class Sem:
    def __init__(self, nc, name):
        self.h = nc.alloc_semaphore(name=name)
        self.n = 0


def build_program(c):
    nc = bass.Bass(trn_type="TRN2", target_bir_lowering=False)

    xsh = nc.dram_tensor("xsh", [c.TOK, c.D], F32, kind="ExternalInput").ap()
    qpos = nc.dram_tensor("qpos", [c.TOK, 1], F32, kind="ExternalInput").ap()
    slv = nc.dram_tensor("slv", [128, c.H // c.R4], F32, kind="ExternalInput").ap()
    wqkvT = nc.dram_tensor("wqkvT", [c.D, 3 * c.D], F32, kind="ExternalInput").ap()
    woutT = nc.dram_tensor("woutT", [c.D, c.D], F32, kind="ExternalInput").ap()
    basewT = nc.dram_tensor("basewT", [c.D, c.D], F32, kind="ExternalInput").ap()
    baseb = nc.dram_tensor("baseb", [1, c.D], F32, kind="ExternalInput").ap()
    aT = nc.dram_tensor("aT", [c.D, c.NR], F32, kind="ExternalInput").ap()
    routerT = nc.dram_tensor("routerT", [c.D, c.NE], F32, kind="ExternalInput").ap()
    bflat = nc.dram_tensor("bflat", [c.ER, c.D], F32, kind="ExternalInput").ap()
    qlnw = nc.dram_tensor("qlnw", [1, c.D], F32, kind="ExternalInput").ap()
    qlnb = nc.dram_tensor("qlnb", [1, c.D], F32, kind="ExternalInput").ap()
    klnw = nc.dram_tensor("klnw", [1, c.D], F32, kind="ExternalInput").ap()
    klnb = nc.dram_tensor("klnb", [1, c.D], F32, kind="ExternalInput").ap()
    y = nc.dram_tensor("y", [c.TOK, c.D], F32, kind="ExternalOutput").ap()

    xnf_s = nc.dram_tensor("xnf_s", [c.D, c.TOK], F32).ap()
    qs_s = nc.dram_tensor("qs_s", [c.TOK, c.D], F32).ap()
    ks_s = nc.dram_tensor("ks_s", [c.TOK, c.D], F32).ap()
    HPN = c.H // c.NC
    QKR = 2 * HPN * 128
    a2aqk_i = nc.dram_tensor("a2aqk_i", [c.NC * QKR, c.TOK], F32).ap()
    a2aqk_o = nc.dram_tensor("a2aqk_o", [c.NC * QKR, c.TOK], F32).ap()
    a2av_i = nc.dram_tensor("a2av_i", [c.NC * HPN * c.TOK, c.DH], F32).ap()
    a2av_o = nc.dram_tensor("a2av_o", [c.NC * HPN * c.TOK, c.DH], F32).ap()
    a2aao_i = nc.dram_tensor("a2aao_i", [c.NC * HPN * 128, c.TOK], F32).ap()
    a2aao_o = nc.dram_tensor("a2aao_o", [c.NC * HPN * 128, c.TOK], F32).ap()
    base_s = nc.dram_tensor("base_s", [c.TOK, c.D], F32).ap()
    g_s = nc.dram_tensor("g_s", [c.TOK, c.D], F32).ap()
    wqkvT_h = nc.dram_tensor("wqkvT_h", [c.D, 3 * c.D], BF16).ap()
    wqkvT_l = nc.dram_tensor("wqkvT_l", [c.D, 3 * c.D], BF16).ap()
    basewT_h = nc.dram_tensor("basewT_h", [c.D, c.D], BF16).ap()
    basewT_l = nc.dram_tensor("basewT_l", [c.D, c.D], BF16).ap()
    woutT_h = nc.dram_tensor("woutT_h", [c.D, c.D], BF16).ap()
    woutT_l = nc.dram_tensor("woutT_l", [c.D, c.D], BF16).ap()
    bflat_h = nc.dram_tensor("bflat_h", [c.ER, c.D], BF16).ap()
    bflat_l = nc.dram_tensor("bflat_l", [c.ER, c.D], BF16).ap()

    ident_d = nc.inline_tensor(np.eye(128, dtype=np.float32), name="identc").ap()
    kpos_d = nc.inline_tensor(
        np.broadcast_to(np.arange(c.S, dtype=np.float32), (128, c.S)).copy(),
        name="kposc").ap()
    sel_np = np.zeros((c.NE, c.NERC * 128), np.float32)
    for _ecc in range(c.NERC):
        for _m in range(128):
            sel_np[(_ecc * 128 + _m) // c.NR, _ecc * 128 + _m] = 1.0
    sel2_d = nc.inline_tensor(sel_np, name="sel2c").ap()
    ones1_d = nc.inline_tensor(np.ones((1, 128), np.float32), name="ones1c").ap()
    negq_np = -(np.arange(c.KB, dtype=np.float32)[None, :] * 128
                + np.arange(128, dtype=np.float32)[:, None])
    negq_d = nc.inline_tensor(np.ascontiguousarray(negq_np), name="negqc").ap()
    dg_np = (np.arange(128, dtype=np.float32)[None, :]
             - np.arange(128, dtype=np.float32)[:, None])
    dg_np = np.where(dg_np <= 0, dg_np, np.float32(-1e30))
    dg_d = nc.inline_tensor(np.ascontiguousarray(dg_np), name="dgmask").ap()
    slopes = alibi_slopes(c.H)

    psall = nc.alloc_psum_tensor("psall", [128, 8 * 512], F32).ap()

    def bank(i):
        return psall[:, i * 512:(i + 1) * 512]

    arena = nc.alloc_sbuf_tensor("arena", [128, ARENA_COLS], F32).ap()
    amark = [0]

    def alloc(cols):
        ap = arena[:, amark[0]:amark[0] + cols]
        amark[0] += cols
        assert amark[0] <= ARENA_COLS, f"arena overflow {amark[0]}"
        return ap

    identt = alloc(128)
    onest = alloc(128)
    dgt = alloc(128)
    selt = alloc(c.NERC * 128)
    basebt = alloc(c.D)
    gqs, gqb, gks, gkb = alloc(c.H), alloc(c.H), alloc(c.H), alloc(c.H)
    qpost = alloc(c.NT)
    negqp = alloc(c.NT)
    negqc = alloc(c.KB)
    slvt = alloc(c.H // c.R4)
    sm = alloc(16 * 36)
    epst = alloc(16)


    def smx(i):
        return sm[:, i * 16:i * 16 + 1]
    PERSIST0 = amark[0]
    logits = alloc(c.NT * c.NE)
    rh2 = alloc(c.TOK)
    combf = alloc(c.TOK)
    atc = alloc(c.KD * c.NR)
    rtc = alloc(c.KD * c.NE)

    def S_(nm):
        return Sem(nc, nm)

    def slots(nm, n):
        return [S_(f"{nm}{i}") for i in range(n)]

    s_sq, s_vst, s_sct, s_xn = S_("sq"), S_("vst"), S_("sct"), S_("xn")
    s_tr0, s_ev0, s_wfr, s_mm1, s_ev1 = (S_(n) for n in ("tr0", "ev0", "wfr", "mm1", "ev1"))
    s_sq2, s_vst2, s_sct2, s_qn, s_tr2, s_ev2 = s_sq, s_vst, s_sct, s_xn, s_tr0, s_ev0
    s_cc, s_ffv, s_ffs, s_trc, s_cbm, s_cbv = (S_(n) for n in
        ("cc", "ffv", "ffs", "trc", "cbm", "cbv"))
    s_bfr, s_dev, s_sil = (S_(n) for n in ("bfr", "dev", "sil"))
    s_dbs, s_dbv, s_smm, s_sev, s_mx, s_e1, s_rs, s_e2 = (S_(n) for n in
        ("dbs", "dbv", "smm", "sev", "mx", "e1", "rs", "e2"))
    s_ptm, s_ptv, s_pv, s_aev, s_wv1 = (S_(n) for n in
        ("ptm", "ptv", "pv", "aev", "wv1"))
    s_ini = S_("ini")
    ldx = slots("ldx", 2)
    st0 = slots("st0", 2)
    ldxn = slots("ldxn", 2)
    s_ldpre = S_("ldpre")
    ldw = slots("ldw", 6)
    st1 = slots("st1", 6)
    lqk = slots("lqk", 2)
    st2 = slots("st2", 8)
    ldb = ldw
    bst = slots("bst", 2)
    st3 = slots("st3", 2)
    s_dup = S_("dup")
    s_loc = S_("loc")
    s_kv = S_("kv")
    qfl = slots("qfl", 2)
    ast = slots("ast", 2)
    lat = slots("lat", 2)
    gx = slots("gx", 2)
    yst = slots("yst", 2)
    spl = slots("spl", 4)
    sps = slots("sps", 4)
    s_sphi = S_("sphi")
    s_splo = S_("splo")
    s_splc = S_("splc")
    ldwl = slots("ldwl", 6)
    s_xsp = S_("xsp")
    s_csp = s_xsp
    s_asp = s_xsp

    ENGS = ("sync", "gpsimd", "tensor", "vector", "scalar")
    ops = {e: [] for e in ENGS}

    def em(eng, fn):
        ops[eng].append(fn)

    def run_phase():
        nonlocal ops
        with nc.Block() as b:
            for e in ENGS:
                if ops[e]:
                    getattr(b, e)(lambda eng, fns=tuple(ops[e]): [f(eng) for f in fns])
        nc.all_engine_barrier()
        ops = {e: [] for e in ENGS}

    def W(*pairs):
        frozen = tuple((s.h, v) for (s, v) in pairs if v > 0)

        def apply(e):
            for h, v in frozen:
                e.wait_ge(h, v)
        return apply

    def DMA(eng, out, in_, sem, *waits, slow=False):
        w = W(*waits)

        def f(e, out=out, in_=in_):
            w(e)
            e.dma_start(out=out, in_=in_,
                        allow_slow_non_contiguous=slow).then_inc(sem.h, 16)
        em(eng, f)
        sem.n += 16
        return sem.n

    def quiesce(eng, sems):
        pairs = tuple((sx.h, sx.n) for sx in sems if sx.n > 0)

        def f(e):
            for h, v in pairs:
                e.wait_ge(h, v)
        em(eng, f)

    DMA("sync", identt, ident_d, s_ini)
    DMA("sync", dgt, dg_d, s_ini)
    DMA("sync", selt[0:c.NE, :], sel2_d, s_ini)
    DMA("sync", onest[0:1, :], ones1_d, s_ini)
    DMA("sync", basebt[0:1, :], baseb, s_ini)
    for dst, src in ((gqs, qlnw), (gqb, qlnb), (gks, klnw), (gkb, klnb)):
        DMA("sync", dst, src[0, :].rearrange("(h d) -> d h", d=128), s_ini, slow=True)
    DMA("sync", qpost, qpos[:, 0].rearrange("(t p) -> p t", p=128), s_ini, slow=True)
    DMA("sync", negqc, negq_d, s_ini)
    DMA("sync", slvt, slv, s_ini)

    def f_initv(e, w=W((s_ini, s_ini.n))):
        w(e)
        isq = 1.0 / math.sqrt(c.DH)
        e.tensor_scalar(gqs, gqs, isq, None, OP.mult)
        e.drain()
        e.tensor_scalar(gqb, gqb, isq, None, OP.mult)
        e.drain()
        e.tensor_scalar(negqp, qpost, -1.0, None, OP.mult)
        e.drain()
        e.memset(epst[:, 0:1], LN_EPS)
    em("vector", f_initv)
    quiesce("sync", [s_ini])
    run_phase()

    am0 = amark[0]
    xa = [alloc(c.D), alloc(c.D)]
    xnb = [alloc(c.D), alloc(c.D)]
    sqs = alloc(c.D)
    stg0 = [alloc(512), alloc(512)]
    NG0 = c.KD // 4

    for i in range(c.NT):
        sl = i % 2
        DMA("sync", xa[sl], xsh[i * 128:(i + 1) * 128, :], ldx[sl],
            (s_xn, max(0, s_xn.n - 1)), (s_sq, max(0, s_sq.n - 1)))

        def f_sc(e, sl=sl, w=W((ldx[sl], ldx[sl].n))):
            w(e)
            e.activation(sqs, xa[sl], AF.Square,
                         accum_out=smx(2 + sl)).then_inc(s_sq.h, 1)
        em("scalar", f_sc)
        s_sq.n += 1

        def f_v1(e, sl=sl, w=W((ldx[sl], ldx[sl].n), (s_sq, s_sq.n))):
            w(e)
            e.reduce_sum(smx(0 + sl), xa[sl], axis=AX)
            e.drain()
            e.tensor_scalar(smx(4 + sl), smx(0 + sl), 1.0 / c.D, None, OP.mult)
            e.tensor_scalar(smx(6 + sl), smx(2 + sl), 1.0 / c.D, None, OP.mult)
            e.drain()
            e.tensor_tensor(smx(8 + sl), smx(4 + sl), smx(4 + sl), OP.mult)
            e.drain()
            e.tensor_tensor(smx(6 + sl), smx(6 + sl), smx(8 + sl),
                            OP.subtract).then_inc(s_vst.h, 1)
        em("vector", f_v1)
        s_vst.n += 1

        def f_sc2(e, sl=sl, w=W((s_vst, s_vst.n))):
            w(e)
            e.activation(smx(10 + sl), smx(6 + sl), AF.Sqrt,
                         bias=epst[:, 0:1]).then_inc(s_sct.h, 1)
        em("scalar", f_sc2)
        s_sct.n += 1

        def f_v2(e, sl=sl, w=W((s_sct, s_sct.n), (s_tr0, max(0, (i - 1) * NG0)))):
            w(e)
            e.reciprocal(smx(12 + sl), smx(10 + sl))
            e.drain()
            e.tensor_scalar(xnb[sl], xa[sl], smx(4 + sl), smx(12 + sl),
                            OP.subtract, OP.mult).then_inc(s_xn.h, 1)
        em("vector", f_v2)
        s_xn.n += 1

        for g in range(NG0):
            def f_tr(e, sl=sl, g=g, w=W((s_xn, s_xn.n), (s_ev0, max(0, s_ev0.n - 1)))):
                w(e)
                for q in range(4):
                    kd = g * 4 + q
                    ins = e.transpose(bank(g % 2)[:, q * 128:(q + 1) * 128],
                                      xnb[sl][:, kd * 128:(kd + 1) * 128], identt)
                ins.then_inc(s_tr0.h, 1)
            em("tensor", f_tr)
            s_tr0.n += 1

            def f_ev(e, g=g, w=W((s_tr0, s_tr0.n), (st0[g % 2], st0[g % 2].n))):
                w(e)
                e.tensor_copy(stg0[g % 2], bank(g % 2)).then_inc(s_ev0.h, 1)
            em("vector", f_ev)
            s_ev0.n += 1

            for q in range(4):
                kd = g * 4 + q
                DMA("gpsimd", xnf_s[kd * 128:(kd + 1) * 128, i * 128:(i + 1) * 128],
                    stg0[g % 2][:, q * 128:(q + 1) * 128], st0[g % 2],
                    (s_ev0, s_ev0.n) if q == 0 else (s_ev0, 0))
    SPW = 512
    wtmp = [None] * 4
    lo_f = [None] * 4
    hi_b = [None] * 4
    lo_b = [None] * 4

    def alloc_split_bufs(n):
        for i in range(n):
            wtmp[i] = alloc(SPW)
            lo_f[i] = alloc(SPW)
            hi_b[i] = alloc(SPW // 2)[:, :].bitcast(BF16)
            lo_b[i] = alloc(SPW // 2)[:, :].bitcast(BF16)

    def emit_split(jobs, nsl):
        spc = [0]
        for srcT, hiT, loT, RR, CC in jobs:
            for r0 in range(RR // 128):
                for c0 in range(CC // SPW):
                    sl = spc[0] % nsl
                    spc[0] += 1
                    DMA("sync", wtmp[sl], srcT[r0 * 128:(r0 + 1) * 128,
                                               c0 * SPW:(c0 + 1) * SPW],
                        spl[sl], (s_splo, max(0, s_splo.n - (nsl - 1))))

                    def f_hi(e, sl=sl, w=W((spl[sl], spl[sl].n),
                                           (s_splo, max(0, s_splo.n - (nsl - 1))),
                                           (sps[sl], sps[sl].n))):
                        w(e)
                        e.tensor_copy(hi_b[sl], wtmp[sl])
                        e.drain()
                        e.scalar_tensor_tensor(lo_f[sl], hi_b[sl], -1.0, wtmp[sl],
                                               OP.mult, OP.add)
                        e.drain()
                        e.tensor_copy(lo_b[sl], lo_f[sl]).then_inc(s_splo.h, 1)
                    em("vector", f_hi)
                    s_splo.n += 1

                    DMA("gpsimd", hiT[r0 * 128:(r0 + 1) * 128, c0 * SPW:(c0 + 1) * SPW],
                        hi_b[sl], sps[sl], (s_splo, s_splo.n))
                    DMA("gpsimd", loT[r0 * 128:(r0 + 1) * 128, c0 * SPW:(c0 + 1) * SPW],
                        lo_b[sl], sps[sl], (s_splo, 0))

    sps_finals = [(x, x.n) for x in sps]
    quiesce("sync", ldx + spl)
    quiesce("gpsimd", st0 + sps)
    run_phase()
    amark[0] = am0

    am1 = amark[0]
    xnfh = alloc(c.KD * c.CHT // 2)[:, :].bitcast(BF16)
    xnfl = alloc(c.KD * c.CHT // 2)[:, :].bitcast(BF16)
    xtmp = [alloc(c.CHT), alloc(c.CHT)]
    xlo = [alloc(c.CHT)]
    xlo = [xlo[0], xlo[0]]
    atch = alloc(c.KD * c.NR // 2)[:, :].bitcast(BF16)
    atcl = alloc(c.KD * c.NR // 2)[:, :].bitcast(BF16)
    rtch = alloc(c.KD * c.NE // 2)[:, :].bitcast(BF16)
    rtcl = alloc(c.KD * c.NE // 2)[:, :].bitcast(BF16)
    wsbh = [alloc(256)[:, :].bitcast(BF16) for _ in range(6)]
    wsbl = [alloc(256)[:, :].bitcast(BF16) for _ in range(6)]
    stg1 = [alloc(512) for _ in range(6)]
    NSTG = 6
    TPT = c.TPT
    wfp = [alloc(512), alloc(512)]
    wlo_t = alloc(512)

    chunks = ([("qkv", i) for i in range(c.NQC)] + [("base", i) for i in range(c.NBC)]
              + [("arh", 0), ("rtr", 0)])
    NCHK = len(chunks)
    wtile = 0
    mm1_at_th_end = []
    wfr_at_th_end = []
    st0_finals = [(st0[0], st0[0].n), (st0[1], st0[1].n)]

    for kd in range(c.KD):
        DMA("sync", atc[:, kd * c.NR:(kd + 1) * c.NR], aT[kd * 128:(kd + 1) * 128, :], s_ldpre)
        DMA("sync", rtc[:, kd * c.NE:(kd + 1) * c.NE], routerT[kd * 128:(kd + 1) * 128, :],
            s_ldpre)
    ldpre_fin = s_ldpre.n

    def _chunks(src, hi, lo, total):
        cw = min(512, c.CHT, total)
        return [(src[:, i * cw:(i + 1) * cw], hi[:, i * cw:(i + 1) * cw],
                 lo[:, i * cw:(i + 1) * cw]) for i in range(total // cw)]
    spre_pairs = (_chunks(atc, atch, atcl, c.KD * c.NR)
                  + _chunks(rtc, rtch, rtcl, c.KD * c.NE))
    for fsrc, fhi, flo in spre_pairs:
        def f_spre(e, fsrc=fsrc, fhi=fhi, w=W((s_ldpre, ldpre_fin),
                                              (s_splo, max(0, s_splo.n)))):
            w(e)
            e.activation(fhi, fsrc, AF.Identity).then_inc(s_sphi.h, 1)
        em("scalar", f_spre)
        s_sphi.n += 1

        cw_ = fsrc.shape[1]

        def f_spre2(e, fsrc=fsrc, fhi=fhi, cw_=cw_, w=W((s_sphi, s_sphi.n),
                                                        (s_splc, max(0, s_splc.n)))):
            w(e)
            e.scalar_tensor_tensor(xlo[0][:, 0:cw_], fhi, -1.0, fsrc,
                                   OP.mult, OP.add).then_inc(s_splo.h, 1)
        em("vector", f_spre2)
        s_splo.n += 1

        def f_spre3(e, flo=flo, cw_=cw_, w=W((s_splo, s_splo.n))):
            w(e)
            e.copy(flo, xlo[0][:, 0:cw_]).then_inc(s_splc.h, 1)
        em("scalar", f_spre3)
        s_splc.n += 1
    spre_done = s_splc.n

    suse = 0
    xspc = 0
    wtile_th1_first = [True]
    for th in range(c.NTH):
        for kd in range(c.KD):
            sx = kd % 2
            extra = (st0_finals + [(s_mm1, mm1_at_th_end[th - 1] if th > 0 else 0),
                                   (s_wfr, wfr_at_th_end[th - 1] if th > 0 else 0),
                                   (s_splc, spre_done)]
                     if kd == 0 else [(s_xsp, max(0, s_xsp.n - 1))])
            DMA("gpsimd", xtmp[sx], xnf_s[kd * 128:(kd + 1) * 128, th * c.CHT:(th + 1) * c.CHT],
                ldxn[sx], *extra)

            def f_xh(e, sx=sx, kd=kd, w=W((ldxn[sx], ldxn[sx].n), (s_xsp, max(0, s_xsp.n - 1)))):
                w(e)
                e.activation(xnfh[:, kd * c.CHT:(kd + 1) * c.CHT], xtmp[sx],
                             AF.Identity).then_inc(s_sphi.h, 1)
            em("scalar", f_xh)
            s_sphi.n += 1

            def f_xl(e, sx=sx, kd=kd, w=W((s_sphi, s_sphi.n), (s_xsp, s_xsp.n))):
                w(e)
                e.scalar_tensor_tensor(xlo[sx], xnfh[:, kd * c.CHT:(kd + 1) * c.CHT],
                                       -1.0, xtmp[sx], OP.mult, OP.add).then_inc(s_splo.h, 1)
            em("vector", f_xl)
            s_splo.n += 1

            def f_xc(e, sx=sx, kd=kd, w=W((s_splo, s_splo.n))):
                w(e)
                e.copy(xnfl[:, kd * c.CHT:(kd + 1) * c.CHT], xlo[sx]).then_inc(s_xsp.h, 1)
            em("scalar", f_xc)
            s_xsp.n += 1
        ldxn_here = s_xsp.n

        for ci, (kind, ec) in enumerate(chunks):
            cgi = th * NCHK + ci
            par = (cgi % 2) * 4
            ev_guard = (cgi - 2 + 1) * TPT if cgi >= 2 else 0

            if kind in ("qkv", "base"):
                src0 = wqkvT if kind == "qkv" else basewT
                srch = wqkvT_h if kind == "qkv" else basewT_h
                srcl = wqkvT_l if kind == "qkv" else basewT_l
                for kd in range(c.KD):
                    sl_w = wtile % 6
                    if th == 0:
                        sf = wtile % 2
                        DMA("sync", wfp[sf],
                            src0[kd * 128:(kd + 1) * 128, ec * 512:ec * 512 + 512],
                            spl[sf], (s_splo, max(0, s_splo.n - 1)))

                        def f_ws(e, sf=sf, sl_w=sl_w,
                                 w=W((spl[sf], spl[sf].n), (s_wfr, max(0, wtile - 5)),
                                     (sps[sl_w % 4], sps[sl_w % 4].n))):
                            w(e)
                            e.tensor_copy(wsbh[sl_w], wfp[sf])
                            e.drain()
                            e.scalar_tensor_tensor(wlo_t, wsbh[sl_w], -1.0, wfp[sf],
                                                   OP.mult, OP.add)
                            e.drain()
                            e.tensor_copy(wsbl[sl_w], wlo_t).then_inc(s_splo.h, 1)
                        em("vector", f_ws)
                        s_splo.n += 1
                        DMA("gpsimd",
                            srch[kd * 128:(kd + 1) * 128, ec * 512:ec * 512 + 512],
                            wsbh[sl_w], sps[sl_w % 4], (s_splo, s_splo.n))
                        DMA("gpsimd",
                            srcl[kd * 128:(kd + 1) * 128, ec * 512:ec * 512 + 512],
                            wsbl[sl_w], sps[sl_w % 4], (s_splo, 0))
                        wt_w = ((s_splo, s_splo.n), (ldw[sl_w], 0))
                    else:
                        extra_w = [(x, x.n) for x in sps] if wtile_th1_first[0] else []
                        wtile_th1_first[0] = False
                        DMA("sync", wsbh[sl_w],
                            srch[kd * 128:(kd + 1) * 128, ec * 512:ec * 512 + 512],
                            ldw[sl_w], (s_wfr, max(0, wtile - 5)), *extra_w)
                        DMA("sync", wsbl[sl_w],
                            srcl[kd * 128:(kd + 1) * 128, ec * 512:ec * 512 + 512],
                            ldwl[sl_w])
                        wt_w = ((ldw[sl_w], ldw[sl_w].n), (ldwl[sl_w], ldwl[sl_w].n))
                    wtile += 1

                    def f_mm(e, kd=kd, par=par, sl_w=sl_w, kind=kind,
                             w=W(*wt_w,
                                 (s_xsp, ldxn_here if (kd == 0 and ci == 0) else 0),
                                 (s_ev1, ev_guard if kd == 0 else 0))):
                        w(e)
                        stop = (kd == c.KD - 1) and kind == "qkv"
                        for tt in range(TPT):
                            xh = xnfh[:, kd * c.CHT + tt * 128:kd * c.CHT + tt * 128 + 128]
                            xl = xnfl[:, kd * c.CHT + tt * 128:kd * c.CHT + tt * 128 + 128]
                            e.matmul(bank(par + tt), xh, wsbh[sl_w],
                                     start=(kd == 0 and tt == 0) if False else (kd == 0),
                                     stop=False)
                            e.matmul(bank(par + tt), xl, wsbh[sl_w], start=False, stop=False)
                            ins = e.matmul(bank(par + tt), xh, wsbl[sl_w], start=False,
                                           stop=stop)
                        ins.then_inc(s_wfr.h, 1)
                    em("tensor", f_mm)
                    s_wfr.n += 1

                if kind == "base":
                    def f_bias(e, par=par, ec=ec):
                        for tt in range(TPT):
                            ins = e.matmul(bank(par + tt), onest[0:1, 0:128],
                                           basebt[0:1, ec * 512:ec * 512 + 512],
                                           start=False, stop=True)
                        ins.then_inc(s_mm1.h, 1)
                    em("tensor", f_bias)
                    s_mm1.n += 1
                done_w = (s_mm1, s_mm1.n) if kind == "base" else (s_wfr, s_wfr.n)

                for tt in range(TPT):
                    sl1 = suse % NSTG

                    def f_ev1(e, par=par, tt=tt, kind=kind, sl1=sl1,
                              w=W(done_w, (st1[sl1], st1[sl1].n))):
                        w(e)
                        if kind == "qkv":
                            e.tensor_scalar(stg1[sl1], bank(par + tt), CLIP, -CLIP,
                                            OP.min, OP.max).then_inc(s_ev1.h, 1)
                        else:
                            e.tensor_copy(stg1[sl1], bank(par + tt)).then_inc(s_ev1.h, 1)
                    em("vector", f_ev1)
                    s_ev1.n += 1

                    trow = th * c.CHT + tt * 128
                    if kind == "qkv":
                        if ec < c.NDC:
                            DMA("gpsimd", qs_s[trow:trow + 128, ec * 512:(ec + 1) * 512],
                                stg1[sl1], st1[sl1], (s_ev1, s_ev1.n))
                        elif ec < 2 * c.NDC:
                            DMA("gpsimd",
                                ks_s[trow:trow + 128, (ec - c.NDC) * 512:(ec - c.NDC + 1) * 512],
                                stg1[sl1], st1[sl1], (s_ev1, s_ev1.n))
                        else:
                            for hb in range(4):
                                h = (ec - 2 * c.NDC) * 4 + hb
                                vrow = ((h // HPN) * HPN + (h % HPN)) * c.TOK + trow
                                DMA("gpsimd", a2av_i[vrow:vrow + 128, :],
                                    stg1[sl1][:, hb * 128:(hb + 1) * 128], st1[sl1],
                                    (s_ev1, s_ev1.n) if hb == 0 else (s_ev1, 0))
                    else:
                        DMA("gpsimd", base_s[trow:trow + 128, ec * 512:(ec + 1) * 512],
                            stg1[sl1], st1[sl1], (s_ev1, s_ev1.n))
                    suse += 1

            elif kind == "arh":
                def f_arh(e, par=par,
                          w=W((s_splc, spre_done), (s_xsp, ldxn_here), (s_ev1, ev_guard))):
                    w(e)
                    for kd in range(c.KD):
                        ah = atch[:, kd * c.NR:(kd + 1) * c.NR]
                        al = atcl[:, kd * c.NR:(kd + 1) * c.NR]
                        xh = xnfh[:, kd * c.CHT:(kd + 1) * c.CHT]
                        xl = xnfl[:, kd * c.CHT:(kd + 1) * c.CHT]
                        e.matmul(bank(par)[0:c.NR, 0:c.CHT], ah, xh,
                                 start=(kd == 0), stop=False)
                        e.matmul(bank(par)[0:c.NR, 0:c.CHT], al, xh, start=False, stop=False)
                        ins = e.matmul(bank(par)[0:c.NR, 0:c.CHT], ah, xl, start=False,
                                       stop=(kd == c.KD - 1))
                    ins.then_inc(s_mm1.h, 1)
                em("tensor", f_arh)
                s_mm1.n += 1

                def f_evarh(e, th=th, par=par, w=W((s_mm1, s_mm1.n))):
                    w(e)
                    e.tensor_copy(rh2[0:c.NR, th * c.CHT:(th + 1) * c.CHT],
                                  bank(par)[0:c.NR, 0:c.CHT]).then_inc(s_ev1.h, TPT)
                em("vector", f_evarh)
                s_ev1.n += TPT
                suse += TPT

            else:
                def f_rtr(e, par=par,
                          w=W((s_splc, spre_done), (s_xsp, ldxn_here), (s_ev1, ev_guard))):
                    w(e)
                    for tt in range(TPT):
                        for kd in range(c.KD):
                            xh = xnfh[:, kd * c.CHT + tt * 128:kd * c.CHT + tt * 128 + 128]
                            xl = xnfl[:, kd * c.CHT + tt * 128:kd * c.CHT + tt * 128 + 128]
                            rh_ = rtch[:, kd * c.NE:(kd + 1) * c.NE]
                            rl_ = rtcl[:, kd * c.NE:(kd + 1) * c.NE]
                            e.matmul(bank(par + tt)[:, 0:c.NE], xh, rh_,
                                     start=(kd == 0), stop=False)
                            e.matmul(bank(par + tt)[:, 0:c.NE], xl, rh_,
                                     start=False, stop=False)
                            ins = e.matmul(bank(par + tt)[:, 0:c.NE], xh, rl_,
                                           start=False, stop=(kd == c.KD - 1))
                    ins.then_inc(s_mm1.h, 1)
                em("tensor", f_rtr)
                s_mm1.n += 1

                def f_evrtr(e, th=th, par=par, w=W((s_mm1, s_mm1.n))):
                    w(e)
                    for tt in range(TPT):
                        g = th * TPT + tt
                        ins = e.tensor_copy(logits[:, g * c.NE:(g + 1) * c.NE],
                                            bank(par + tt)[:, 0:c.NE])
                    ins.then_inc(s_ev1.h, TPT)
                em("vector", f_evrtr)
                s_ev1.n += TPT
                suse += TPT
        mm1_at_th_end.append(s_mm1.n)
        wfr_at_th_end.append(s_wfr.n)
    quiesce("sync", [s_ldpre] + ldw + ldwl + spl)
    quiesce("gpsimd", ldxn + st1 + sps)
    run_phase()
    amark[0] = am1
    st1_finals = [(s, s.n) for s in st1]

    am2 = amark[0]
    qa = [alloc(c.D), alloc(c.D)]
    ka = [alloc(c.D), alloc(c.D)]
    qn = qa
    kn = ka
    sq2 = alloc(c.D)
    stg2 = [alloc(128) for _ in range(8)]
    suse2 = 0

    def f_agv(e, w=W(*st1_finals)):
        w(e)
        from concourse.aot_env import allow_ici_all_to_all
        with allow_ici_all_to_all():
            e.collective_compute("AllToAll", OP.bypass,
                                 replica_groups=[list(range(c.NC))],
                                 ins=[a2av_i.opt()],
                                 outs=[a2av_o.opt()]).then_inc(s_cc.h, 1)
    em("gpsimd", f_agv)
    s_cc.n += 1

    for i in range(c.NT):
        sl = i % 2
        extra = (st1_finals if i == 0 else []) + [(s_qn, max(0, s_qn.n - 2)),
                                                   (s_tr2, max(0, (i - 1) * 2 * c.H))]
        DMA("gpsimd", qa[sl], qs_s[i * 128:(i + 1) * 128, :], lqk[sl], *extra)
        DMA("gpsimd", ka[sl], ks_s[i * 128:(i + 1) * 128, :], lqk[sl])
        for which, buf, nbuf, gw, gb in (("q", qa, qn, gqs, gqb), ("k", ka, kn, gks, gkb)):
            o = 0 if which == "q" else 16

            def f_sc(e, sl=sl, buf=buf, o=o, w=W((lqk[sl], lqk[sl].n))):
                w(e)
                e.activation(sq2, buf[sl], AF.Square,
                             accum_out=smx(o + 2 + sl)).then_inc(s_sq2.h, 1)
            em("scalar", f_sc)
            s_sq2.n += 1

            def f_v1(e, sl=sl, buf=buf, o=o, w=W((lqk[sl], lqk[sl].n), (s_sq2, s_sq2.n))):
                w(e)
                e.reduce_sum(smx(o + 0 + sl), buf[sl], axis=AX)
                e.drain()
                e.tensor_scalar(smx(o + 4 + sl), smx(o + 0 + sl),
                                1.0 / c.D, None, OP.mult)
                e.tensor_scalar(smx(o + 6 + sl), smx(o + 2 + sl),
                                1.0 / c.D, None, OP.mult)
                e.drain()
                e.tensor_tensor(smx(o + 8 + sl), smx(o + 4 + sl),
                                smx(o + 4 + sl), OP.mult)
                e.drain()
                e.tensor_tensor(smx(o + 6 + sl), smx(o + 6 + sl),
                                smx(o + 8 + sl), OP.subtract).then_inc(s_vst2.h, 1)
            em("vector", f_v1)
            s_vst2.n += 1

            def f_sc2(e, sl=sl, o=o, w=W((s_vst2, s_vst2.n))):
                w(e)
                e.activation(smx(o + 10 + sl), smx(o + 6 + sl),
                             AF.Sqrt, bias=epst[:, 0:1]).then_inc(s_sct2.h, 1)
            em("scalar", f_sc2)
            s_sct2.n += 1

            def f_v2(e, sl=sl, buf=buf, nbuf=nbuf, o=o,
                     w=W((s_sct2, s_sct2.n), (s_tr2, max(0, (i - 1) * 2 * c.H)))):
                w(e)
                e.reciprocal(smx(o + 12 + sl), smx(o + 10 + sl))
                e.drain()
                e.tensor_scalar(nbuf[sl], buf[sl], smx(o + 4 + sl),
                                smx(o + 12 + sl),
                                OP.subtract, OP.mult).then_inc(s_qn.h, 1)
            em("vector", f_v2)
            s_qn.n += 1

            for h in range(c.H):
                bnk = (h % 4) + (0 if sl == 0 else 4)

                def f_tr(e, sl=sl, nbuf=nbuf, h=h, bnk=bnk,
                         w=W((s_qn, s_qn.n), (s_ev2, max(0, s_ev2.n - 3)))):
                    w(e)
                    e.transpose(bank(bnk)[:, 0:128], nbuf[sl][:, h * 128:(h + 1) * 128],
                                identt).then_inc(s_tr2.h, 1)
                em("tensor", f_tr)
                s_tr2.n += 1

                s2 = suse2 % 8

                def f_ev(e, h=h, bnk=bnk, gw=gw, gb=gb, s2=s2,
                         w=W((s_tr2, s_tr2.n), (st2[s2], st2[s2].n))):
                    w(e)
                    e.tensor_scalar(stg2[s2], bank(bnk)[:, 0:128],
                                    gw[:, h:h + 1], gb[:, h:h + 1],
                                    OP.mult, OP.add).then_inc(s_ev2.h, 1)
                em("vector", f_ev)
                s_ev2.n += 1
                roff = ((h // HPN) * QKR + (0 if which == "q" else HPN * 128)
                        + (h % HPN) * 128)
                DMA("gpsimd", a2aqk_i[roff:roff + 128, i * 128:(i + 1) * 128],
                    stg2[s2], st2[s2], (s_ev2, s_ev2.n))
                suse2 += 1

    st2_finals = [(s, s.n) for s in st2]

    def f_ag(e, w=W(*st2_finals)):
        w(e)
        from concourse.aot_env import allow_ici_all_to_all
        with allow_ici_all_to_all():
            e.collective_compute("AllToAll", OP.bypass,
                                 replica_groups=[list(range(c.NC))],
                                 ins=[a2aqk_i.opt()],
                                 outs=[a2aqk_o.opt()]).then_inc(s_cc.h, 1)
    em("gpsimd", f_ag)
    s_cc.n += 1
    cc_qkv = s_cc.n
    quiesce("gpsimd", lqk + st2)
    run_phase()
    amark[0] = am2

    csb = alloc(c.NERC * c.TOK)
    csbh = alloc(c.NERC * c.TOK // 2)[:, :].bitcast(BF16)
    csbl = alloc(c.NERC * c.TOK // 2)[:, :].bitcast(BF16)
    clo = alloc(c.CHT)
    wsb3h = [alloc(256)[:, :].bitcast(BF16) for _ in range(6)]
    wsb3l = [alloc(256)[:, :].bitcast(BF16) for _ in range(6)]
    bstg = [alloc(512) for _ in range(4)]
    t1st = [alloc(512) for _ in range(4)]
    gstg = [alloc(512) for _ in range(4)]
    bst4 = [bst[0], bst[1], lqk[0], lqk[1]]
    st34 = [st3[0], st3[1], st2[0], st2[1]]
    prbf = alloc(64)
    prb2f = alloc(64)
    prb = prbf[:, 0:c.NE]
    prb2 = prb2f[:, 0:c.NE]

    for g in range(c.NT):
        lg = logits[:, g * c.NE:(g + 1) * c.NE]

        def f_v(e, lg=lg, w=W((s_ffs, s_ffs.n), (s_trc, s_trc.n))):
            w(e)
            e.reduce_max(smx(0), lg, axis=AX)
            e.drain()
            e.tensor_scalar(smx(1), smx(0), -1.0, None, OP.mult).then_inc(s_ffv.h, 1)
        em("vector", f_v)
        s_ffv.n += 1

        def f_s(e, lg=lg, w=W((s_ffv, s_ffv.n))):
            w(e)
            e.activation(prb, lg, AF.Exp, bias=smx(1),
                         accum_out=smx(2)).then_inc(s_ffs.h, 1)
        em("scalar", f_s)
        s_ffs.n += 1

        def f_v2(e, w=W((s_ffs, s_ffs.n))):
            w(e)
            e.reciprocal(smx(3), smx(2))
            e.drain()
            e.tensor_scalar(prb, prb, smx(3), None, OP.mult)
            e.drain()
            e.reduce_max(smx(4), prb, axis=AX)
            e.drain()
            e.tensor_scalar(prb2, prb, smx(4), None, OP.is_ge)
            e.drain()
            e.scalar_tensor_tensor(prb2, prb2, -2.0, prb, OP.mult, OP.add)
            e.drain()
            e.reduce_max(smx(5), prb2, axis=AX)
            e.drain()
            e.tensor_tensor(smx(6), smx(4), smx(5), OP.add)
            e.drain()
            e.tensor_scalar(smx(6), smx(6), 1e-6, None, OP.add)
            e.drain()
            e.reciprocal(smx(7), smx(6))
            e.drain()
            e.tensor_tensor(smx(8), smx(4), smx(7), OP.mult)
            e.tensor_tensor(smx(9), smx(5), smx(7), OP.mult)
            e.tensor_scalar(prb2, prb2, smx(5), None, OP.is_ge)
            e.tensor_scalar(prb, prb, smx(4), None, OP.is_ge)
            e.drain()
            e.tensor_scalar(prb, prb, smx(8), None, OP.mult)
            e.drain()
            e.scalar_tensor_tensor(prb, prb2, smx(9), prb, OP.mult,
                                   OP.add).then_inc(s_ffv.h, 1)
        em("vector", f_v2)
        s_ffv.n += 1

        def f_tr(e, g=g, w=W((s_ffv, s_ffv.n), (s_cbv, max(0, s_cbv.n - 1)))):
            w(e)
            e.transpose(bank(g % 2)[0:c.NE, 0:128], prb, identt).then_inc(s_trc.h, 1)
        em("tensor", f_tr)
        s_trc.n += 1

        def f_ev(e, g=g, w=W((s_trc, s_trc.n))):
            w(e)
            e.tensor_copy(combf[0:c.NE, g * 128:(g + 1) * 128],
                          bank(g % 2)[0:c.NE, 0:128]).then_inc(s_cbv.h, 1)
        em("vector", f_ev)
        s_cbv.n += 1

    def f_dup(e, w=W((s_ev1, s_ev1.n))):
        w(e)
        e.dma_start(out=rh2[c.NR:2 * c.NR, :], in_=rh2[0:c.NR, :]).then_inc(s_dup.h, 16)
    em("gpsimd", f_dup)
    s_dup.n += 16

    cbv_base = s_cbv.n
    csp_base = s_csp.n
    cb = 0
    for ecc in range(c.NERC):
        for tch in range(c.NTH):
            def f_cb(e, ecc=ecc, tch=tch, cb=cb,
                     w=W((s_cbv, cbv_base + max(0, cb - 1)), (s_dup, 16))):
                w(e)
                e.matmul(bank(2 + cb % 2)[:, 0:c.CHT],
                         selt[0:c.NE, ecc * 128:(ecc + 1) * 128],
                         combf[0:c.NE, tch * c.CHT:(tch + 1) * c.CHT],
                         start=True, stop=True).then_inc(s_cbm.h, 1)
            em("tensor", f_cb)
            s_cbm.n += 1

            def f_cv(e, ecc=ecc, tch=tch, cb=cb, w=W((s_cbm, s_cbm.n))):
                w(e)
                e.tensor_tensor(
                    csb[:, ecc * c.TOK + tch * c.CHT:ecc * c.TOK + (tch + 1) * c.CHT],
                    rh2[:, tch * c.CHT:(tch + 1) * c.CHT],
                    bank(2 + cb % 2)[:, 0:c.CHT], OP.mult).then_inc(s_cbv.h, 1)
            em("vector", f_cv)
            s_cbv.n += 1

            def f_ch(e, ecc=ecc, tch=tch, w=W((s_cbv, s_cbv.n), (s_csp, max(0, s_csp.n - 1)))):
                w(e)
                e.activation(csbh[:, ecc * c.TOK + tch * c.CHT:ecc * c.TOK + (tch + 1) * c.CHT],
                             csb[:, ecc * c.TOK + tch * c.CHT:ecc * c.TOK + (tch + 1) * c.CHT],
                             AF.Identity).then_inc(s_sphi.h, 1)
            em("scalar", f_ch)
            s_sphi.n += 1

            def f_cl(e, ecc=ecc, tch=tch, w=W((s_sphi, s_sphi.n))):
                w(e)
                e.scalar_tensor_tensor(
                    clo, csbh[:, ecc * c.TOK + tch * c.CHT:ecc * c.TOK + (tch + 1) * c.CHT],
                    -1.0, csb[:, ecc * c.TOK + tch * c.CHT:ecc * c.TOK + (tch + 1) * c.CHT],
                    OP.mult, OP.add).then_inc(s_splo.h, 1)
            em("vector", f_cl)
            s_splo.n += 1

            def f_cc2(e, ecc=ecc, tch=tch, w=W((s_splo, s_splo.n))):
                w(e)
                e.copy(csbl[:, ecc * c.TOK + tch * c.CHT:ecc * c.TOK + (tch + 1) * c.CHT],
                       clo).then_inc(s_csp.h, 1)
            em("scalar", f_cc2)
            s_csp.n += 1
            cb += 1
    csp_done = s_csp.n

    btile = 0
    bfp = [alloc(512), alloc(512)]
    blo_t = alloc(512)
    for oc in range(c.NBC):
        for ecc in range(c.NERC):
            sl_b = btile % 6
            sf = btile % 2
            DMA("sync", bfp[sf], bflat[ecc * 128:(ecc + 1) * 128, oc * 512:(oc + 1) * 512],
                spl[sf], (s_splo, max(0, s_splo.n - 1)))

            def f_bs(e, sf=sf, sl_b=sl_b,
                     w=W((spl[sf], spl[sf].n), (s_bfr, max(0, btile - 5)))):
                w(e)
                e.tensor_copy(wsb3h[sl_b], bfp[sf])
                e.drain()
                e.scalar_tensor_tensor(blo_t, wsb3h[sl_b], -1.0, bfp[sf], OP.mult, OP.add)
                e.drain()
                e.tensor_copy(wsb3l[sl_b], blo_t).then_inc(s_splo.h, 1)
            em("vector", f_bs)
            s_splo.n += 1
            bt_w = (s_splo, s_splo.n)
            btile += 1

            def f_dm(e, ecc=ecc, sl_b=sl_b, oc=oc,
                     w=W(bt_w,
                         (s_csp, csp_base + (ecc + 1) * c.NTH if oc == 0 else 0),
                         (s_dev, oc * c.NT if ecc == 0 else 0))):
                w(e)
                for tt in range(c.NT):
                    chs = csbh[:, ecc * c.TOK + tt * 128:ecc * c.TOK + (tt + 1) * 128]
                    cls = csbl[:, ecc * c.TOK + tt * 128:ecc * c.TOK + (tt + 1) * 128]
                    e.matmul(bank(tt), chs, wsb3h[sl_b], start=(ecc == 0), stop=False)
                    e.matmul(bank(tt), cls, wsb3h[sl_b], start=False, stop=False)
                    ins = e.matmul(bank(tt), chs, wsb3l[sl_b], start=False,
                                   stop=(ecc == c.NERC - 1))
                ins.then_inc(s_bfr.h, 1)
            em("tensor", f_dm)
            s_bfr.n += 1

        for tt in range(c.NT):
            dl = (oc * c.NT + tt) % 4
            extra = st1_finals if (oc == 0 and tt == 0) else []
            DMA("gpsimd", bstg[dl], base_s[tt * 128:(tt + 1) * 128, oc * 512:(oc + 1) * 512],
                bst4[dl], (s_dev, max(0, s_dev.n - 3)), *extra)

            def f_dv(e, tt=tt, dl=dl, w=W((s_bfr, s_bfr.n), (bst4[dl], bst4[dl].n),
                                          (s_sil, max(0, s_sil.n - 3)))):
                w(e)
                e.scalar_tensor_tensor(t1st[dl], bank(tt), 0.5, bstg[dl],
                                       OP.mult, OP.add).then_inc(s_dev.h, 1)
            em("vector", f_dv)
            s_dev.n += 1

            def f_sil(e, dl=dl, w=W((s_dev, s_dev.n), (st34[dl], st34[dl].n))):
                w(e)
                e.activation(gstg[dl], t1st[dl], AF.Silu).then_inc(s_sil.h, 1)
            em("scalar", f_sil)
            s_sil.n += 1
            DMA("gpsimd", g_s[tt * 128:(tt + 1) * 128, oc * 512:(oc + 1) * 512],
                gstg[dl], st34[dl], (s_sil, s_sil.n))

    quiesce("sync", ldb)
    quiesce("gpsimd", bst4 + st34 + [s_dup])
    run_phase()

    amark[0] = PERSIST0
    kf2 = [alloc(c.S), alloc(c.S)]
    vt2 = [alloc(c.S), alloc(c.S)]
    scb = [alloc(c.S), alloc(c.S)]
    pT = alloc(c.KB * 256)
    pT3 = pT.rearrange("p (b w) -> p b w", w=256)
    slopek = alloc(c.S)
    negqs = alloc(c.KB)
    qtl = [alloc(128), alloc(128)]
    astg = [alloc(256), alloc(256)]
    NTASK = c.B * HPN
    NTB = c.TOK // 128
    NCH = c.KB // 2
    scg = [0]
    sev_bank = [0, 0]
    tb_ctr = [0]
    TBANKS = (2, 3, 6, 7)
    ptv_bank = [0, 0, 0, 0]
    last_smm = {0: 0, 1: 0}
    ptm_scb = {0: 0, 1: 0}
    e1_after = {0: 0, 1: 0}
    rs_after = {0: 0, 1: 0}
    e2_after = {0: 0, 1: 0}
    aev_after = {0: 0, 1: 0}
    kfree = {0: 0, 1: 0}
    vfree = {0: 0, 1: 0}
    pv_chunk = [0]

    def emit_pt(t, m, ql, e2n):
        qt = 2 * m + ql
        qp = qt % 2
        P = qt + 1
        NG = (P + 3) // 4
        for gk in range(NG):
            nb = min(4, P - gk * 4)
            tb = tb_ctr[0] % 4
            tbk = TBANKS[tb]
            tb_ctr[0] += 1

            def f_pt(e, qp=qp, gk=gk, nb=nb, tbk=tbk, tb=tb, e2n=e2n,
                     w=W((s_e2, e2n if gk == 0 else 0),
                         (s_ptv, ptv_bank[tb]))):
                w(e)
                for q4 in range(nb):
                    kb = gk * 4 + q4
                    ins = e.transpose(bank(tbk)[:, q4 * 128:(q4 + 1) * 128],
                                      scb[qp][:, kb * 128:(kb + 1) * 128], identt)
                ins.then_inc(s_ptm.h, 1)
            em("tensor", f_pt)
            s_ptm.n += 1
            last_g = (ql == 1 and gk == NG - 1)

            def f_ptv(e, ql=ql, gk=gk, nb=nb, tbk=tbk, m=m, last_g=last_g,
                      w=W((s_ptm, s_ptm.n),
                          (s_pv, pv_chunk[0] if (ql == 0 and gk == 0) else 0))):
                w(e)
                ins = e.tensor_copy(
                    pT3[:, gk * 4:gk * 4 + nb, ql * 128:ql * 128 + 128],
                    bank(tbk)[:, 0:nb * 128].rearrange("p (b w) -> p b w",
                                                       w=128))
                if last_g:
                    ins = e.memset(pT[:, (2 * m + 1) * 256:(2 * m + 1) * 256
                                      + 128], 0.0)
                ins.then_inc(s_ptv.h, 1)
            em("vector", f_ptv)
            s_ptv.n += 1
            ptv_bank[tb] = s_ptv.n
        ptm_scb[qp] = s_ptm.n

    def emit_pvtail(t, tp, m, cp):
        NKB = 2 * m + 2

        def f_pv(e, tp=tp, cp=cp, NKB=NKB,
                 w=W((s_ptv, s_ptv.n), (s_aev, aev_after[cp]))):
            w(e)
            for kb in range(NKB):
                ins = e.matmul(bank(4 + cp)[:, 0:256],
                               vt2[tp][:, kb * 128:(kb + 1) * 128],
                               pT[:, kb * 256:(kb + 1) * 256],
                               start=(kb == 0), stop=(kb == NKB - 1))
            ins.then_inc(s_pv.h, 1)
        em("tensor", f_pv)
        s_pv.n += 1
        pv_chunk[0] = s_pv.n

        def f_aev(e, cp=cp, w=W((s_pv, s_pv.n), (ast[cp], ast[cp].n))):
            w(e)
            e.tensor_copy(astg[cp], bank(4 + cp)[:, 0:256]).then_inc(s_aev.h, 1)
        em("vector", f_aev)
        s_aev.n += 1
        aev_after[cp] = s_aev.n
        rpq = (2 * m * 128) // c.TOK
        qc2 = 2 * m * 128 - rpq * c.TOK
        aor = ((t // HPN) * c.R4 + rpq) * HPN * 128 + (t % HPN) * 128
        DMA("gpsimd", a2aao_i[aor:aor + 128, qc2:qc2 + 256], astg[cp],
            ast[cp], (s_aev, s_aev.n))

    sev_task = 0
    for t in range(NTASK):
        tp = t % 2
        hd = t % HPN
        bsel = t // HPN
        first = True
        for rp in range(c.R4):
            snd = bsel * c.R4 + rp
            gw_ = (((s_smm, kfree[tp]), (s_cc, cc_qkv)) if first else ())
            first = False
            krow = snd * QKR + HPN * 128 + hd * 128
            DMA("sync", kf2[tp][:, rp * c.TOK:(rp + 1) * c.TOK],
                a2aqk_o[krow:krow + 128, :], s_kv, *gw_)
        vw_ = ((s_pv, vfree[tp]),)
        for rp in range(c.R4):
            snd = bsel * c.R4 + rp
            for lb in range(NTB):
                vrow = snd * HPN * c.TOK + hd * c.TOK + lb * 128
                vcol = rp * c.TOK + lb * 128
                DMA("sync", vt2[tp][:, vcol:vcol + 128],
                    a2av_o[vrow:vrow + 128, :], s_kv, *vw_)
                vw_ = ()
        kv_here = s_kv.n

        DMA("sync", slopek, kpos_d, s_dbs, (s_sev, sev_task))

        def f_nqs(e, t=t, w=W((s_dbs, s_dbs.n))):
            w(e)
            e.tensor_scalar(slopek, slopek, slvt[:, t:t + 1], None, OP.mult)
            e.tensor_scalar(negqs, negqc, slvt[:, t:t + 1], None,
                            OP.mult).then_inc(s_dbv.h, 1)
        em("vector", f_nqs)
        s_dbv.n += 1
        dbv_task = s_dbv.n

        def emit_qt(m, ql, kv_first, t=t, tp=tp, hd=hd, bsel=bsel,
                    dbv_task=dbv_task, kv_here=kv_here):
            qt = 2 * m + ql
            qp = qt % 2
            PW = (qt + 1) * 128
            SCq = (PW + 511) // 512
            rp = (qt * 128) // c.TOK
            qcol = qt * 128 - rp * c.TOK
            qrow = (bsel * c.R4 + rp) * QKR + hd * 128
            DMA("sync", qtl[qp], a2aqk_o[qrow:qrow + 128, qcol:qcol + 128],
                qfl[qp], (s_smm, last_smm[qp]))
            dstart = qt * 128
            for sc in range(SCq):
                c0 = sc * 512
                wdt = min(512, PW - c0)
                bp = scg[0] % 2
                scg[0] += 1

                def f_smm(e, qp=qp, sc=sc, wdt=wdt, bp=bp, tp=tp,
                          kvw=(kv_here if (kv_first and sc == 0) else 0),
                          w=W((qfl[qp], qfl[qp].n if sc == 0 else 0),
                              (s_sev, sev_bank[bp]))):
                    if kvw:
                        e.wait_ge(s_kv.h, kvw)
                    w(e)
                    e.matmul(bank(bp)[:, 0:wdt], qtl[qp],
                             kf2[tp][:, sc * 512:sc * 512 + wdt],
                             start=True, stop=True).then_inc(s_smm.h, 1)
                em("tensor", f_smm)
                s_smm.n += 1
                w1 = wdt if c0 + wdt <= dstart else max(0, dstart - c0)

                def f_sev(e, qt=qt, qp=qp, t=t, c0=c0, wdt=wdt, w1=w1, bp=bp,
                          w=W((s_smm, s_smm.n), (s_dbv, dbv_task),
                              (s_ptm, ptm_scb[qp] if sc == 0 else 0))):
                    w(e)
                    ins = None
                    if w1 > 0:
                        ins = e.scalar_tensor_tensor(
                            scb[qp][:, c0:c0 + w1], slopek[:, c0:c0 + w1],
                            negqs[:, qt:qt + 1], bank(bp)[:, 0:w1],
                            OP.add, OP.add)
                    if w1 < wdt:
                        ins = e.scalar_tensor_tensor(
                            scb[qp][:, c0 + w1:c0 + wdt], dgt,
                            slvt[:, t:t + 1], bank(bp)[:, w1:wdt],
                            OP.mult, OP.add)
                    ins.then_inc(s_sev.h, 1)
                em("vector", f_sev)
                s_sev.n += 1
                sev_bank[bp] = s_sev.n
            last_smm[qp] = s_smm.n

            def f_mx(e, qp=qp, PW=PW, w=W((s_sev, s_sev.n),
                                          (s_e1, e1_after[qp]))):
                w(e)
                e.reduce_max(smx(20 + qp), scb[qp][:, 0:PW],
                             axis=AX).then_inc(s_mx.h, 1)
            em("vector", f_mx)
            s_mx.n += 1

            def f_e1(e, qp=qp, PW=PW, w=W((s_mx, s_mx.n),
                                          (s_rs, rs_after[qp]))):
                w(e)
                e.activation(smx(22 + qp), smx(20 + qp), AF.Identity,
                             scale=-1.0)
                e.drain()
                e.activation(scb[qp][:, 0:PW], scb[qp][:, 0:PW], AF.Exp,
                             bias=smx(22 + qp),
                             accum_out=smx(24 + qp)).then_inc(s_e1.h, 1)
            em("scalar", f_e1)
            s_e1.n += 1
            e1_after[qp] = s_e1.n

            def f_rs(e, qp=qp, w=W((s_e1, s_e1.n), (s_e2, e2_after[qp]))):
                w(e)
                e.reciprocal(smx(26 + qp), smx(24 + qp)).then_inc(s_rs.h, 1)
            em("vector", f_rs)
            s_rs.n += 1
            rs_after[qp] = s_rs.n

            def f_e2(e, qp=qp, PW=PW, w=W((s_rs, s_rs.n))):
                w(e)
                e.activation(scb[qp][:, 0:PW], scb[qp][:, 0:PW], AF.Identity,
                             scale=smx(26 + qp)).then_inc(s_e2.h, 1)
            em("scalar", f_e2)
            s_e2.n += 1
            e2_after[qp] = s_e2.n
            return s_e2.n

        pend = None
        for m in range(NCH):
            e2q0 = emit_qt(m, 0, kv_first=(m == 0))
            if pend is not None:
                emit_pt(t, pend[0], 1, pend[1])
                emit_pvtail(t, tp, pend[0], pend[0] % 2)
            e2q1 = emit_qt(m, 1, kv_first=False)
            emit_pt(t, m, 0, e2q0)
            pend = (m, e2q1)
        emit_pt(t, pend[0], 1, pend[1])
        emit_pvtail(t, tp, pend[0], pend[0] % 2)
        kfree[tp] = s_smm.n
        vfree[tp] = s_pv.n
        sev_task = s_sev.n

    def f_ao(e, w=W((ast[0], ast[0].n), (ast[1], ast[1].n))):
        w(e)
        from concourse.aot_env import allow_ici_all_to_all
        with allow_ici_all_to_all():
            e.collective_compute("AllToAll", OP.bypass,
                                 replica_groups=[list(range(c.NC))],
                                 ins=[a2aao_i.opt()],
                                 outs=[a2aao_o.opt()]).then_inc(s_cc.h, 1)
    em("gpsimd", f_ao)
    s_cc.n += 1
    cc_ao = s_cc.n
    quiesce("sync", [s_kv, s_dbs] + qfl)
    run_phase()
    amark[0] = PERSIST0

    atnh = alloc(c.KD * c.CHT // 2)[:, :].bitcast(BF16)
    atnl = alloc(c.KD * c.CHT // 2)[:, :].bitcast(BF16)
    atmp = [alloc(c.CHT), alloc(c.CHT)]
    alo = [alloc(c.CHT), alloc(c.CHT)]
    wsb5h = [alloc(256)[:, :].bitcast(BF16) for _ in range(6)]
    wsb5l = [alloc(256)[:, :].bitcast(BF16) for _ in range(6)]
    gld5 = [alloc(512), alloc(512)]
    xld5 = [alloc(512), alloc(512)]
    t1s = [alloc(512), alloc(512)]
    ost = [alloc(512), alloc(512)]
    ast_finals = [(s, s.n) for s in ast]
    st3_finals = [(s, s.n) for s in st3]
    TPT = c.TPT
    wfr5_after = []

    wfp5 = [alloc(512), alloc(512)]
    wlo5 = alloc(512)
    w5_first = [True]
    for th in range(c.NTH):
        for fc in range(c.KD):
            sx = fc % 2
            extra = ([(s_wfr, wfr5_after[th * c.NBC - 1] if th > 0 else 0),
                      (s_cc, cc_ao)]
                     if fc == 0 else [(s_asp, max(0, s_asp.n - 1))])
            DMA("gpsimd", atmp[sx],
                a2aao_o[fc * 128:(fc + 1) * 128, th * c.CHT:(th + 1) * c.CHT],
                lat[sx], *extra)

            def f_ah(e, sx=sx, fc=fc, w=W((lat[sx], lat[sx].n), (s_asp, max(0, s_asp.n - 1)))):
                w(e)
                e.activation(atnh[:, fc * c.CHT:(fc + 1) * c.CHT], atmp[sx],
                             AF.Identity).then_inc(s_sphi.h, 1)
            em("scalar", f_ah)
            s_sphi.n += 1

            def f_al(e, sx=sx, fc=fc, w=W((s_sphi, s_sphi.n))):
                w(e)
                e.scalar_tensor_tensor(alo[sx], atnh[:, fc * c.CHT:(fc + 1) * c.CHT],
                                       -1.0, atmp[sx], OP.mult, OP.add).then_inc(s_splo.h, 1)
            em("vector", f_al)
            s_splo.n += 1

            def f_ac(e, sx=sx, fc=fc, w=W((s_splo, s_splo.n))):
                w(e)
                e.copy(atnl[:, fc * c.CHT:(fc + 1) * c.CHT], alo[sx]).then_inc(s_asp.h, 1)
            em("scalar", f_ac)
            s_asp.n += 1
        lat_here = s_asp.n
        for oc in range(c.NBC):
            cgi5 = th * c.NBC + oc
            par = (cgi5 % 2) * 4
            for fc in range(c.KD):
                sl_w = (cgi5 * c.KD + fc) % 6
                if th == 0:
                    sf = fc % 2
                    DMA("sync", wfp5[sf],
                        woutT[fc * 128:(fc + 1) * 128, oc * 512:(oc + 1) * 512],
                        spl[sf], (s_splo, max(0, s_splo.n - 1)))

                    def f_w5(e, sf=sf, sl_w=sl_w,
                             w=W((spl[sf], spl[sf].n), (s_wfr, max(0, s_wfr.n - 5)),
                                 (sps[sl_w % 4], sps[sl_w % 4].n))):
                        w(e)
                        e.tensor_copy(wsb5h[sl_w], wfp5[sf])
                        e.drain()
                        e.scalar_tensor_tensor(wlo5, wsb5h[sl_w], -1.0, wfp5[sf],
                                               OP.mult, OP.add)
                        e.drain()
                        e.tensor_copy(wsb5l[sl_w], wlo5).then_inc(s_splo.h, 1)
                    em("vector", f_w5)
                    s_splo.n += 1
                    DMA("gpsimd", woutT_h[fc * 128:(fc + 1) * 128, oc * 512:(oc + 1) * 512],
                        wsb5h[sl_w], sps[sl_w % 4], (s_splo, s_splo.n))
                    DMA("gpsimd", woutT_l[fc * 128:(fc + 1) * 128, oc * 512:(oc + 1) * 512],
                        wsb5l[sl_w], sps[sl_w % 4], (s_splo, 0))
                    w5_w = ((s_splo, s_splo.n),)
                else:
                    extra5 = [(x, x.n) for x in sps] if w5_first[0] else []
                    w5_first[0] = False
                    DMA("sync", wsb5h[sl_w],
                        woutT_h[fc * 128:(fc + 1) * 128, oc * 512:(oc + 1) * 512], ldw[sl_w],
                        (s_wfr, max(0, s_wfr.n - 5)), *extra5)
                    DMA("sync", wsb5l[sl_w],
                        woutT_l[fc * 128:(fc + 1) * 128, oc * 512:(oc + 1) * 512],
                        ldwl[sl_w])
                    w5_w = ((ldw[sl_w], ldw[sl_w].n), (ldwl[sl_w], ldwl[sl_w].n))

                def f_wm(e, fc=fc, par=par, sl_w=sl_w,
                         w=W(*w5_w,
                             (s_asp, lat_here if (oc == 0 and fc == 0) else 0),
                             (s_wv1, (cgi5 - 2 + 1) * TPT if (cgi5 >= 2 and fc == 0) else 0))):
                    w(e)
                    for tt in range(TPT):
                        ah_ = atnh[:, fc * c.CHT + tt * 128:fc * c.CHT + tt * 128 + 128]
                        al_ = atnl[:, fc * c.CHT + tt * 128:fc * c.CHT + tt * 128 + 128]
                        e.matmul(bank(par + tt), ah_, wsb5h[sl_w], start=(fc == 0), stop=False)
                        e.matmul(bank(par + tt), al_, wsb5h[sl_w], start=False, stop=False)
                        ins = e.matmul(bank(par + tt), ah_, wsb5l[sl_w], start=False,
                                       stop=(fc == c.KD - 1))
                    ins.then_inc(s_wfr.h, 1)
                em("tensor", f_wm)
                s_wfr.n += 1
            wfr5_after.append(s_wfr.n)

            for tt in range(TPT):
                dl = (cgi5 * TPT + tt) % 2
                trow = th * c.CHT + tt * 128
                extra = st3_finals if (cgi5 == 0 and tt == 0) else []
                DMA("gpsimd", gld5[dl], g_s[trow:trow + 128, oc * 512:(oc + 1) * 512],
                    gx[dl], (s_wv1, max(0, s_wv1.n - 1)), *extra)
                DMA("gpsimd", xld5[dl], xsh[trow:trow + 128, oc * 512:(oc + 1) * 512], gx[dl])

                def f_wv(e, par=par, tt=tt, dl=dl,
                         w=W((s_wfr, s_wfr.n), (gx[dl], gx[dl].n),
                             (yst[dl], yst[dl].n))):
                    w(e)
                    e.tensor_tensor(t1s[dl], bank(par + tt), gld5[dl], OP.mult)
                    e.drain()
                    e.tensor_tensor(ost[dl], t1s[dl], xld5[dl], OP.add).then_inc(s_wv1.h, 1)
                em("vector", f_wv)
                s_wv1.n += 1
                DMA("gpsimd", y[trow:trow + 128, oc * 512:(oc + 1) * 512], ost[dl],
                    yst[dl], (s_wv1, s_wv1.n))

    def f_fin(e, w=W((yst[0], yst[0].n), (yst[1], yst[1].n))):
        w(e)
    em("gpsimd", f_fin)
    quiesce("sync", ldw + ldwl)
    quiesce("gpsimd", gx + lat)
    run_phase()
    return nc


def shard_inputs(c, L0_attn_hidden_states, L0_attn_Wqkv, L0_attn_q_ln_w, L0_attn_q_ln_b,
                 L0_attn_k_ln_w, L0_attn_k_ln_b, L0_attn_Wout, L0_ffn_base_w,
                 L0_ffn_base_b, L0_ffn_A, L0_ffn_B, L0_ffn_router_w):
    x = np.asarray(L0_attn_hidden_states, dtype=np.float32)
    com = {
        "wqkvT": np.ascontiguousarray(np.asarray(L0_attn_Wqkv, np.float32).T),
        "woutT": np.ascontiguousarray(np.asarray(L0_attn_Wout, np.float32).T),
        "basewT": np.ascontiguousarray(np.asarray(L0_ffn_base_w, np.float32).T),
        "baseb": np.asarray(L0_ffn_base_b, np.float32).reshape(1, c.D),
        "aT": np.ascontiguousarray(np.asarray(L0_ffn_A, np.float32).T),
        "routerT": np.ascontiguousarray(np.asarray(L0_ffn_router_w, np.float32).T),
        "bflat": np.ascontiguousarray(
            np.asarray(L0_ffn_B, np.float32).transpose(0, 2, 1).reshape(c.ER, c.D)),
        "qlnw": np.asarray(L0_attn_q_ln_w, np.float32).reshape(1, c.D),
        "qlnb": np.asarray(L0_attn_q_ln_b, np.float32).reshape(1, c.D),
        "klnw": np.asarray(L0_attn_k_ln_w, np.float32).reshape(1, c.D),
        "klnb": np.asarray(L0_attn_k_ln_b, np.float32).reshape(1, c.D),
    }
    in_maps = []
    for core in range(c.NC):
        b, r = core // c.R4, core % c.R4
        sh = dict(com)
        sh["xsh"] = np.ascontiguousarray(x[b, r * c.TOK:(r + 1) * c.TOK, :])
        sh["qpos"] = np.arange(r * c.TOK, (r + 1) * c.TOK, dtype=np.float32).reshape(c.TOK, 1)
        hpn = c.H // c.NC
        sl = alibi_slopes(c.H)[core * hpn:(core + 1) * hpn]
        sh["slv"] = np.ascontiguousarray(np.broadcast_to(
            np.tile(sl, c.B).reshape(1, c.B * hpn), (128, c.B * hpn)))
        in_maps.append(sh)
    return in_maps


def gather_out(c, results):
    out = np.empty((c.B, c.S, c.D), np.float32)
    for core in range(c.NC):
        b, r = core // c.R4, core % c.R4
        out[b, r * c.TOK:(r + 1) * c.TOK, :] = results[core]["y"]
    return out


def kernel(**inputs):
    c = Cfg()
    in_maps = shard_inputs(c, **inputs)
    nc = build_program(c)
    res = run_bass_kernel_spmd(nc, in_maps, core_ids=list(range(c.NC)))
    return gather_out(c, res.results)


# version 14
# speedup vs baseline: 1.0213x; 1.0008x over previous
import math
import numpy as np
import concourse.bass as bass
import concourse.mybir as mybir
from concourse.bass_utils import run_bass_kernel_spmd

F32 = mybir.dt.float32
BF16 = mybir.dt.bfloat16
AX = mybir.AxisListType.X
OP = mybir.AluOpType
AF = mybir.ActivationFunctionType

LN_EPS = 1e-5
CLIP = 8.0
MASKMUL = -1e27
ARENA_COLS = 45056


def alibi_slopes(n_heads, bias_max=8.0):
    next_pow2 = 2 ** int(np.ceil(np.log2(n_heads)))
    m = np.arange(1, next_pow2 + 1, dtype=np.float32) * (bias_max / next_pow2)
    slopes = 1.0 / np.power(2.0, m)
    if next_pow2 != n_heads:
        slopes = np.concatenate([slopes[1::2], slopes[::2]])[:n_heads]
    return slopes.astype(np.float32)


class Cfg:
    def __init__(self, D=4096, S=4096, B=2, H=32, NE=16, NR=64):
        self.D, self.S, self.B, self.H = D, S, B, H
        self.DH = 128
        assert H * self.DH == D
        self.NC = 8
        self.R4 = self.NC // B
        self.TOK = B * S // self.NC
        self.NT = self.TOK // 128
        self.KD = D // 128
        self.CH = 512
        self.CHT = min(512, self.TOK)
        self.NTH = self.TOK // self.CHT
        self.TPT = self.CHT // 128
        self.NQC = 3 * D // self.CH
        self.NDC = D // self.CH
        self.NBC = D // self.CH
        self.NE, self.NR = NE, NR
        self.ER = NE * NR
        self.NERC = self.ER // 128
        self.SC = S // self.CH
        self.KB = S // 128
        self.HQ = min(2, H // self.R4)
        self.NHG = H // self.HQ


class Sem:
    def __init__(self, nc, name):
        self.h = nc.alloc_semaphore(name=name)
        self.n = 0


def build_program(c):
    nc = bass.Bass(trn_type="TRN2", target_bir_lowering=False)

    xsh = nc.dram_tensor("xsh", [c.TOK, c.D], F32, kind="ExternalInput").ap()
    qpos = nc.dram_tensor("qpos", [c.TOK, 1], F32, kind="ExternalInput").ap()
    slv = nc.dram_tensor("slv", [128, c.H // c.R4], F32, kind="ExternalInput").ap()
    wqkvT = nc.dram_tensor("wqkvT", [c.D, 3 * c.D], F32, kind="ExternalInput").ap()
    woutT = nc.dram_tensor("woutT", [c.D, c.D], F32, kind="ExternalInput").ap()
    basewT = nc.dram_tensor("basewT", [c.D, c.D], F32, kind="ExternalInput").ap()
    baseb = nc.dram_tensor("baseb", [1, c.D], F32, kind="ExternalInput").ap()
    aT = nc.dram_tensor("aT", [c.D, c.NR], F32, kind="ExternalInput").ap()
    routerT = nc.dram_tensor("routerT", [c.D, c.NE], F32, kind="ExternalInput").ap()
    bflat = nc.dram_tensor("bflat", [c.ER, c.D], F32, kind="ExternalInput").ap()
    qlnw = nc.dram_tensor("qlnw", [1, c.D], F32, kind="ExternalInput").ap()
    qlnb = nc.dram_tensor("qlnb", [1, c.D], F32, kind="ExternalInput").ap()
    klnw = nc.dram_tensor("klnw", [1, c.D], F32, kind="ExternalInput").ap()
    klnb = nc.dram_tensor("klnb", [1, c.D], F32, kind="ExternalInput").ap()
    y = nc.dram_tensor("y", [c.TOK, c.D], F32, kind="ExternalOutput").ap()

    xnf_s = nc.dram_tensor("xnf_s", [c.D, c.TOK], F32).ap()
    qs_s = nc.dram_tensor("qs_s", [c.TOK, c.D], F32).ap()
    ks_s = nc.dram_tensor("ks_s", [c.TOK, c.D], F32).ap()
    HPN = c.H // c.NC
    QKR = 2 * HPN * 128
    a2aqk_i = nc.dram_tensor("a2aqk_i", [c.NC * QKR, c.TOK], F32).ap()
    a2aqk_o = nc.dram_tensor("a2aqk_o", [c.NC * QKR, c.TOK], F32).ap()
    a2av_i = nc.dram_tensor("a2av_i", [c.NC * HPN * c.TOK, c.DH], F32).ap()
    a2av_o = nc.dram_tensor("a2av_o", [c.NC * HPN * c.TOK, c.DH], F32).ap()
    a2aao_i = nc.dram_tensor("a2aao_i", [c.NC * HPN * 128, c.TOK], F32).ap()
    a2aao_o = nc.dram_tensor("a2aao_o", [c.NC * HPN * 128, c.TOK], F32).ap()
    base_s = nc.dram_tensor("base_s", [c.TOK, c.D], F32).ap()
    g_s = nc.dram_tensor("g_s", [c.TOK, c.D], F32).ap()
    wqkvT_h = nc.dram_tensor("wqkvT_h", [c.D, 3 * c.D], BF16).ap()
    wqkvT_l = nc.dram_tensor("wqkvT_l", [c.D, 3 * c.D], BF16).ap()
    basewT_h = nc.dram_tensor("basewT_h", [c.D, c.D], BF16).ap()
    basewT_l = nc.dram_tensor("basewT_l", [c.D, c.D], BF16).ap()
    woutT_h = nc.dram_tensor("woutT_h", [c.D, c.D], BF16).ap()
    woutT_l = nc.dram_tensor("woutT_l", [c.D, c.D], BF16).ap()
    bflat_h = nc.dram_tensor("bflat_h", [c.ER, c.D], BF16).ap()
    bflat_l = nc.dram_tensor("bflat_l", [c.ER, c.D], BF16).ap()

    ident_d = nc.inline_tensor(np.eye(128, dtype=np.float32), name="identc").ap()
    kpos_d = nc.inline_tensor(
        np.broadcast_to(np.arange(c.S, dtype=np.float32), (128, c.S)).copy(),
        name="kposc").ap()
    sel_np = np.zeros((c.NE, c.NERC * 128), np.float32)
    for _ecc in range(c.NERC):
        for _m in range(128):
            sel_np[(_ecc * 128 + _m) // c.NR, _ecc * 128 + _m] = 1.0
    sel2_d = nc.inline_tensor(sel_np, name="sel2c").ap()
    ones1_d = nc.inline_tensor(np.ones((1, 128), np.float32), name="ones1c").ap()
    negq_np = -(np.arange(c.KB, dtype=np.float32)[None, :] * 128
                + np.arange(128, dtype=np.float32)[:, None])
    negq_d = nc.inline_tensor(np.ascontiguousarray(negq_np), name="negqc").ap()
    dg_np = (np.arange(128, dtype=np.float32)[None, :]
             - np.arange(128, dtype=np.float32)[:, None])
    dg_np = np.where(dg_np <= 0, dg_np, np.float32(-1e30))
    dg_d = nc.inline_tensor(np.ascontiguousarray(dg_np), name="dgmask").ap()
    slopes = alibi_slopes(c.H)

    psall = nc.alloc_psum_tensor("psall", [128, 8 * 512], F32).ap()

    def bank(i):
        return psall[:, i * 512:(i + 1) * 512]

    arena = nc.alloc_sbuf_tensor("arena", [128, ARENA_COLS], F32).ap()
    amark = [0]

    def alloc(cols):
        ap = arena[:, amark[0]:amark[0] + cols]
        amark[0] += cols
        assert amark[0] <= ARENA_COLS, f"arena overflow {amark[0]}"
        return ap

    identt = alloc(128)
    onest = alloc(128)
    dgt = alloc(128)
    selt = alloc(c.NERC * 128)
    basebt = alloc(c.D)
    gqs, gqb, gks, gkb = alloc(c.H), alloc(c.H), alloc(c.H), alloc(c.H)
    qpost = alloc(c.NT)
    negqp = alloc(c.NT)
    negqc = alloc(c.KB)
    slvt = alloc(c.H // c.R4)
    sm = alloc(16 * 36)
    epst = alloc(16)


    def smx(i):
        return sm[:, i * 16:i * 16 + 1]
    PERSIST0 = amark[0]
    logits = alloc(c.NT * c.NE)
    rh2 = alloc(c.TOK)
    combf = alloc(c.TOK)
    atc = alloc(c.KD * c.NR)
    rtc = alloc(c.KD * c.NE)

    def S_(nm):
        return Sem(nc, nm)

    def slots(nm, n):
        return [S_(f"{nm}{i}") for i in range(n)]

    s_sq, s_vst, s_sct, s_xn = S_("sq"), S_("vst"), S_("sct"), S_("xn")
    s_tr0, s_ev0, s_wfr, s_mm1, s_ev1 = (S_(n) for n in ("tr0", "ev0", "wfr", "mm1", "ev1"))
    s_sq2, s_vst2, s_sct2, s_qn, s_tr2, s_ev2 = s_sq, s_vst, s_sct, s_xn, s_tr0, s_ev0
    s_cc, s_ffv, s_ffs, s_trc, s_cbm, s_cbv = (S_(n) for n in
        ("cc", "ffv", "ffs", "trc", "cbm", "cbv"))
    s_bfr, s_dev, s_sil = (S_(n) for n in ("bfr", "dev", "sil"))
    s_dbs, s_dbv, s_smm, s_sev, s_mx, s_e1, s_rs, s_e2 = (S_(n) for n in
        ("dbs", "dbv", "smm", "sev", "mx", "e1", "rs", "e2"))
    s_ptm, s_ptv, s_pv, s_aev, s_wv1 = (S_(n) for n in
        ("ptm", "ptv", "pv", "aev", "wv1"))
    s_ini = S_("ini")
    ldx = slots("ldx", 2)
    st0 = slots("st0", 2)
    ldxn = slots("ldxn", 2)
    s_ldpre = S_("ldpre")
    ldw = slots("ldw", 6)
    st1 = slots("st1", 6)
    lqk = slots("lqk", 2)
    st2 = slots("st2", 8)
    ldb = ldw
    bst = slots("bst", 2)
    st3 = slots("st3", 2)
    s_dup = S_("dup")
    s_loc = S_("loc")
    s_kv = S_("kv")
    qfl = slots("qfl", 2)
    ast = slots("ast", 2)
    lat = slots("lat", 2)
    gx = slots("gx", 2)
    yst = slots("yst", 2)
    spl = slots("spl", 4)
    sps = slots("sps", 4)
    s_sphi = S_("sphi")
    s_splo = S_("splo")
    s_splc = S_("splc")
    ldwl = slots("ldwl", 6)
    s_xsp = S_("xsp")
    s_csp = s_xsp
    s_asp = s_xsp

    ENGS = ("sync", "gpsimd", "tensor", "vector", "scalar")
    ops = {e: [] for e in ENGS}

    def em(eng, fn):
        ops[eng].append(fn)

    def run_phase():
        nonlocal ops
        with nc.Block() as b:
            for e in ENGS:
                if ops[e]:
                    getattr(b, e)(lambda eng, fns=tuple(ops[e]): [f(eng) for f in fns])
        nc.all_engine_barrier()
        ops = {e: [] for e in ENGS}

    def W(*pairs):
        frozen = tuple((s.h, v) for (s, v) in pairs if v > 0)

        def apply(e):
            for h, v in frozen:
                e.wait_ge(h, v)
        return apply

    def DMA(eng, out, in_, sem, *waits, slow=False):
        w = W(*waits)

        def f(e, out=out, in_=in_):
            w(e)
            e.dma_start(out=out, in_=in_,
                        allow_slow_non_contiguous=slow).then_inc(sem.h, 16)
        em(eng, f)
        sem.n += 16
        return sem.n

    def quiesce(eng, sems):
        pairs = tuple((sx.h, sx.n) for sx in sems if sx.n > 0)

        def f(e):
            for h, v in pairs:
                e.wait_ge(h, v)
        em(eng, f)

    DMA("sync", identt, ident_d, s_ini)
    DMA("sync", dgt, dg_d, s_ini)
    DMA("sync", selt[0:c.NE, :], sel2_d, s_ini)
    DMA("sync", onest[0:1, :], ones1_d, s_ini)
    DMA("sync", basebt[0:1, :], baseb, s_ini)
    for dst, src in ((gqs, qlnw), (gqb, qlnb), (gks, klnw), (gkb, klnb)):
        DMA("sync", dst, src[0, :].rearrange("(h d) -> d h", d=128), s_ini, slow=True)
    DMA("sync", qpost, qpos[:, 0].rearrange("(t p) -> p t", p=128), s_ini, slow=True)
    DMA("sync", negqc, negq_d, s_ini)
    DMA("sync", slvt, slv, s_ini)

    def f_initv(e, w=W((s_ini, s_ini.n))):
        w(e)
        isq = 1.0 / math.sqrt(c.DH)
        e.tensor_scalar(gqs, gqs, isq, None, OP.mult)
        e.drain()
        e.tensor_scalar(gqb, gqb, isq, None, OP.mult)
        e.drain()
        e.tensor_scalar(negqp, qpost, -1.0, None, OP.mult)
        e.drain()
        e.memset(epst[:, 0:1], LN_EPS)
    em("vector", f_initv)
    quiesce("sync", [s_ini])
    run_phase()

    am0 = amark[0]
    xa = [alloc(c.D), alloc(c.D)]
    xnb = [alloc(c.D), alloc(c.D)]
    sqs = alloc(c.D)
    stg0 = [alloc(512), alloc(512)]
    NG0 = c.KD // 4

    for i in range(c.NT):
        sl = i % 2
        DMA("sync", xa[sl], xsh[i * 128:(i + 1) * 128, :], ldx[sl],
            (s_xn, max(0, s_xn.n - 1)), (s_sq, max(0, s_sq.n - 1)))

        def f_sc(e, sl=sl, w=W((ldx[sl], ldx[sl].n))):
            w(e)
            e.activation(sqs, xa[sl], AF.Square,
                         accum_out=smx(2 + sl)).then_inc(s_sq.h, 1)
        em("scalar", f_sc)
        s_sq.n += 1

        def f_v1(e, sl=sl, w=W((ldx[sl], ldx[sl].n), (s_sq, s_sq.n))):
            w(e)
            e.reduce_sum(smx(0 + sl), xa[sl], axis=AX)
            e.drain()
            e.tensor_scalar(smx(4 + sl), smx(0 + sl), 1.0 / c.D, None, OP.mult)
            e.tensor_scalar(smx(6 + sl), smx(2 + sl), 1.0 / c.D, None, OP.mult)
            e.drain()
            e.tensor_tensor(smx(8 + sl), smx(4 + sl), smx(4 + sl), OP.mult)
            e.drain()
            e.tensor_tensor(smx(6 + sl), smx(6 + sl), smx(8 + sl),
                            OP.subtract).then_inc(s_vst.h, 1)
        em("vector", f_v1)
        s_vst.n += 1

        def f_sc2(e, sl=sl, w=W((s_vst, s_vst.n))):
            w(e)
            e.activation(smx(10 + sl), smx(6 + sl), AF.Sqrt,
                         bias=epst[:, 0:1]).then_inc(s_sct.h, 1)
        em("scalar", f_sc2)
        s_sct.n += 1

        def f_v2(e, sl=sl, w=W((s_sct, s_sct.n), (s_tr0, max(0, (i - 1) * NG0)))):
            w(e)
            e.reciprocal(smx(12 + sl), smx(10 + sl))
            e.drain()
            e.tensor_scalar(xnb[sl], xa[sl], smx(4 + sl), smx(12 + sl),
                            OP.subtract, OP.mult).then_inc(s_xn.h, 1)
        em("vector", f_v2)
        s_xn.n += 1

        for g in range(NG0):
            def f_tr(e, sl=sl, g=g, w=W((s_xn, s_xn.n), (s_ev0, max(0, s_ev0.n - 1)))):
                w(e)
                for q in range(4):
                    kd = g * 4 + q
                    ins = e.transpose(bank(g % 2)[:, q * 128:(q + 1) * 128],
                                      xnb[sl][:, kd * 128:(kd + 1) * 128], identt)
                ins.then_inc(s_tr0.h, 1)
            em("tensor", f_tr)
            s_tr0.n += 1

            def f_ev(e, g=g, w=W((s_tr0, s_tr0.n), (st0[g % 2], st0[g % 2].n))):
                w(e)
                e.tensor_copy(stg0[g % 2], bank(g % 2)).then_inc(s_ev0.h, 1)
            em("vector", f_ev)
            s_ev0.n += 1

            for q in range(4):
                kd = g * 4 + q
                DMA("gpsimd", xnf_s[kd * 128:(kd + 1) * 128, i * 128:(i + 1) * 128],
                    stg0[g % 2][:, q * 128:(q + 1) * 128], st0[g % 2],
                    (s_ev0, s_ev0.n) if q == 0 else (s_ev0, 0))
    SPW = 512
    wtmp = [None] * 4
    lo_f = [None] * 4
    hi_b = [None] * 4
    lo_b = [None] * 4

    def alloc_split_bufs(n):
        for i in range(n):
            wtmp[i] = alloc(SPW)
            lo_f[i] = alloc(SPW)
            hi_b[i] = alloc(SPW // 2)[:, :].bitcast(BF16)
            lo_b[i] = alloc(SPW // 2)[:, :].bitcast(BF16)

    def emit_split(jobs, nsl):
        spc = [0]
        for srcT, hiT, loT, RR, CC in jobs:
            for r0 in range(RR // 128):
                for c0 in range(CC // SPW):
                    sl = spc[0] % nsl
                    spc[0] += 1
                    DMA("sync", wtmp[sl], srcT[r0 * 128:(r0 + 1) * 128,
                                               c0 * SPW:(c0 + 1) * SPW],
                        spl[sl], (s_splo, max(0, s_splo.n - (nsl - 1))))

                    def f_hi(e, sl=sl, w=W((spl[sl], spl[sl].n),
                                           (s_splo, max(0, s_splo.n - (nsl - 1))),
                                           (sps[sl], sps[sl].n))):
                        w(e)
                        e.tensor_copy(hi_b[sl], wtmp[sl])
                        e.drain()
                        e.scalar_tensor_tensor(lo_f[sl], hi_b[sl], -1.0, wtmp[sl],
                                               OP.mult, OP.add)
                        e.drain()
                        e.tensor_copy(lo_b[sl], lo_f[sl]).then_inc(s_splo.h, 1)
                    em("vector", f_hi)
                    s_splo.n += 1

                    DMA("gpsimd", hiT[r0 * 128:(r0 + 1) * 128, c0 * SPW:(c0 + 1) * SPW],
                        hi_b[sl], sps[sl], (s_splo, s_splo.n))
                    DMA("gpsimd", loT[r0 * 128:(r0 + 1) * 128, c0 * SPW:(c0 + 1) * SPW],
                        lo_b[sl], sps[sl], (s_splo, 0))

    sps_finals = [(x, x.n) for x in sps]
    quiesce("sync", ldx + spl)
    quiesce("gpsimd", st0 + sps)
    run_phase()
    amark[0] = am0

    am1 = amark[0]
    xnfh = alloc(c.KD * c.CHT // 2)[:, :].bitcast(BF16)
    xnfl = alloc(c.KD * c.CHT // 2)[:, :].bitcast(BF16)
    xtmp = [alloc(c.CHT), alloc(c.CHT)]
    xlo = [alloc(c.CHT), alloc(c.CHT)]
    atch = alloc(c.KD * c.NR // 2)[:, :].bitcast(BF16)
    atcl = alloc(c.KD * c.NR // 2)[:, :].bitcast(BF16)
    rtch = alloc(c.KD * c.NE // 2)[:, :].bitcast(BF16)
    rtcl = alloc(c.KD * c.NE // 2)[:, :].bitcast(BF16)
    wsbh = [alloc(256)[:, :].bitcast(BF16) for _ in range(6)]
    wsbl = [alloc(256)[:, :].bitcast(BF16) for _ in range(6)]
    stg1 = [alloc(512) for _ in range(6)]
    NSTG = 6
    TPT = c.TPT
    wfp = [alloc(512), alloc(512)]
    wlo_t = alloc(512)

    chunks = ([("qkv", i) for i in range(c.NQC)] + [("base", i) for i in range(c.NBC)]
              + [("arh", 0), ("rtr", 0)])
    NCHK = len(chunks)
    wtile = 0
    mm1_at_th_end = []
    wfr_at_th_end = []
    st0_finals = [(st0[0], st0[0].n), (st0[1], st0[1].n)]

    for kd in range(c.KD):
        DMA("sync", atc[:, kd * c.NR:(kd + 1) * c.NR], aT[kd * 128:(kd + 1) * 128, :], s_ldpre)
        DMA("sync", rtc[:, kd * c.NE:(kd + 1) * c.NE], routerT[kd * 128:(kd + 1) * 128, :],
            s_ldpre)
    ldpre_fin = s_ldpre.n

    def _chunks(src, hi, lo, total):
        cw = min(512, c.CHT, total)
        return [(src[:, i * cw:(i + 1) * cw], hi[:, i * cw:(i + 1) * cw],
                 lo[:, i * cw:(i + 1) * cw]) for i in range(total // cw)]
    spre_pairs = (_chunks(atc, atch, atcl, c.KD * c.NR)
                  + _chunks(rtc, rtch, rtcl, c.KD * c.NE))
    for fsrc, fhi, flo in spre_pairs:
        def f_spre(e, fsrc=fsrc, fhi=fhi, w=W((s_ldpre, ldpre_fin),
                                              (s_splo, max(0, s_splo.n)))):
            w(e)
            e.activation(fhi, fsrc, AF.Identity).then_inc(s_sphi.h, 1)
        em("scalar", f_spre)
        s_sphi.n += 1

        cw_ = fsrc.shape[1]

        def f_spre2(e, fsrc=fsrc, fhi=fhi, cw_=cw_, w=W((s_sphi, s_sphi.n),
                                                        (s_splc, max(0, s_splc.n)))):
            w(e)
            e.scalar_tensor_tensor(xlo[0][:, 0:cw_], fhi, -1.0, fsrc,
                                   OP.mult, OP.add).then_inc(s_splo.h, 1)
        em("vector", f_spre2)
        s_splo.n += 1

        def f_spre3(e, flo=flo, cw_=cw_, w=W((s_splo, s_splo.n))):
            w(e)
            e.copy(flo, xlo[0][:, 0:cw_]).then_inc(s_splc.h, 1)
        em("scalar", f_spre3)
        s_splc.n += 1
    spre_done = s_splc.n

    suse = 0
    xspc = 0
    wtile_th1_first = [True]
    for th in range(c.NTH):
        for kd in range(c.KD):
            sx = kd % 2
            extra = (st0_finals + [(s_mm1, mm1_at_th_end[th - 1] if th > 0 else 0),
                                   (s_wfr, wfr_at_th_end[th - 1] if th > 0 else 0),
                                   (s_splc, spre_done)]
                     if kd == 0 else [(s_xsp, max(0, s_xsp.n - 1))])
            DMA("gpsimd", xtmp[sx], xnf_s[kd * 128:(kd + 1) * 128, th * c.CHT:(th + 1) * c.CHT],
                ldxn[sx], *extra)

            def f_xh(e, sx=sx, kd=kd, w=W((ldxn[sx], ldxn[sx].n), (s_xsp, max(0, s_xsp.n - 1)))):
                w(e)
                e.activation(xnfh[:, kd * c.CHT:(kd + 1) * c.CHT], xtmp[sx],
                             AF.Identity).then_inc(s_sphi.h, 1)
            em("scalar", f_xh)
            s_sphi.n += 1

            def f_xl(e, sx=sx, kd=kd, w=W((s_sphi, s_sphi.n),
                                          (s_xsp, max(0, s_xsp.n - 1)))):
                w(e)
                e.scalar_tensor_tensor(xlo[sx], xnfh[:, kd * c.CHT:(kd + 1) * c.CHT],
                                       -1.0, xtmp[sx], OP.mult, OP.add).then_inc(s_splo.h, 1)
            em("vector", f_xl)
            s_splo.n += 1

            def f_xc(e, sx=sx, kd=kd, w=W((s_splo, s_splo.n))):
                w(e)
                e.copy(xnfl[:, kd * c.CHT:(kd + 1) * c.CHT], xlo[sx]).then_inc(s_xsp.h, 1)
            em("scalar", f_xc)
            s_xsp.n += 1
        ldxn_here = s_xsp.n

        for ci, (kind, ec) in enumerate(chunks):
            cgi = th * NCHK + ci
            par = (cgi % 2) * 4
            ev_guard = (cgi - 2 + 1) * TPT if cgi >= 2 else 0

            if kind in ("qkv", "base"):
                src0 = wqkvT if kind == "qkv" else basewT
                srch = wqkvT_h if kind == "qkv" else basewT_h
                srcl = wqkvT_l if kind == "qkv" else basewT_l
                for kd in range(c.KD):
                    sl_w = wtile % 6
                    if th == 0:
                        sf = wtile % 2
                        DMA("sync", wfp[sf],
                            src0[kd * 128:(kd + 1) * 128, ec * 512:ec * 512 + 512],
                            spl[sf], (s_splo, max(0, s_splo.n - 1)))

                        def f_ws(e, sf=sf, sl_w=sl_w,
                                 w=W((spl[sf], spl[sf].n), (s_wfr, max(0, wtile - 5)),
                                     (sps[sl_w % 4], sps[sl_w % 4].n))):
                            w(e)
                            e.tensor_copy(wsbh[sl_w], wfp[sf])
                            e.drain()
                            e.scalar_tensor_tensor(wlo_t, wsbh[sl_w], -1.0, wfp[sf],
                                                   OP.mult, OP.add)
                            e.drain()
                            e.tensor_copy(wsbl[sl_w], wlo_t).then_inc(s_splo.h, 1)
                        em("vector", f_ws)
                        s_splo.n += 1
                        DMA("gpsimd",
                            srch[kd * 128:(kd + 1) * 128, ec * 512:ec * 512 + 512],
                            wsbh[sl_w], sps[sl_w % 4], (s_splo, s_splo.n))
                        DMA("gpsimd",
                            srcl[kd * 128:(kd + 1) * 128, ec * 512:ec * 512 + 512],
                            wsbl[sl_w], sps[sl_w % 4], (s_splo, 0))
                        wt_w = ((s_splo, s_splo.n), (ldw[sl_w], 0))
                    else:
                        extra_w = [(x, x.n) for x in sps] if wtile_th1_first[0] else []
                        wtile_th1_first[0] = False
                        DMA("sync", wsbh[sl_w],
                            srch[kd * 128:(kd + 1) * 128, ec * 512:ec * 512 + 512],
                            ldw[sl_w], (s_wfr, max(0, wtile - 5)), *extra_w)
                        DMA("sync", wsbl[sl_w],
                            srcl[kd * 128:(kd + 1) * 128, ec * 512:ec * 512 + 512],
                            ldwl[sl_w])
                        wt_w = ((ldw[sl_w], ldw[sl_w].n), (ldwl[sl_w], ldwl[sl_w].n))
                    wtile += 1

                    def f_mm(e, kd=kd, par=par, sl_w=sl_w, kind=kind,
                             w=W(*wt_w,
                                 (s_xsp, ldxn_here if (kd == 0 and ci == 0) else 0),
                                 (s_ev1, ev_guard if kd == 0 else 0))):
                        w(e)
                        stop = (kd == c.KD - 1) and kind == "qkv"
                        for tt in range(TPT):
                            xh = xnfh[:, kd * c.CHT + tt * 128:kd * c.CHT + tt * 128 + 128]
                            xl = xnfl[:, kd * c.CHT + tt * 128:kd * c.CHT + tt * 128 + 128]
                            e.matmul(bank(par + tt), xh, wsbh[sl_w],
                                     start=(kd == 0 and tt == 0) if False else (kd == 0),
                                     stop=False)
                            e.matmul(bank(par + tt), xl, wsbh[sl_w], start=False, stop=False)
                            ins = e.matmul(bank(par + tt), xh, wsbl[sl_w], start=False,
                                           stop=stop)
                        ins.then_inc(s_wfr.h, 1)
                    em("tensor", f_mm)
                    s_wfr.n += 1

                if kind == "base":
                    def f_bias(e, par=par, ec=ec):
                        for tt in range(TPT):
                            ins = e.matmul(bank(par + tt), onest[0:1, 0:128],
                                           basebt[0:1, ec * 512:ec * 512 + 512],
                                           start=False, stop=True)
                        ins.then_inc(s_mm1.h, 1)
                    em("tensor", f_bias)
                    s_mm1.n += 1
                done_w = (s_mm1, s_mm1.n) if kind == "base" else (s_wfr, s_wfr.n)

                for tt in range(TPT):
                    sl1 = suse % NSTG

                    def f_ev1(e, par=par, tt=tt, kind=kind, sl1=sl1,
                              w=W(done_w, (st1[sl1], st1[sl1].n))):
                        w(e)
                        if kind == "qkv":
                            e.tensor_scalar(stg1[sl1], bank(par + tt), CLIP, -CLIP,
                                            OP.min, OP.max).then_inc(s_ev1.h, 1)
                        else:
                            e.tensor_copy(stg1[sl1], bank(par + tt)).then_inc(s_ev1.h, 1)
                    em("vector", f_ev1)
                    s_ev1.n += 1

                    trow = th * c.CHT + tt * 128
                    if kind == "qkv":
                        if ec < c.NDC:
                            DMA("gpsimd", qs_s[trow:trow + 128, ec * 512:(ec + 1) * 512],
                                stg1[sl1], st1[sl1], (s_ev1, s_ev1.n))
                        elif ec < 2 * c.NDC:
                            DMA("gpsimd",
                                ks_s[trow:trow + 128, (ec - c.NDC) * 512:(ec - c.NDC + 1) * 512],
                                stg1[sl1], st1[sl1], (s_ev1, s_ev1.n))
                        else:
                            for hb in range(4):
                                h = (ec - 2 * c.NDC) * 4 + hb
                                vrow = ((h // HPN) * HPN + (h % HPN)) * c.TOK + trow
                                DMA("gpsimd", a2av_i[vrow:vrow + 128, :],
                                    stg1[sl1][:, hb * 128:(hb + 1) * 128], st1[sl1],
                                    (s_ev1, s_ev1.n) if hb == 0 else (s_ev1, 0))
                    else:
                        DMA("gpsimd", base_s[trow:trow + 128, ec * 512:(ec + 1) * 512],
                            stg1[sl1], st1[sl1], (s_ev1, s_ev1.n))
                    suse += 1

            elif kind == "arh":
                def f_arh(e, par=par,
                          w=W((s_splc, spre_done), (s_xsp, ldxn_here), (s_ev1, ev_guard))):
                    w(e)
                    for kd in range(c.KD):
                        ah = atch[:, kd * c.NR:(kd + 1) * c.NR]
                        al = atcl[:, kd * c.NR:(kd + 1) * c.NR]
                        xh = xnfh[:, kd * c.CHT:(kd + 1) * c.CHT]
                        xl = xnfl[:, kd * c.CHT:(kd + 1) * c.CHT]
                        e.matmul(bank(par)[0:c.NR, 0:c.CHT], ah, xh,
                                 start=(kd == 0), stop=False)
                        e.matmul(bank(par)[0:c.NR, 0:c.CHT], al, xh, start=False, stop=False)
                        ins = e.matmul(bank(par)[0:c.NR, 0:c.CHT], ah, xl, start=False,
                                       stop=(kd == c.KD - 1))
                    ins.then_inc(s_mm1.h, 1)
                em("tensor", f_arh)
                s_mm1.n += 1

                def f_evarh(e, th=th, par=par, w=W((s_mm1, s_mm1.n))):
                    w(e)
                    e.tensor_copy(rh2[0:c.NR, th * c.CHT:(th + 1) * c.CHT],
                                  bank(par)[0:c.NR, 0:c.CHT]).then_inc(s_ev1.h, TPT)
                em("vector", f_evarh)
                s_ev1.n += TPT
                suse += TPT

            else:
                def f_rtr(e, par=par,
                          w=W((s_splc, spre_done), (s_xsp, ldxn_here), (s_ev1, ev_guard))):
                    w(e)
                    for tt in range(TPT):
                        for kd in range(c.KD):
                            xh = xnfh[:, kd * c.CHT + tt * 128:kd * c.CHT + tt * 128 + 128]
                            xl = xnfl[:, kd * c.CHT + tt * 128:kd * c.CHT + tt * 128 + 128]
                            rh_ = rtch[:, kd * c.NE:(kd + 1) * c.NE]
                            rl_ = rtcl[:, kd * c.NE:(kd + 1) * c.NE]
                            e.matmul(bank(par + tt)[:, 0:c.NE], xh, rh_,
                                     start=(kd == 0), stop=False)
                            e.matmul(bank(par + tt)[:, 0:c.NE], xl, rh_,
                                     start=False, stop=False)
                            ins = e.matmul(bank(par + tt)[:, 0:c.NE], xh, rl_,
                                           start=False, stop=(kd == c.KD - 1))
                    ins.then_inc(s_mm1.h, 1)
                em("tensor", f_rtr)
                s_mm1.n += 1

                def f_evrtr(e, th=th, par=par, w=W((s_mm1, s_mm1.n))):
                    w(e)
                    for tt in range(TPT):
                        g = th * TPT + tt
                        ins = e.tensor_copy(logits[:, g * c.NE:(g + 1) * c.NE],
                                            bank(par + tt)[:, 0:c.NE])
                    ins.then_inc(s_ev1.h, TPT)
                em("vector", f_evrtr)
                s_ev1.n += TPT
                suse += TPT
        mm1_at_th_end.append(s_mm1.n)
        wfr_at_th_end.append(s_wfr.n)
    quiesce("sync", [s_ldpre] + ldw + ldwl + spl)
    quiesce("gpsimd", ldxn + st1 + sps)
    run_phase()
    amark[0] = am1
    st1_finals = [(s, s.n) for s in st1]

    am2 = amark[0]
    qa = [alloc(c.D), alloc(c.D)]
    ka = [alloc(c.D), alloc(c.D)]
    qn = qa
    kn = ka
    sq2 = alloc(c.D)
    stg2 = [alloc(128) for _ in range(8)]
    suse2 = 0

    def f_agv(e, w=W(*st1_finals)):
        w(e)
        from concourse.aot_env import allow_ici_all_to_all
        with allow_ici_all_to_all():
            e.collective_compute("AllToAll", OP.bypass,
                                 replica_groups=[list(range(c.NC))],
                                 ins=[a2av_i.opt()],
                                 outs=[a2av_o.opt()]).then_inc(s_cc.h, 1)
    em("gpsimd", f_agv)
    s_cc.n += 1

    for i in range(c.NT):
        sl = i % 2
        extra = (st1_finals if i == 0 else []) + [(s_qn, max(0, s_qn.n - 2)),
                                                   (s_tr2, max(0, (i - 1) * 2 * c.H))]
        DMA("gpsimd", qa[sl], qs_s[i * 128:(i + 1) * 128, :], lqk[sl], *extra)
        DMA("gpsimd", ka[sl], ks_s[i * 128:(i + 1) * 128, :], lqk[sl])
        for which, buf, nbuf, gw, gb in (("q", qa, qn, gqs, gqb), ("k", ka, kn, gks, gkb)):
            o = 0 if which == "q" else 16

            def f_sc(e, sl=sl, buf=buf, o=o, w=W((lqk[sl], lqk[sl].n))):
                w(e)
                e.activation(sq2, buf[sl], AF.Square,
                             accum_out=smx(o + 2 + sl)).then_inc(s_sq2.h, 1)
            em("scalar", f_sc)
            s_sq2.n += 1

            def f_v1(e, sl=sl, buf=buf, o=o, w=W((lqk[sl], lqk[sl].n), (s_sq2, s_sq2.n))):
                w(e)
                e.reduce_sum(smx(o + 0 + sl), buf[sl], axis=AX)
                e.drain()
                e.tensor_scalar(smx(o + 4 + sl), smx(o + 0 + sl),
                                1.0 / c.D, None, OP.mult)
                e.tensor_scalar(smx(o + 6 + sl), smx(o + 2 + sl),
                                1.0 / c.D, None, OP.mult)
                e.drain()
                e.tensor_tensor(smx(o + 8 + sl), smx(o + 4 + sl),
                                smx(o + 4 + sl), OP.mult)
                e.drain()
                e.tensor_tensor(smx(o + 6 + sl), smx(o + 6 + sl),
                                smx(o + 8 + sl), OP.subtract).then_inc(s_vst2.h, 1)
            em("vector", f_v1)
            s_vst2.n += 1

            def f_sc2(e, sl=sl, o=o, w=W((s_vst2, s_vst2.n))):
                w(e)
                e.activation(smx(o + 10 + sl), smx(o + 6 + sl),
                             AF.Sqrt, bias=epst[:, 0:1]).then_inc(s_sct2.h, 1)
            em("scalar", f_sc2)
            s_sct2.n += 1

            def f_v2(e, sl=sl, buf=buf, nbuf=nbuf, o=o,
                     w=W((s_sct2, s_sct2.n), (s_tr2, max(0, (i - 1) * 2 * c.H)))):
                w(e)
                e.reciprocal(smx(o + 12 + sl), smx(o + 10 + sl))
                e.drain()
                e.tensor_scalar(nbuf[sl], buf[sl], smx(o + 4 + sl),
                                smx(o + 12 + sl),
                                OP.subtract, OP.mult).then_inc(s_qn.h, 1)
            em("vector", f_v2)
            s_qn.n += 1

            for h in range(c.H):
                bnk = (h % 4) + (0 if sl == 0 else 4)

                def f_tr(e, sl=sl, nbuf=nbuf, h=h, bnk=bnk,
                         w=W((s_qn, s_qn.n), (s_ev2, max(0, s_ev2.n - 3)))):
                    w(e)
                    e.transpose(bank(bnk)[:, 0:128], nbuf[sl][:, h * 128:(h + 1) * 128],
                                identt).then_inc(s_tr2.h, 1)
                em("tensor", f_tr)
                s_tr2.n += 1

                s2 = suse2 % 8

                def f_ev(e, h=h, bnk=bnk, gw=gw, gb=gb, s2=s2,
                         w=W((s_tr2, s_tr2.n), (st2[s2], st2[s2].n))):
                    w(e)
                    e.tensor_scalar(stg2[s2], bank(bnk)[:, 0:128],
                                    gw[:, h:h + 1], gb[:, h:h + 1],
                                    OP.mult, OP.add).then_inc(s_ev2.h, 1)
                em("vector", f_ev)
                s_ev2.n += 1
                roff = ((h // HPN) * QKR + (0 if which == "q" else HPN * 128)
                        + (h % HPN) * 128)
                DMA("gpsimd", a2aqk_i[roff:roff + 128, i * 128:(i + 1) * 128],
                    stg2[s2], st2[s2], (s_ev2, s_ev2.n))
                suse2 += 1

    st2_finals = [(s, s.n) for s in st2]

    def f_ag(e, w=W(*st2_finals)):
        w(e)
        from concourse.aot_env import allow_ici_all_to_all
        with allow_ici_all_to_all():
            e.collective_compute("AllToAll", OP.bypass,
                                 replica_groups=[list(range(c.NC))],
                                 ins=[a2aqk_i.opt()],
                                 outs=[a2aqk_o.opt()]).then_inc(s_cc.h, 1)
    em("gpsimd", f_ag)
    s_cc.n += 1
    cc_qkv = s_cc.n
    quiesce("gpsimd", lqk + st2)
    run_phase()
    amark[0] = am2

    csb = alloc(c.NERC * c.TOK)
    csbh = alloc(c.NERC * c.TOK // 2)[:, :].bitcast(BF16)
    csbl = alloc(c.NERC * c.TOK // 2)[:, :].bitcast(BF16)
    clo = alloc(c.CHT)
    wsb3h = [alloc(256)[:, :].bitcast(BF16) for _ in range(6)]
    wsb3l = [alloc(256)[:, :].bitcast(BF16) for _ in range(6)]
    bstg = [alloc(512) for _ in range(4)]
    t1st = [alloc(512) for _ in range(4)]
    gstg = [alloc(512) for _ in range(4)]
    bst4 = [bst[0], bst[1], lqk[0], lqk[1]]
    st34 = [st3[0], st3[1], st2[0], st2[1]]
    prbf = alloc(64)
    prb2f = alloc(64)
    prb = prbf[:, 0:c.NE]
    prb2 = prb2f[:, 0:c.NE]

    for g in range(c.NT):
        lg = logits[:, g * c.NE:(g + 1) * c.NE]

        def f_v(e, lg=lg, w=W((s_ffs, s_ffs.n), (s_trc, s_trc.n))):
            w(e)
            e.reduce_max(smx(0), lg, axis=AX)
            e.drain()
            e.tensor_scalar(smx(1), smx(0), -1.0, None, OP.mult).then_inc(s_ffv.h, 1)
        em("vector", f_v)
        s_ffv.n += 1

        def f_s(e, lg=lg, w=W((s_ffv, s_ffv.n))):
            w(e)
            e.activation(prb, lg, AF.Exp, bias=smx(1),
                         accum_out=smx(2)).then_inc(s_ffs.h, 1)
        em("scalar", f_s)
        s_ffs.n += 1

        def f_v2(e, w=W((s_ffs, s_ffs.n))):
            w(e)
            e.reciprocal(smx(3), smx(2))
            e.drain()
            e.tensor_scalar(prb, prb, smx(3), None, OP.mult)
            e.drain()
            e.reduce_max(smx(4), prb, axis=AX)
            e.drain()
            e.tensor_scalar(prb2, prb, smx(4), None, OP.is_ge)
            e.drain()
            e.scalar_tensor_tensor(prb2, prb2, -2.0, prb, OP.mult, OP.add)
            e.drain()
            e.reduce_max(smx(5), prb2, axis=AX)
            e.drain()
            e.tensor_tensor(smx(6), smx(4), smx(5), OP.add)
            e.drain()
            e.tensor_scalar(smx(6), smx(6), 1e-6, None, OP.add)
            e.drain()
            e.reciprocal(smx(7), smx(6))
            e.drain()
            e.tensor_tensor(smx(8), smx(4), smx(7), OP.mult)
            e.tensor_tensor(smx(9), smx(5), smx(7), OP.mult)
            e.tensor_scalar(prb2, prb2, smx(5), None, OP.is_ge)
            e.tensor_scalar(prb, prb, smx(4), None, OP.is_ge)
            e.drain()
            e.tensor_scalar(prb, prb, smx(8), None, OP.mult)
            e.drain()
            e.scalar_tensor_tensor(prb, prb2, smx(9), prb, OP.mult,
                                   OP.add).then_inc(s_ffv.h, 1)
        em("vector", f_v2)
        s_ffv.n += 1

        def f_tr(e, g=g, w=W((s_ffv, s_ffv.n), (s_cbv, max(0, s_cbv.n - 1)))):
            w(e)
            e.transpose(bank(g % 2)[0:c.NE, 0:128], prb, identt).then_inc(s_trc.h, 1)
        em("tensor", f_tr)
        s_trc.n += 1

        def f_ev(e, g=g, w=W((s_trc, s_trc.n))):
            w(e)
            e.tensor_copy(combf[0:c.NE, g * 128:(g + 1) * 128],
                          bank(g % 2)[0:c.NE, 0:128]).then_inc(s_cbv.h, 1)
        em("vector", f_ev)
        s_cbv.n += 1

    def f_dup(e, w=W((s_ev1, s_ev1.n))):
        w(e)
        e.dma_start(out=rh2[c.NR:2 * c.NR, :], in_=rh2[0:c.NR, :]).then_inc(s_dup.h, 16)
    em("gpsimd", f_dup)
    s_dup.n += 16

    cbv_base = s_cbv.n
    csp_base = s_csp.n
    cb = 0
    for ecc in range(c.NERC):
        for tch in range(c.NTH):
            def f_cb(e, ecc=ecc, tch=tch, cb=cb,
                     w=W((s_cbv, cbv_base + max(0, cb - 1)), (s_dup, 16))):
                w(e)
                e.matmul(bank(2 + cb % 2)[:, 0:c.CHT],
                         selt[0:c.NE, ecc * 128:(ecc + 1) * 128],
                         combf[0:c.NE, tch * c.CHT:(tch + 1) * c.CHT],
                         start=True, stop=True).then_inc(s_cbm.h, 1)
            em("tensor", f_cb)
            s_cbm.n += 1

            def f_cv(e, ecc=ecc, tch=tch, cb=cb, w=W((s_cbm, s_cbm.n))):
                w(e)
                e.tensor_tensor(
                    csb[:, ecc * c.TOK + tch * c.CHT:ecc * c.TOK + (tch + 1) * c.CHT],
                    rh2[:, tch * c.CHT:(tch + 1) * c.CHT],
                    bank(2 + cb % 2)[:, 0:c.CHT], OP.mult).then_inc(s_cbv.h, 1)
            em("vector", f_cv)
            s_cbv.n += 1

            def f_ch(e, ecc=ecc, tch=tch, w=W((s_cbv, s_cbv.n), (s_csp, max(0, s_csp.n - 1)))):
                w(e)
                e.activation(csbh[:, ecc * c.TOK + tch * c.CHT:ecc * c.TOK + (tch + 1) * c.CHT],
                             csb[:, ecc * c.TOK + tch * c.CHT:ecc * c.TOK + (tch + 1) * c.CHT],
                             AF.Identity).then_inc(s_sphi.h, 1)
            em("scalar", f_ch)
            s_sphi.n += 1

            def f_cl(e, ecc=ecc, tch=tch, w=W((s_sphi, s_sphi.n))):
                w(e)
                e.scalar_tensor_tensor(
                    clo, csbh[:, ecc * c.TOK + tch * c.CHT:ecc * c.TOK + (tch + 1) * c.CHT],
                    -1.0, csb[:, ecc * c.TOK + tch * c.CHT:ecc * c.TOK + (tch + 1) * c.CHT],
                    OP.mult, OP.add).then_inc(s_splo.h, 1)
            em("vector", f_cl)
            s_splo.n += 1

            def f_cc2(e, ecc=ecc, tch=tch, w=W((s_splo, s_splo.n))):
                w(e)
                e.copy(csbl[:, ecc * c.TOK + tch * c.CHT:ecc * c.TOK + (tch + 1) * c.CHT],
                       clo).then_inc(s_csp.h, 1)
            em("scalar", f_cc2)
            s_csp.n += 1
            cb += 1
    csp_done = s_csp.n

    btile = 0
    bfp = [alloc(512), alloc(512)]
    blo_t = alloc(512)
    for oc in range(c.NBC):
        for ecc in range(c.NERC):
            sl_b = btile % 6
            sf = btile % 2
            DMA("sync", bfp[sf], bflat[ecc * 128:(ecc + 1) * 128, oc * 512:(oc + 1) * 512],
                spl[sf], (s_splo, max(0, s_splo.n - 1)))

            def f_bs(e, sf=sf, sl_b=sl_b,
                     w=W((spl[sf], spl[sf].n), (s_bfr, max(0, btile - 5)))):
                w(e)
                e.tensor_copy(wsb3h[sl_b], bfp[sf])
                e.drain()
                e.scalar_tensor_tensor(blo_t, wsb3h[sl_b], -1.0, bfp[sf], OP.mult, OP.add)
                e.drain()
                e.tensor_copy(wsb3l[sl_b], blo_t).then_inc(s_splo.h, 1)
            em("vector", f_bs)
            s_splo.n += 1
            bt_w = (s_splo, s_splo.n)
            btile += 1

            def f_dm(e, ecc=ecc, sl_b=sl_b, oc=oc,
                     w=W(bt_w,
                         (s_csp, csp_base + (ecc + 1) * c.NTH if oc == 0 else 0),
                         (s_dev, oc * c.NT if ecc == 0 else 0))):
                w(e)
                for tt in range(c.NT):
                    chs = csbh[:, ecc * c.TOK + tt * 128:ecc * c.TOK + (tt + 1) * 128]
                    cls = csbl[:, ecc * c.TOK + tt * 128:ecc * c.TOK + (tt + 1) * 128]
                    e.matmul(bank(tt), chs, wsb3h[sl_b], start=(ecc == 0), stop=False)
                    e.matmul(bank(tt), cls, wsb3h[sl_b], start=False, stop=False)
                    ins = e.matmul(bank(tt), chs, wsb3l[sl_b], start=False,
                                   stop=(ecc == c.NERC - 1))
                ins.then_inc(s_bfr.h, 1)
            em("tensor", f_dm)
            s_bfr.n += 1

        for tt in range(c.NT):
            dl = (oc * c.NT + tt) % 4
            extra = st1_finals if (oc == 0 and tt == 0) else []
            DMA("gpsimd", bstg[dl], base_s[tt * 128:(tt + 1) * 128, oc * 512:(oc + 1) * 512],
                bst4[dl], (s_dev, max(0, s_dev.n - 3)), *extra)

            def f_dv(e, tt=tt, dl=dl, w=W((s_bfr, s_bfr.n), (bst4[dl], bst4[dl].n),
                                          (s_sil, max(0, s_sil.n - 3)))):
                w(e)
                e.scalar_tensor_tensor(t1st[dl], bank(tt), 0.5, bstg[dl],
                                       OP.mult, OP.add).then_inc(s_dev.h, 1)
            em("vector", f_dv)
            s_dev.n += 1

            def f_sil(e, dl=dl, w=W((s_dev, s_dev.n), (st34[dl], st34[dl].n))):
                w(e)
                e.activation(gstg[dl], t1st[dl], AF.Silu).then_inc(s_sil.h, 1)
            em("scalar", f_sil)
            s_sil.n += 1
            DMA("gpsimd", g_s[tt * 128:(tt + 1) * 128, oc * 512:(oc + 1) * 512],
                gstg[dl], st34[dl], (s_sil, s_sil.n))

    quiesce("sync", ldb)
    quiesce("gpsimd", bst4 + st34 + [s_dup])
    run_phase()

    amark[0] = PERSIST0
    kf2 = [alloc(c.S), alloc(c.S)]
    vt2 = [alloc(c.S), alloc(c.S)]
    scb = [alloc(c.S), alloc(c.S)]
    pT = alloc(c.KB * 256)
    pT3 = pT.rearrange("p (b w) -> p b w", w=256)
    slopek = alloc(c.S)
    negqs = alloc(c.KB)
    qtl = [alloc(128), alloc(128)]
    astg = [alloc(256), alloc(256)]
    pmaxb = alloc(32)
    NTASK = c.B * HPN
    NTB = c.TOK // 128
    NCH = c.KB // 2
    scg = [0]
    sev_bank = [0, 0]
    tb_ctr = [0]
    TBANKS = (2, 3, 6, 7)
    ptv_bank = [0, 0, 0, 0]
    last_smm = {0: 0, 1: 0}
    ptm_scb = {0: 0, 1: 0}
    e1_after = {0: 0, 1: 0}
    rs_after = {0: 0, 1: 0}
    e2_after = {0: 0, 1: 0}
    e2b_after = {0: 0, 1: 0}
    aev_after = {0: 0, 1: 0}
    kfree = {0: 0, 1: 0}
    vfree = {0: 0, 1: 0}
    pv_chunk = [0]

    def emit_pt(t, m, ql, e2n):
        qt = 2 * m + ql
        qp = qt % 2
        P = qt + 1
        e2a_n, e2b_n, PB = e2n
        NG = (P + 3) // 4
        seen_a = [False]
        seen_b = [False]
        for gk in range(NG):
            nb = min(4, P - gk * 4)
            tb = tb_ctr[0] % 4
            tbk = TBANKS[tb]
            tb_ctr[0] += 1
            wa = wb = 0
            if gk * 4 < PB and not seen_a[0]:
                wa = e2a_n
                seen_a[0] = True
            if gk * 4 + nb > PB and not seen_b[0]:
                wb = e2b_n
                seen_b[0] = True

            def f_pt(e, qp=qp, gk=gk, nb=nb, tbk=tbk, tb=tb, wa=wa, wb=wb,
                     w=W((s_e2, wa), (s_cbm, wb),
                         (s_ptv, ptv_bank[tb]))):
                w(e)
                for q4 in range(nb):
                    kb = gk * 4 + q4
                    ins = e.transpose(bank(tbk)[:, q4 * 128:(q4 + 1) * 128],
                                      scb[qp][:, kb * 128:(kb + 1) * 128], identt)
                ins.then_inc(s_ptm.h, 1)
            em("tensor", f_pt)
            s_ptm.n += 1
            last_g = (ql == 1 and gk == NG - 1)

            def f_ptv(e, ql=ql, gk=gk, nb=nb, tbk=tbk, m=m, last_g=last_g,
                      w=W((s_ptm, s_ptm.n),
                          (s_pv, pv_chunk[0] if (ql == 0 and gk == 0) else 0))):
                w(e)
                ins = e.tensor_copy(
                    pT3[:, gk * 4:gk * 4 + nb, ql * 128:ql * 128 + 128],
                    bank(tbk)[:, 0:nb * 128].rearrange("p (b w) -> p b w",
                                                       w=128))
                if last_g:
                    ins = e.memset(pT[:, (2 * m + 1) * 256:(2 * m + 1) * 256
                                      + 128], 0.0)
                ins.then_inc(s_ptv.h, 1)
            em("vector", f_ptv)
            s_ptv.n += 1
            ptv_bank[tb] = s_ptv.n
        ptm_scb[qp] = s_ptm.n

    def emit_pvtail(t, tp, m, cp, kvw=0):
        NKB = 2 * m + 2

        def f_pv(e, tp=tp, cp=cp, NKB=NKB,
                 w=W((s_ptv, s_ptv.n), (s_aev, aev_after[cp]), (s_kv, kvw))):
            w(e)
            for kb in range(NKB):
                ins = e.matmul(bank(4 + cp)[:, 0:256],
                               vt2[tp][:, kb * 128:(kb + 1) * 128],
                               pT[:, kb * 256:(kb + 1) * 256],
                               start=(kb == 0), stop=(kb == NKB - 1))
            ins.then_inc(s_pv.h, 1)
        em("tensor", f_pv)
        s_pv.n += 1
        pv_chunk[0] = s_pv.n

        def f_aev(e, cp=cp, w=W((s_pv, s_pv.n), (ast[cp], ast[cp].n))):
            w(e)
            e.tensor_copy(astg[cp], bank(4 + cp)[:, 0:256]).then_inc(s_aev.h, 1)
        em("vector", f_aev)
        s_aev.n += 1
        aev_after[cp] = s_aev.n
        rpq = (2 * m * 128) // c.TOK
        qc2 = 2 * m * 128 - rpq * c.TOK
        aor = ((t // HPN) * c.R4 + rpq) * HPN * 128 + (t % HPN) * 128
        DMA("gpsimd", a2aao_i[aor:aor + 128, qc2:qc2 + 256], astg[cp],
            ast[cp], (s_aev, s_aev.n))

    sev_task = 0
    for t in range(NTASK):
        tp = t % 2
        hd = t % HPN
        bsel = t // HPN
        first = True
        for rp in range(c.R4):
            snd = bsel * c.R4 + rp
            gw_ = (((s_smm, kfree[tp]), (s_cc, cc_qkv)) if first else ())
            first = False
            krow = snd * QKR + HPN * 128 + hd * 128
            DMA("sync", kf2[tp][:, rp * c.TOK:(rp + 1) * c.TOK],
                a2aqk_o[krow:krow + 128, :], s_ini, *gw_)
        k4 = s_ini.n
        vw_ = ((s_pv, vfree[tp]),)
        for rp in range(c.R4):
            snd = bsel * c.R4 + rp
            for lb in range(NTB):
                vrow = snd * HPN * c.TOK + hd * c.TOK + lb * 128
                vcol = rp * c.TOK + lb * 128
                DMA("sync", vt2[tp][:, vcol:vcol + 128],
                    a2av_o[vrow:vrow + 128, :], s_kv, *vw_)
                vw_ = ()
        kv_here = s_kv.n

        DMA("sync", slopek, kpos_d, s_dbs, (s_sev, sev_task))

        def f_nqs(e, t=t, w=W((s_dbs, s_dbs.n))):
            w(e)
            e.tensor_scalar(slopek, slopek, slvt[:, t:t + 1], None, OP.mult)
            e.tensor_scalar(negqs, negqc, slvt[:, t:t + 1], None,
                            OP.mult).then_inc(s_dbv.h, 1)
        em("vector", f_nqs)
        s_dbv.n += 1
        dbv_task = s_dbv.n

        def emit_qt(m, ql, kv_first, t=t, tp=tp, hd=hd, bsel=bsel,
                    dbv_task=dbv_task, kv_here=k4):
            qt = 2 * m + ql
            qp = qt % 2
            PW = (qt + 1) * 128
            SCq = (PW + 511) // 512
            rp = (qt * 128) // c.TOK
            qcol = qt * 128 - rp * c.TOK
            qrow = (bsel * c.R4 + rp) * QKR + hd * 128
            DMA("sync", qtl[qp], a2aqk_o[qrow:qrow + 128, qcol:qcol + 128],
                qfl[qp], (s_smm, last_smm[qp]))
            dstart = qt * 128
            for sc in range(SCq):
                c0 = sc * 512
                wdt = min(512, PW - c0)
                bp = scg[0] % 2
                scg[0] += 1

                def f_smm(e, qp=qp, sc=sc, wdt=wdt, bp=bp, tp=tp,
                          kvw=(kv_here if (kv_first and sc == 0) else 0),
                          w=W((qfl[qp], qfl[qp].n if sc == 0 else 0),
                              (s_sev, sev_bank[bp]))):
                    if kvw:
                        e.wait_ge(s_ini.h, kvw)
                    w(e)
                    e.matmul(bank(bp)[:, 0:wdt], qtl[qp],
                             kf2[tp][:, sc * 512:sc * 512 + wdt],
                             start=True, stop=True).then_inc(s_smm.h, 1)
                em("tensor", f_smm)
                s_smm.n += 1
                w1 = wdt if c0 + wdt <= dstart else max(0, dstart - c0)

                def f_sev(e, qt=qt, qp=qp, t=t, c0=c0, wdt=wdt, w1=w1, bp=bp,
                          w=W((s_smm, s_smm.n), (s_dbv, dbv_task),
                              (s_ptm, ptm_scb[qp] if sc == 0 else 0))):
                    w(e)
                    ins = None
                    if w1 > 0:
                        ins = e.scalar_tensor_tensor(
                            scb[qp][:, c0:c0 + w1], slopek[:, c0:c0 + w1],
                            negqs[:, qt:qt + 1], bank(bp)[:, 0:w1],
                            OP.add, OP.add)
                    if w1 < wdt:
                        ins = e.scalar_tensor_tensor(
                            scb[qp][:, c0 + w1:c0 + wdt], dgt,
                            slvt[:, t:t + 1], bank(bp)[:, w1:wdt],
                            OP.mult, OP.add)
                    ins.then_inc(s_sev.h, 1)
                em("vector", f_sev)
                s_sev.n += 1
                sev_bank[bp] = s_sev.n

                def f_pm(e, qp=qp, sc=sc, c0=c0, wdt=wdt,
                         lastsc=(sc == SCq - 1), w=W((s_sev, s_sev.n))):
                    w(e)
                    ins = e.reduce_max(pmaxb[:, qp * 16 + sc:qp * 16 + sc + 1],
                                       scb[qp][:, c0:c0 + wdt], axis=AX)
                    if lastsc:
                        ins.then_inc(s_mx.h, 1)
                em("vector", f_pm)
            s_mx.n += 1
            last_smm[qp] = s_smm.n

            def f_mxf(e, qp=qp, SCq=SCq, w=W((s_mx, s_mx.n),
                                             (s_e1, e1_after[qp]))):
                w(e)
                e.reduce_max(smx(20 + qp), pmaxb[:, qp * 16:qp * 16 + SCq],
                             axis=AX).then_inc(s_mx.h, 1)
            em("vector", f_mxf)
            s_mx.n += 1

            def f_e1(e, qp=qp, PW=PW, w=W((s_mx, s_mx.n),
                                          (s_rs, rs_after[qp]))):
                w(e)
                e.activation(smx(22 + qp), smx(20 + qp), AF.Identity,
                             scale=-1.0)
                e.drain()
                e.activation(scb[qp][:, 0:PW], scb[qp][:, 0:PW], AF.Exp,
                             bias=smx(22 + qp),
                             accum_out=smx(24 + qp)).then_inc(s_e1.h, 1)
            em("scalar", f_e1)
            s_e1.n += 1
            e1_after[qp] = s_e1.n

            def f_rs(e, qp=qp, w=W((s_e1, s_e1.n), (s_e2, e2_after[qp]),
                                   (s_cbm, e2b_after[qp]))):
                w(e)
                e.reciprocal(smx(26 + qp), smx(24 + qp)).then_inc(s_rs.h, 1)
            em("vector", f_rs)
            s_rs.n += 1
            rs_after[qp] = s_rs.n

            P_ = qt + 1
            PB = (P_ + 1) // 2
            PWa = PB * 128

            def f_e2a(e, qp=qp, PWa=PWa, w=W((s_rs, s_rs.n))):
                w(e)
                e.activation(scb[qp][:, 0:PWa], scb[qp][:, 0:PWa], AF.Identity,
                             scale=smx(26 + qp)).then_inc(s_e2.h, 1)
            em("scalar", f_e2a)
            s_e2.n += 1
            e2_after[qp] = s_e2.n
            if PW > PWa:
                def f_e2b(e, qp=qp, PWa=PWa, PW=PW, w=W((s_rs, s_rs.n))):
                    w(e)
                    e.tensor_scalar(scb[qp][:, PWa:PW], scb[qp][:, PWa:PW],
                                    smx(26 + qp), None,
                                    OP.mult).then_inc(s_cbm.h, 1)
                em("vector", f_e2b)
                s_cbm.n += 1
            e2b_after[qp] = s_cbm.n
            return (s_e2.n, s_cbm.n, PB)

        pend = None
        for m in range(NCH):
            e2q0 = emit_qt(m, 0, kv_first=(m == 0))
            if pend is not None:
                emit_pt(t, pend[0], 1, pend[1])
                emit_pvtail(t, tp, pend[0], pend[0] % 2,
                            kvw=(kv_here if pend[0] == 0 else 0))
            e2q1 = emit_qt(m, 1, kv_first=False)
            emit_pt(t, m, 0, e2q0)
            pend = (m, e2q1)
        emit_pt(t, pend[0], 1, pend[1])
        emit_pvtail(t, tp, pend[0], pend[0] % 2)
        kfree[tp] = s_smm.n
        vfree[tp] = s_pv.n
        sev_task = s_sev.n

    def f_ao(e, w=W((ast[0], ast[0].n), (ast[1], ast[1].n))):
        w(e)
        from concourse.aot_env import allow_ici_all_to_all
        with allow_ici_all_to_all():
            e.collective_compute("AllToAll", OP.bypass,
                                 replica_groups=[list(range(c.NC))],
                                 ins=[a2aao_i.opt()],
                                 outs=[a2aao_o.opt()]).then_inc(s_cc.h, 1)
    em("gpsimd", f_ao)
    s_cc.n += 1
    cc_ao = s_cc.n
    quiesce("sync", [s_kv, s_ini, s_dbs] + qfl)
    run_phase()
    amark[0] = PERSIST0

    atnh = alloc(c.KD * c.CHT // 2)[:, :].bitcast(BF16)
    atnl = alloc(c.KD * c.CHT // 2)[:, :].bitcast(BF16)
    atmp = [alloc(c.CHT), alloc(c.CHT)]
    alo = [alloc(c.CHT), alloc(c.CHT)]
    wsb5h = [alloc(256)[:, :].bitcast(BF16) for _ in range(6)]
    wsb5l = [alloc(256)[:, :].bitcast(BF16) for _ in range(6)]
    gld5 = [alloc(512), alloc(512)]
    xld5 = [alloc(512), alloc(512)]
    t1s = [alloc(512), alloc(512)]
    ost = [alloc(512), alloc(512)]
    ast_finals = [(s, s.n) for s in ast]
    st3_finals = [(s, s.n) for s in st3]
    TPT = c.TPT
    wfr5_after = []

    wfp5 = [alloc(512), alloc(512)]
    wlo5 = alloc(512)
    w5_first = [True]
    for th in range(c.NTH):
        asp_base = s_asp.n
        for fc in range(c.KD):
            sx = fc % 2
            extra = ([(s_wfr, wfr5_after[th * c.NBC - 1] if th > 0 else 0),
                      (s_cc, cc_ao)]
                     if fc == 0 else [(s_asp, max(0, s_asp.n - 1))])
            DMA("gpsimd", atmp[sx],
                a2aao_o[fc * 128:(fc + 1) * 128, th * c.CHT:(th + 1) * c.CHT],
                lat[sx], *extra)

            def f_ah(e, sx=sx, fc=fc, w=W((lat[sx], lat[sx].n), (s_asp, max(0, s_asp.n - 1)))):
                w(e)
                e.activation(atnh[:, fc * c.CHT:(fc + 1) * c.CHT], atmp[sx],
                             AF.Identity).then_inc(s_sphi.h, 1)
            em("scalar", f_ah)
            s_sphi.n += 1

            def f_al(e, sx=sx, fc=fc, w=W((s_sphi, s_sphi.n))):
                w(e)
                e.scalar_tensor_tensor(alo[sx], atnh[:, fc * c.CHT:(fc + 1) * c.CHT],
                                       -1.0, atmp[sx], OP.mult, OP.add).then_inc(s_splo.h, 1)
            em("vector", f_al)
            s_splo.n += 1

            def f_ac(e, sx=sx, fc=fc, w=W((s_splo, s_splo.n))):
                w(e)
                e.copy(atnl[:, fc * c.CHT:(fc + 1) * c.CHT], alo[sx]).then_inc(s_asp.h, 1)
            em("scalar", f_ac)
            s_asp.n += 1
        lat_here = s_asp.n
        for oc in range(c.NBC):
            cgi5 = th * c.NBC + oc
            par = (cgi5 % 2) * 4
            for fc in range(c.KD):
                sl_w = (cgi5 * c.KD + fc) % 6
                if th == 0:
                    sf = fc % 2
                    DMA("sync", wfp5[sf],
                        woutT[fc * 128:(fc + 1) * 128, oc * 512:(oc + 1) * 512],
                        spl[sf], (s_splo, max(0, s_splo.n - 1)))

                    def f_w5(e, sf=sf, sl_w=sl_w,
                             w=W((spl[sf], spl[sf].n), (s_wfr, max(0, s_wfr.n - 5)),
                                 (sps[sl_w % 4], sps[sl_w % 4].n))):
                        w(e)
                        e.tensor_copy(wsb5h[sl_w], wfp5[sf])
                        e.drain()
                        e.scalar_tensor_tensor(wlo5, wsb5h[sl_w], -1.0, wfp5[sf],
                                               OP.mult, OP.add)
                        e.drain()
                        e.tensor_copy(wsb5l[sl_w], wlo5).then_inc(s_splo.h, 1)
                    em("vector", f_w5)
                    s_splo.n += 1
                    DMA("gpsimd", woutT_h[fc * 128:(fc + 1) * 128, oc * 512:(oc + 1) * 512],
                        wsb5h[sl_w], sps[sl_w % 4], (s_splo, s_splo.n))
                    DMA("gpsimd", woutT_l[fc * 128:(fc + 1) * 128, oc * 512:(oc + 1) * 512],
                        wsb5l[sl_w], sps[sl_w % 4], (s_splo, 0))
                    w5_w = ((s_splo, s_splo.n),)
                else:
                    extra5 = [(x, x.n) for x in sps] if w5_first[0] else []
                    w5_first[0] = False
                    DMA("sync", wsb5h[sl_w],
                        woutT_h[fc * 128:(fc + 1) * 128, oc * 512:(oc + 1) * 512], ldw[sl_w],
                        (s_wfr, max(0, s_wfr.n - 5)), *extra5)
                    DMA("sync", wsb5l[sl_w],
                        woutT_l[fc * 128:(fc + 1) * 128, oc * 512:(oc + 1) * 512],
                        ldwl[sl_w])
                    w5_w = ((ldw[sl_w], ldw[sl_w].n), (ldwl[sl_w], ldwl[sl_w].n))

                def f_wm(e, fc=fc, par=par, sl_w=sl_w,
                         w=W(*w5_w,
                             (s_asp, asp_base + fc + 1 if oc == 0 else 0),
                             (s_wv1, (cgi5 - 2 + 1) * TPT if (cgi5 >= 2 and fc == 0) else 0))):
                    w(e)
                    for tt in range(TPT):
                        ah_ = atnh[:, fc * c.CHT + tt * 128:fc * c.CHT + tt * 128 + 128]
                        al_ = atnl[:, fc * c.CHT + tt * 128:fc * c.CHT + tt * 128 + 128]
                        e.matmul(bank(par + tt), ah_, wsb5h[sl_w], start=(fc == 0), stop=False)
                        e.matmul(bank(par + tt), al_, wsb5h[sl_w], start=False, stop=False)
                        ins = e.matmul(bank(par + tt), ah_, wsb5l[sl_w], start=False,
                                       stop=(fc == c.KD - 1))
                    ins.then_inc(s_wfr.h, 1)
                em("tensor", f_wm)
                s_wfr.n += 1
            wfr5_after.append(s_wfr.n)

            for tt in range(TPT):
                dl = (cgi5 * TPT + tt) % 2
                trow = th * c.CHT + tt * 128
                extra = st3_finals if (cgi5 == 0 and tt == 0) else []
                DMA("gpsimd", gld5[dl], g_s[trow:trow + 128, oc * 512:(oc + 1) * 512],
                    gx[dl], (s_wv1, max(0, s_wv1.n - 1)), *extra)
                DMA("gpsimd", xld5[dl], xsh[trow:trow + 128, oc * 512:(oc + 1) * 512], gx[dl])

                def f_wv(e, par=par, tt=tt, dl=dl,
                         w=W((s_wfr, s_wfr.n), (gx[dl], gx[dl].n),
                             (yst[dl], yst[dl].n))):
                    w(e)
                    e.tensor_tensor(t1s[dl], bank(par + tt), gld5[dl], OP.mult)
                    e.drain()
                    e.tensor_tensor(ost[dl], t1s[dl], xld5[dl], OP.add).then_inc(s_wv1.h, 1)
                em("vector", f_wv)
                s_wv1.n += 1
                DMA("gpsimd", y[trow:trow + 128, oc * 512:(oc + 1) * 512], ost[dl],
                    yst[dl], (s_wv1, s_wv1.n))

    def f_fin(e, w=W((yst[0], yst[0].n), (yst[1], yst[1].n))):
        w(e)
    em("gpsimd", f_fin)
    quiesce("sync", ldw + ldwl)
    quiesce("gpsimd", gx + lat)
    run_phase()
    return nc


def shard_inputs(c, L0_attn_hidden_states, L0_attn_Wqkv, L0_attn_q_ln_w, L0_attn_q_ln_b,
                 L0_attn_k_ln_w, L0_attn_k_ln_b, L0_attn_Wout, L0_ffn_base_w,
                 L0_ffn_base_b, L0_ffn_A, L0_ffn_B, L0_ffn_router_w):
    x = np.asarray(L0_attn_hidden_states, dtype=np.float32)
    com = {
        "wqkvT": np.ascontiguousarray(np.asarray(L0_attn_Wqkv, np.float32).T),
        "woutT": np.ascontiguousarray(np.asarray(L0_attn_Wout, np.float32).T),
        "basewT": np.ascontiguousarray(np.asarray(L0_ffn_base_w, np.float32).T),
        "baseb": np.asarray(L0_ffn_base_b, np.float32).reshape(1, c.D),
        "aT": np.ascontiguousarray(np.asarray(L0_ffn_A, np.float32).T),
        "routerT": np.ascontiguousarray(np.asarray(L0_ffn_router_w, np.float32).T),
        "bflat": np.ascontiguousarray(
            np.asarray(L0_ffn_B, np.float32).transpose(0, 2, 1).reshape(c.ER, c.D)),
        "qlnw": np.asarray(L0_attn_q_ln_w, np.float32).reshape(1, c.D),
        "qlnb": np.asarray(L0_attn_q_ln_b, np.float32).reshape(1, c.D),
        "klnw": np.asarray(L0_attn_k_ln_w, np.float32).reshape(1, c.D),
        "klnb": np.asarray(L0_attn_k_ln_b, np.float32).reshape(1, c.D),
    }
    in_maps = []
    for core in range(c.NC):
        b, r = core // c.R4, core % c.R4
        sh = dict(com)
        sh["xsh"] = np.ascontiguousarray(x[b, r * c.TOK:(r + 1) * c.TOK, :])
        sh["qpos"] = np.arange(r * c.TOK, (r + 1) * c.TOK, dtype=np.float32).reshape(c.TOK, 1)
        hpn = c.H // c.NC
        sl = alibi_slopes(c.H)[core * hpn:(core + 1) * hpn]
        sh["slv"] = np.ascontiguousarray(np.broadcast_to(
            np.tile(sl, c.B).reshape(1, c.B * hpn), (128, c.B * hpn)))
        in_maps.append(sh)
    return in_maps


def gather_out(c, results):
    out = np.empty((c.B, c.S, c.D), np.float32)
    for core in range(c.NC):
        b, r = core // c.R4, core % c.R4
        out[b, r * c.TOK:(r + 1) * c.TOK, :] = results[core]["y"]
    return out


def kernel(**inputs):
    c = Cfg()
    in_maps = shard_inputs(c, **inputs)
    nc = build_program(c)
    res = run_bass_kernel_spmd(nc, in_maps, core_ids=list(range(c.NC)))
    return gather_out(c, res.results)


# version 15
# speedup vs baseline: 1.0249x; 1.0035x over previous
import math
import numpy as np
import concourse.bass as bass
import concourse.mybir as mybir
from concourse.bass_utils import run_bass_kernel_spmd

F32 = mybir.dt.float32
BF16 = mybir.dt.bfloat16
AX = mybir.AxisListType.X
OP = mybir.AluOpType
AF = mybir.ActivationFunctionType

LN_EPS = 1e-5
CLIP = 8.0
MASKMUL = -1e27
ARENA_COLS = 45056


def alibi_slopes(n_heads, bias_max=8.0):
    next_pow2 = 2 ** int(np.ceil(np.log2(n_heads)))
    m = np.arange(1, next_pow2 + 1, dtype=np.float32) * (bias_max / next_pow2)
    slopes = 1.0 / np.power(2.0, m)
    if next_pow2 != n_heads:
        slopes = np.concatenate([slopes[1::2], slopes[::2]])[:n_heads]
    return slopes.astype(np.float32)


class Cfg:
    def __init__(self, D=4096, S=4096, B=2, H=32, NE=16, NR=64):
        self.D, self.S, self.B, self.H = D, S, B, H
        self.DH = 128
        assert H * self.DH == D
        self.NC = 8
        self.R4 = self.NC // B
        self.TOK = B * S // self.NC
        self.NT = self.TOK // 128
        self.KD = D // 128
        self.CH = 512
        self.CHT = min(512, self.TOK)
        self.NTH = self.TOK // self.CHT
        self.TPT = self.CHT // 128
        self.NQC = 3 * D // self.CH
        self.NDC = D // self.CH
        self.NBC = D // self.CH
        self.NE, self.NR = NE, NR
        self.ER = NE * NR
        self.NERC = self.ER // 128
        self.SC = S // self.CH
        self.KB = S // 128
        self.HQ = min(2, H // self.R4)
        self.NHG = H // self.HQ


class Sem:
    def __init__(self, nc, name):
        self.h = nc.alloc_semaphore(name=name)
        self.n = 0


def build_program(c):
    nc = bass.Bass(trn_type="TRN2", target_bir_lowering=False)

    xsh = nc.dram_tensor("xsh", [c.TOK, c.D], F32, kind="ExternalInput").ap()
    qpos = nc.dram_tensor("qpos", [c.TOK, 1], F32, kind="ExternalInput").ap()
    slv = nc.dram_tensor("slv", [128, c.H // c.R4], F32, kind="ExternalInput").ap()
    wqkvT = nc.dram_tensor("wqkvT", [c.D, 3 * c.D], F32, kind="ExternalInput").ap()
    woutT = nc.dram_tensor("woutT", [c.D, c.D], F32, kind="ExternalInput").ap()
    basewT = nc.dram_tensor("basewT", [c.D, c.D], F32, kind="ExternalInput").ap()
    baseb = nc.dram_tensor("baseb", [1, c.D], F32, kind="ExternalInput").ap()
    aT = nc.dram_tensor("aT", [c.D, c.NR], F32, kind="ExternalInput").ap()
    routerT = nc.dram_tensor("routerT", [c.D, c.NE], F32, kind="ExternalInput").ap()
    bflat = nc.dram_tensor("bflat", [c.ER, c.D], F32, kind="ExternalInput").ap()
    qlnw = nc.dram_tensor("qlnw", [1, c.D], F32, kind="ExternalInput").ap()
    qlnb = nc.dram_tensor("qlnb", [1, c.D], F32, kind="ExternalInput").ap()
    klnw = nc.dram_tensor("klnw", [1, c.D], F32, kind="ExternalInput").ap()
    klnb = nc.dram_tensor("klnb", [1, c.D], F32, kind="ExternalInput").ap()
    y = nc.dram_tensor("y", [c.TOK, c.D], F32, kind="ExternalOutput").ap()

    xnf_s = nc.dram_tensor("xnf_s", [c.D, c.TOK], F32).ap()
    qs_s = nc.dram_tensor("qs_s", [c.TOK, c.D], F32).ap()
    ks_s = nc.dram_tensor("ks_s", [c.TOK, c.D], F32).ap()
    HPN = c.H // c.NC
    QKR = 2 * HPN * 128
    a2aqk_i = nc.dram_tensor("a2aqk_i", [c.NC * QKR, c.TOK], F32).ap()
    a2aqk_o = nc.dram_tensor("a2aqk_o", [c.NC * QKR, c.TOK], F32).ap()
    a2av_i = nc.dram_tensor("a2av_i", [c.NC * HPN * c.TOK, c.DH], F32).ap()
    a2av_o = nc.dram_tensor("a2av_o", [c.NC * HPN * c.TOK, c.DH], F32).ap()
    a2aao_i = nc.dram_tensor("a2aao_i", [c.NC * HPN * 128, c.TOK], F32).ap()
    a2aao_o = nc.dram_tensor("a2aao_o", [c.NC * HPN * 128, c.TOK], F32).ap()
    base_s = nc.dram_tensor("base_s", [c.TOK, c.D], F32).ap()
    g_s = nc.dram_tensor("g_s", [c.TOK, c.D], F32).ap()
    wqkvT_h = nc.dram_tensor("wqkvT_h", [c.D, 3 * c.D], BF16).ap()
    wqkvT_l = nc.dram_tensor("wqkvT_l", [c.D, 3 * c.D], BF16).ap()
    basewT_h = nc.dram_tensor("basewT_h", [c.D, c.D], BF16).ap()
    basewT_l = nc.dram_tensor("basewT_l", [c.D, c.D], BF16).ap()
    woutT_h = nc.dram_tensor("woutT_h", [c.D, c.D], BF16).ap()
    woutT_l = nc.dram_tensor("woutT_l", [c.D, c.D], BF16).ap()
    bflat_h = nc.dram_tensor("bflat_h", [c.ER, c.D], BF16).ap()
    bflat_l = nc.dram_tensor("bflat_l", [c.ER, c.D], BF16).ap()

    ident_d = nc.inline_tensor(np.eye(128, dtype=np.float32), name="identc").ap()
    kpos_d = nc.inline_tensor(
        np.broadcast_to(np.arange(c.S, dtype=np.float32), (128, c.S)).copy(),
        name="kposc").ap()
    sel_np = np.zeros((c.NE, c.NERC * 128), np.float32)
    for _ecc in range(c.NERC):
        for _m in range(128):
            sel_np[(_ecc * 128 + _m) // c.NR, _ecc * 128 + _m] = 1.0
    sel2_d = nc.inline_tensor(sel_np, name="sel2c").ap()
    ones1_d = nc.inline_tensor(np.ones((1, 128), np.float32), name="ones1c").ap()
    negq_np = -(np.arange(c.KB, dtype=np.float32)[None, :] * 128
                + np.arange(128, dtype=np.float32)[:, None])
    negq_d = nc.inline_tensor(np.ascontiguousarray(negq_np), name="negqc").ap()
    dg_np = (np.arange(128, dtype=np.float32)[None, :]
             - np.arange(128, dtype=np.float32)[:, None])
    dg_np = np.where(dg_np <= 0, dg_np, np.float32(-1e30))
    dg_d = nc.inline_tensor(np.ascontiguousarray(dg_np), name="dgmask").ap()
    slopes = alibi_slopes(c.H)

    psall = nc.alloc_psum_tensor("psall", [128, 8 * 512], F32).ap()

    def bank(i):
        return psall[:, i * 512:(i + 1) * 512]

    arena = nc.alloc_sbuf_tensor("arena", [128, ARENA_COLS], F32).ap()
    amark = [0]

    def alloc(cols):
        ap = arena[:, amark[0]:amark[0] + cols]
        amark[0] += cols
        assert amark[0] <= ARENA_COLS, f"arena overflow {amark[0]}"
        return ap

    identt = alloc(128)
    onest = alloc(128)
    dgt = alloc(128)
    selt = alloc(c.NERC * 128)
    basebt = alloc(c.D)
    gqs, gqb, gks, gkb = alloc(c.H), alloc(c.H), alloc(c.H), alloc(c.H)
    qpost = alloc(c.NT)
    negqp = alloc(c.NT)
    negqc = alloc(c.KB)
    slvt = alloc(c.H // c.R4)
    sm = alloc(16 * 36)
    epst = alloc(16)


    def smx(i):
        return sm[:, i * 16:i * 16 + 1]
    PERSIST0 = amark[0]
    logits = alloc(c.NT * c.NE)
    rh2 = alloc(c.TOK)
    combf = alloc(c.TOK)
    atc = alloc(c.KD * c.NR)
    rtc = alloc(c.KD * c.NE)

    def S_(nm):
        return Sem(nc, nm)

    def slots(nm, n):
        return [S_(f"{nm}{i}") for i in range(n)]

    s_sq, s_vst, s_sct, s_xn = S_("sq"), S_("vst"), S_("sct"), S_("xn")
    s_tr0, s_ev0, s_wfr, s_mm1, s_ev1 = (S_(n) for n in ("tr0", "ev0", "wfr", "mm1", "ev1"))
    s_sq2, s_vst2, s_sct2, s_qn, s_tr2, s_ev2 = s_sq, s_vst, s_sct, s_xn, s_tr0, s_ev0
    s_cc, s_ffv, s_ffs, s_trc, s_cbm, s_cbv = (S_(n) for n in
        ("cc", "ffv", "ffs", "trc", "cbm", "cbv"))
    s_bfr, s_dev, s_sil = (S_(n) for n in ("bfr", "dev", "sil"))
    s_dbs, s_dbv, s_smm, s_sev, s_mx, s_e1, s_rs, s_e2 = (S_(n) for n in
        ("dbs", "dbv", "smm", "sev", "mx", "e1", "rs", "e2"))
    s_ptm, s_ptv, s_pv, s_aev, s_wv1 = (S_(n) for n in
        ("ptm", "ptv", "pv", "aev", "wv1"))
    s_ini = S_("ini")
    ldx = slots("ldx", 2)
    st0 = slots("st0", 2)
    ldxn = slots("ldxn", 2)
    s_ldpre = S_("ldpre")
    ldw = slots("ldw", 6)
    st1 = slots("st1", 6)
    lqk = slots("lqk", 2)
    st2 = slots("st2", 8)
    ldb = ldw
    bst = slots("bst", 2)
    st3 = slots("st3", 2)
    s_dup = S_("dup")
    s_loc = S_("loc")
    s_kv = S_("kv")
    qfl = slots("qfl", 2)
    ast = slots("ast", 2)
    lat = slots("lat", 2)
    gx = slots("gx", 2)
    yst = slots("yst", 2)
    spl = slots("spl", 4)
    sps = slots("sps", 4)
    s_sphi = S_("sphi")
    s_splo = S_("splo")
    s_splc = S_("splc")
    ldwl = slots("ldwl", 6)
    s_xsp = S_("xsp")
    s_csp = s_xsp
    s_asp = s_xsp

    ENGS = ("sync", "gpsimd", "tensor", "vector", "scalar")
    ops = {e: [] for e in ENGS}

    def em(eng, fn):
        ops[eng].append(fn)

    def run_phase():
        nonlocal ops
        with nc.Block() as b:
            for e in ENGS:
                if ops[e]:
                    getattr(b, e)(lambda eng, fns=tuple(ops[e]): [f(eng) for f in fns])
        nc.all_engine_barrier()
        ops = {e: [] for e in ENGS}

    def W(*pairs):
        frozen = tuple((s.h, v) for (s, v) in pairs if v > 0)

        def apply(e):
            for h, v in frozen:
                e.wait_ge(h, v)
        return apply

    def DMA(eng, out, in_, sem, *waits, slow=False):
        w = W(*waits)

        def f(e, out=out, in_=in_):
            w(e)
            e.dma_start(out=out, in_=in_,
                        allow_slow_non_contiguous=slow).then_inc(sem.h, 16)
        em(eng, f)
        sem.n += 16
        return sem.n

    def quiesce(eng, sems):
        pairs = tuple((sx.h, sx.n) for sx in sems if sx.n > 0)

        def f(e):
            for h, v in pairs:
                e.wait_ge(h, v)
        em(eng, f)

    DMA("sync", identt, ident_d, s_ini)
    DMA("sync", dgt, dg_d, s_ini)
    DMA("sync", selt[0:c.NE, :], sel2_d, s_ini)
    DMA("sync", onest[0:1, :], ones1_d, s_ini)
    DMA("sync", basebt[0:1, :], baseb, s_ini)
    for dst, src in ((gqs, qlnw), (gqb, qlnb), (gks, klnw), (gkb, klnb)):
        DMA("sync", dst, src[0, :].rearrange("(h d) -> d h", d=128), s_ini, slow=True)
    DMA("sync", qpost, qpos[:, 0].rearrange("(t p) -> p t", p=128), s_ini, slow=True)
    DMA("sync", negqc, negq_d, s_ini)
    DMA("sync", slvt, slv, s_ini)

    def f_initv(e, w=W((s_ini, s_ini.n))):
        w(e)
        isq = 1.0 / math.sqrt(c.DH)
        e.tensor_scalar(gqs, gqs, isq, None, OP.mult)
        e.drain()
        e.tensor_scalar(gqb, gqb, isq, None, OP.mult)
        e.drain()
        e.tensor_scalar(negqp, qpost, -1.0, None, OP.mult)
        e.drain()
        e.memset(epst[:, 0:1], LN_EPS)
    em("vector", f_initv)
    quiesce("sync", [s_ini])
    run_phase()

    am0 = amark[0]
    xa = [alloc(c.D), alloc(c.D)]
    xnb = [alloc(c.D), alloc(c.D)]
    sqs = alloc(c.D)
    stg0 = [alloc(512), alloc(512)]
    NG0 = c.KD // 4

    for i in range(c.NT):
        sl = i % 2
        DMA("sync", xa[sl], xsh[i * 128:(i + 1) * 128, :], ldx[sl],
            (s_xn, max(0, s_xn.n - 1)), (s_sq, max(0, s_sq.n - 1)))

        def f_sc(e, sl=sl, w=W((ldx[sl], ldx[sl].n))):
            w(e)
            e.activation(sqs, xa[sl], AF.Square,
                         accum_out=smx(2 + sl)).then_inc(s_sq.h, 1)
        em("scalar", f_sc)
        s_sq.n += 1

        def f_v1(e, sl=sl, w=W((ldx[sl], ldx[sl].n), (s_sq, s_sq.n))):
            w(e)
            e.reduce_sum(smx(0 + sl), xa[sl], axis=AX)
            e.drain()
            e.tensor_scalar(smx(4 + sl), smx(0 + sl), 1.0 / c.D, None, OP.mult)
            e.tensor_scalar(smx(6 + sl), smx(2 + sl), 1.0 / c.D, None, OP.mult)
            e.drain()
            e.tensor_tensor(smx(8 + sl), smx(4 + sl), smx(4 + sl), OP.mult)
            e.drain()
            e.tensor_tensor(smx(6 + sl), smx(6 + sl), smx(8 + sl),
                            OP.subtract).then_inc(s_vst.h, 1)
        em("vector", f_v1)
        s_vst.n += 1

        def f_sc2(e, sl=sl, w=W((s_vst, s_vst.n))):
            w(e)
            e.activation(smx(10 + sl), smx(6 + sl), AF.Sqrt,
                         bias=epst[:, 0:1]).then_inc(s_sct.h, 1)
        em("scalar", f_sc2)
        s_sct.n += 1

        def f_v2(e, sl=sl, w=W((s_sct, s_sct.n), (s_tr0, max(0, (i - 1) * NG0)))):
            w(e)
            e.reciprocal(smx(12 + sl), smx(10 + sl))
            e.drain()
            e.tensor_scalar(xnb[sl], xa[sl], smx(4 + sl), smx(12 + sl),
                            OP.subtract, OP.mult).then_inc(s_xn.h, 1)
        em("vector", f_v2)
        s_xn.n += 1

        for g in range(NG0):
            def f_tr(e, sl=sl, g=g, w=W((s_xn, s_xn.n), (s_ev0, max(0, s_ev0.n - 1)))):
                w(e)
                for q in range(4):
                    kd = g * 4 + q
                    ins = e.transpose(bank(g % 2)[:, q * 128:(q + 1) * 128],
                                      xnb[sl][:, kd * 128:(kd + 1) * 128], identt)
                ins.then_inc(s_tr0.h, 1)
            em("tensor", f_tr)
            s_tr0.n += 1

            def f_ev(e, g=g, w=W((s_tr0, s_tr0.n), (st0[g % 2], st0[g % 2].n))):
                w(e)
                e.tensor_copy(stg0[g % 2], bank(g % 2)).then_inc(s_ev0.h, 1)
            em("vector", f_ev)
            s_ev0.n += 1

            for q in range(4):
                kd = g * 4 + q
                DMA("gpsimd", xnf_s[kd * 128:(kd + 1) * 128, i * 128:(i + 1) * 128],
                    stg0[g % 2][:, q * 128:(q + 1) * 128], st0[g % 2],
                    (s_ev0, s_ev0.n) if q == 0 else (s_ev0, 0))
    SPW = 512
    wtmp = [None] * 4
    lo_f = [None] * 4
    hi_b = [None] * 4
    lo_b = [None] * 4

    def alloc_split_bufs(n):
        for i in range(n):
            wtmp[i] = alloc(SPW)
            lo_f[i] = alloc(SPW)
            hi_b[i] = alloc(SPW // 2)[:, :].bitcast(BF16)
            lo_b[i] = alloc(SPW // 2)[:, :].bitcast(BF16)

    def emit_split(jobs, nsl):
        spc = [0]
        for srcT, hiT, loT, RR, CC in jobs:
            for r0 in range(RR // 128):
                for c0 in range(CC // SPW):
                    sl = spc[0] % nsl
                    spc[0] += 1
                    DMA("sync", wtmp[sl], srcT[r0 * 128:(r0 + 1) * 128,
                                               c0 * SPW:(c0 + 1) * SPW],
                        spl[sl], (s_splo, max(0, s_splo.n - (nsl - 1))))

                    def f_hi(e, sl=sl, w=W((spl[sl], spl[sl].n),
                                           (s_splo, max(0, s_splo.n - (nsl - 1))),
                                           (sps[sl], sps[sl].n))):
                        w(e)
                        e.tensor_copy(hi_b[sl], wtmp[sl])
                        e.drain()
                        e.scalar_tensor_tensor(lo_f[sl], hi_b[sl], -1.0, wtmp[sl],
                                               OP.mult, OP.add)
                        e.drain()
                        e.tensor_copy(lo_b[sl], lo_f[sl]).then_inc(s_splo.h, 1)
                    em("vector", f_hi)
                    s_splo.n += 1

                    DMA("gpsimd", hiT[r0 * 128:(r0 + 1) * 128, c0 * SPW:(c0 + 1) * SPW],
                        hi_b[sl], sps[sl], (s_splo, s_splo.n))
                    DMA("gpsimd", loT[r0 * 128:(r0 + 1) * 128, c0 * SPW:(c0 + 1) * SPW],
                        lo_b[sl], sps[sl], (s_splo, 0))

    sps_finals = [(x, x.n) for x in sps]
    quiesce("sync", ldx + spl)
    quiesce("gpsimd", st0 + sps)
    run_phase()
    amark[0] = am0

    am1 = amark[0]
    xnfh = alloc(c.KD * c.CHT // 2)[:, :].bitcast(BF16)
    xnfl = alloc(c.KD * c.CHT // 2)[:, :].bitcast(BF16)
    xtmp = [alloc(c.CHT), alloc(c.CHT)]
    xlo = [alloc(c.CHT), alloc(c.CHT)]
    atch = alloc(c.KD * c.NR // 2)[:, :].bitcast(BF16)
    atcl = alloc(c.KD * c.NR // 2)[:, :].bitcast(BF16)
    rtch = alloc(c.KD * c.NE // 2)[:, :].bitcast(BF16)
    rtcl = alloc(c.KD * c.NE // 2)[:, :].bitcast(BF16)
    wsbh = [alloc(256)[:, :].bitcast(BF16) for _ in range(6)]
    wsbl = [alloc(256)[:, :].bitcast(BF16) for _ in range(6)]
    stg1 = [alloc(512) for _ in range(6)]
    NSTG = 6
    TPT = c.TPT
    wfp = [alloc(512), alloc(512)]
    wlo_t = alloc(512)

    chunks = ([("qkv", i) for i in range(c.NQC)] + [("base", i) for i in range(c.NBC)]
              + [("arh", 0), ("rtr", 0)])
    NCHK = len(chunks)
    wtile = 0
    mm1_at_th_end = []
    wfr_at_th_end = []
    st0_finals = [(st0[0], st0[0].n), (st0[1], st0[1].n)]

    for kd in range(c.KD):
        DMA("sync", atc[:, kd * c.NR:(kd + 1) * c.NR], aT[kd * 128:(kd + 1) * 128, :], s_ldpre)
        DMA("sync", rtc[:, kd * c.NE:(kd + 1) * c.NE], routerT[kd * 128:(kd + 1) * 128, :],
            s_ldpre)
    ldpre_fin = s_ldpre.n

    def _chunks(src, hi, lo, total):
        cw = min(512, c.CHT, total)
        return [(src[:, i * cw:(i + 1) * cw], hi[:, i * cw:(i + 1) * cw],
                 lo[:, i * cw:(i + 1) * cw]) for i in range(total // cw)]
    spre_pairs = (_chunks(atc, atch, atcl, c.KD * c.NR)
                  + _chunks(rtc, rtch, rtcl, c.KD * c.NE))
    for fsrc, fhi, flo in spre_pairs:
        def f_spre(e, fsrc=fsrc, fhi=fhi, w=W((s_ldpre, ldpre_fin),
                                              (s_splo, max(0, s_splo.n)))):
            w(e)
            e.activation(fhi, fsrc, AF.Identity).then_inc(s_sphi.h, 1)
        em("scalar", f_spre)
        s_sphi.n += 1

        cw_ = fsrc.shape[1]

        def f_spre2(e, fsrc=fsrc, fhi=fhi, cw_=cw_, w=W((s_sphi, s_sphi.n),
                                                        (s_splc, max(0, s_splc.n)))):
            w(e)
            e.scalar_tensor_tensor(xlo[0][:, 0:cw_], fhi, -1.0, fsrc,
                                   OP.mult, OP.add).then_inc(s_splo.h, 1)
        em("vector", f_spre2)
        s_splo.n += 1

        def f_spre3(e, flo=flo, cw_=cw_, w=W((s_splo, s_splo.n))):
            w(e)
            e.copy(flo, xlo[0][:, 0:cw_]).then_inc(s_splc.h, 1)
        em("scalar", f_spre3)
        s_splc.n += 1
    spre_done = s_splc.n

    suse = 0
    xspc = 0
    wtile_th1_first = [True]
    for th in range(c.NTH):
        for kd in range(c.KD):
            sx = kd % 2
            extra = (st0_finals + [(s_mm1, mm1_at_th_end[th - 1] if th > 0 else 0),
                                   (s_wfr, wfr_at_th_end[th - 1] if th > 0 else 0),
                                   (s_splc, spre_done)]
                     if kd == 0 else [(s_xsp, max(0, s_xsp.n - 1))])
            DMA("gpsimd", xtmp[sx], xnf_s[kd * 128:(kd + 1) * 128, th * c.CHT:(th + 1) * c.CHT],
                ldxn[sx], *extra)

            def f_xh(e, sx=sx, kd=kd, w=W((ldxn[sx], ldxn[sx].n), (s_xsp, max(0, s_xsp.n - 1)))):
                w(e)
                e.activation(xnfh[:, kd * c.CHT:(kd + 1) * c.CHT], xtmp[sx],
                             AF.Identity).then_inc(s_sphi.h, 1)
            em("scalar", f_xh)
            s_sphi.n += 1

            def f_xl(e, sx=sx, kd=kd, w=W((s_sphi, s_sphi.n),
                                          (s_xsp, max(0, s_xsp.n - 1)))):
                w(e)
                e.scalar_tensor_tensor(xlo[sx], xnfh[:, kd * c.CHT:(kd + 1) * c.CHT],
                                       -1.0, xtmp[sx], OP.mult, OP.add).then_inc(s_splo.h, 1)
            em("vector", f_xl)
            s_splo.n += 1

            def f_xc(e, sx=sx, kd=kd, w=W((s_splo, s_splo.n))):
                w(e)
                e.copy(xnfl[:, kd * c.CHT:(kd + 1) * c.CHT], xlo[sx]).then_inc(s_xsp.h, 1)
            em("scalar", f_xc)
            s_xsp.n += 1
        ldxn_here = s_xsp.n

        for ci, (kind, ec) in enumerate(chunks):
            cgi = th * NCHK + ci
            par = (cgi % 2) * 4
            ev_guard = (cgi - 2 + 1) * TPT if cgi >= 2 else 0

            if kind in ("qkv", "base"):
                src0 = wqkvT if kind == "qkv" else basewT
                srch = wqkvT_h if kind == "qkv" else basewT_h
                srcl = wqkvT_l if kind == "qkv" else basewT_l
                for kd in range(c.KD):
                    sl_w = wtile % 6
                    if th == 0:
                        sf = wtile % 2
                        DMA("sync", wfp[sf],
                            src0[kd * 128:(kd + 1) * 128, ec * 512:ec * 512 + 512],
                            spl[sf], (s_splo, max(0, s_splo.n - 1)))

                        def f_ws(e, sf=sf, sl_w=sl_w,
                                 w=W((spl[sf], spl[sf].n), (s_wfr, max(0, wtile - 5)),
                                     (sps[sl_w % 4], sps[sl_w % 4].n))):
                            w(e)
                            e.tensor_copy(wsbh[sl_w], wfp[sf])
                            e.drain()
                            e.scalar_tensor_tensor(wlo_t, wsbh[sl_w], -1.0, wfp[sf],
                                                   OP.mult, OP.add)
                            e.drain()
                            e.tensor_copy(wsbl[sl_w], wlo_t).then_inc(s_splo.h, 1)
                        em("vector", f_ws)
                        s_splo.n += 1
                        DMA("gpsimd",
                            srch[kd * 128:(kd + 1) * 128, ec * 512:ec * 512 + 512],
                            wsbh[sl_w], sps[sl_w % 4], (s_splo, s_splo.n))
                        DMA("gpsimd",
                            srcl[kd * 128:(kd + 1) * 128, ec * 512:ec * 512 + 512],
                            wsbl[sl_w], sps[sl_w % 4], (s_splo, 0))
                        wt_w = ((s_splo, s_splo.n), (ldw[sl_w], 0))
                    else:
                        extra_w = [(x, x.n) for x in sps] if wtile_th1_first[0] else []
                        wtile_th1_first[0] = False
                        DMA("sync", wsbh[sl_w],
                            srch[kd * 128:(kd + 1) * 128, ec * 512:ec * 512 + 512],
                            ldw[sl_w], (s_wfr, max(0, wtile - 5)), *extra_w)
                        DMA("sync", wsbl[sl_w],
                            srcl[kd * 128:(kd + 1) * 128, ec * 512:ec * 512 + 512],
                            ldwl[sl_w])
                        wt_w = ((ldw[sl_w], ldw[sl_w].n), (ldwl[sl_w], ldwl[sl_w].n))
                    wtile += 1

                    def f_mm(e, kd=kd, par=par, sl_w=sl_w, kind=kind,
                             w=W(*wt_w,
                                 (s_xsp, ldxn_here if (kd == 0 and ci == 0) else 0),
                                 (s_ev1, ev_guard if kd == 0 else 0))):
                        w(e)
                        stop = (kd == c.KD - 1) and kind == "qkv"
                        for tt in range(TPT):
                            xh = xnfh[:, kd * c.CHT + tt * 128:kd * c.CHT + tt * 128 + 128]
                            xl = xnfl[:, kd * c.CHT + tt * 128:kd * c.CHT + tt * 128 + 128]
                            e.matmul(bank(par + tt), xh, wsbh[sl_w],
                                     start=(kd == 0 and tt == 0) if False else (kd == 0),
                                     stop=False)
                            e.matmul(bank(par + tt), xl, wsbh[sl_w], start=False, stop=False)
                            ins = e.matmul(bank(par + tt), xh, wsbl[sl_w], start=False,
                                           stop=stop)
                        ins.then_inc(s_wfr.h, 1)
                    em("tensor", f_mm)
                    s_wfr.n += 1

                if kind == "base":
                    def f_bias(e, par=par, ec=ec):
                        for tt in range(TPT):
                            ins = e.matmul(bank(par + tt), onest[0:1, 0:128],
                                           basebt[0:1, ec * 512:ec * 512 + 512],
                                           start=False, stop=True)
                        ins.then_inc(s_mm1.h, 1)
                    em("tensor", f_bias)
                    s_mm1.n += 1
                done_w = (s_mm1, s_mm1.n) if kind == "base" else (s_wfr, s_wfr.n)

                for tt in range(TPT):
                    sl1 = suse % NSTG

                    def f_ev1(e, par=par, tt=tt, kind=kind, sl1=sl1,
                              w=W(done_w, (st1[sl1], st1[sl1].n))):
                        w(e)
                        if kind == "qkv":
                            e.tensor_scalar(stg1[sl1], bank(par + tt), CLIP, -CLIP,
                                            OP.min, OP.max).then_inc(s_ev1.h, 1)
                        else:
                            e.tensor_copy(stg1[sl1], bank(par + tt)).then_inc(s_ev1.h, 1)
                    em("vector", f_ev1)
                    s_ev1.n += 1

                    trow = th * c.CHT + tt * 128
                    if kind == "qkv":
                        if ec < c.NDC:
                            DMA("gpsimd", qs_s[trow:trow + 128, ec * 512:(ec + 1) * 512],
                                stg1[sl1], st1[sl1], (s_ev1, s_ev1.n))
                        elif ec < 2 * c.NDC:
                            DMA("gpsimd",
                                ks_s[trow:trow + 128, (ec - c.NDC) * 512:(ec - c.NDC + 1) * 512],
                                stg1[sl1], st1[sl1], (s_ev1, s_ev1.n))
                        else:
                            for hb in range(4):
                                h = (ec - 2 * c.NDC) * 4 + hb
                                vrow = ((h // HPN) * HPN + (h % HPN)) * c.TOK + trow
                                DMA("gpsimd", a2av_i[vrow:vrow + 128, :],
                                    stg1[sl1][:, hb * 128:(hb + 1) * 128], st1[sl1],
                                    (s_ev1, s_ev1.n) if hb == 0 else (s_ev1, 0))
                    else:
                        DMA("gpsimd", base_s[trow:trow + 128, ec * 512:(ec + 1) * 512],
                            stg1[sl1], st1[sl1], (s_ev1, s_ev1.n))
                    suse += 1

            elif kind == "arh":
                def f_arh(e, par=par,
                          w=W((s_splc, spre_done), (s_xsp, ldxn_here), (s_ev1, ev_guard))):
                    w(e)
                    for kd in range(c.KD):
                        ah = atch[:, kd * c.NR:(kd + 1) * c.NR]
                        al = atcl[:, kd * c.NR:(kd + 1) * c.NR]
                        xh = xnfh[:, kd * c.CHT:(kd + 1) * c.CHT]
                        xl = xnfl[:, kd * c.CHT:(kd + 1) * c.CHT]
                        e.matmul(bank(par)[0:c.NR, 0:c.CHT], ah, xh,
                                 start=(kd == 0), stop=False)
                        e.matmul(bank(par)[0:c.NR, 0:c.CHT], al, xh, start=False, stop=False)
                        ins = e.matmul(bank(par)[0:c.NR, 0:c.CHT], ah, xl, start=False,
                                       stop=(kd == c.KD - 1))
                    ins.then_inc(s_mm1.h, 1)
                em("tensor", f_arh)
                s_mm1.n += 1

                def f_evarh(e, th=th, par=par, w=W((s_mm1, s_mm1.n))):
                    w(e)
                    e.tensor_copy(rh2[0:c.NR, th * c.CHT:(th + 1) * c.CHT],
                                  bank(par)[0:c.NR, 0:c.CHT]).then_inc(s_ev1.h, TPT)
                em("vector", f_evarh)
                s_ev1.n += TPT
                suse += TPT

            else:
                def f_rtr(e, par=par,
                          w=W((s_splc, spre_done), (s_xsp, ldxn_here), (s_ev1, ev_guard))):
                    w(e)
                    for tt in range(TPT):
                        for kd in range(c.KD):
                            xh = xnfh[:, kd * c.CHT + tt * 128:kd * c.CHT + tt * 128 + 128]
                            xl = xnfl[:, kd * c.CHT + tt * 128:kd * c.CHT + tt * 128 + 128]
                            rh_ = rtch[:, kd * c.NE:(kd + 1) * c.NE]
                            rl_ = rtcl[:, kd * c.NE:(kd + 1) * c.NE]
                            e.matmul(bank(par + tt)[:, 0:c.NE], xh, rh_,
                                     start=(kd == 0), stop=False)
                            e.matmul(bank(par + tt)[:, 0:c.NE], xl, rh_,
                                     start=False, stop=False)
                            ins = e.matmul(bank(par + tt)[:, 0:c.NE], xh, rl_,
                                           start=False, stop=(kd == c.KD - 1))
                    ins.then_inc(s_mm1.h, 1)
                em("tensor", f_rtr)
                s_mm1.n += 1

                def f_evrtr(e, th=th, par=par, w=W((s_mm1, s_mm1.n))):
                    w(e)
                    for tt in range(TPT):
                        g = th * TPT + tt
                        ins = e.tensor_copy(logits[:, g * c.NE:(g + 1) * c.NE],
                                            bank(par + tt)[:, 0:c.NE])
                    ins.then_inc(s_ev1.h, TPT)
                em("vector", f_evrtr)
                s_ev1.n += TPT
                suse += TPT
        mm1_at_th_end.append(s_mm1.n)
        wfr_at_th_end.append(s_wfr.n)
    quiesce("sync", [s_ldpre] + ldw + ldwl + spl)
    quiesce("gpsimd", ldxn + st1 + sps)
    run_phase()
    amark[0] = am1
    st1_finals = [(s, s.n) for s in st1]

    am2 = amark[0]
    qa = [alloc(c.D), alloc(c.D)]
    ka = [alloc(c.D), alloc(c.D)]
    qn = qa
    kn = ka
    sq2 = alloc(c.D)
    stg2 = [alloc(128) for _ in range(8)]
    suse2 = 0

    def f_agv(e, w=W(*st1_finals)):
        w(e)
        from concourse.aot_env import allow_ici_all_to_all
        with allow_ici_all_to_all():
            e.collective_compute("AllToAll", OP.bypass,
                                 replica_groups=[list(range(c.NC))],
                                 ins=[a2av_i.opt()],
                                 outs=[a2av_o.opt()]).then_inc(s_cc.h, 1)
    em("gpsimd", f_agv)
    s_cc.n += 1

    for i in range(c.NT):
        sl = i % 2
        extra = (st1_finals if i == 0 else []) + [(s_qn, max(0, s_qn.n - 2)),
                                                   (s_tr2, max(0, (i - 1) * 2 * c.H))]
        DMA("gpsimd", qa[sl], qs_s[i * 128:(i + 1) * 128, :], lqk[sl], *extra)
        DMA("gpsimd", ka[sl], ks_s[i * 128:(i + 1) * 128, :], lqk[sl])
        for which, buf, nbuf, gw, gb in (("q", qa, qn, gqs, gqb), ("k", ka, kn, gks, gkb)):
            o = 0 if which == "q" else 16

            def f_sc(e, sl=sl, buf=buf, o=o, w=W((lqk[sl], lqk[sl].n))):
                w(e)
                e.activation(sq2, buf[sl], AF.Square,
                             accum_out=smx(o + 2 + sl)).then_inc(s_sq2.h, 1)
            em("scalar", f_sc)
            s_sq2.n += 1

            def f_v1(e, sl=sl, buf=buf, o=o, w=W((lqk[sl], lqk[sl].n), (s_sq2, s_sq2.n))):
                w(e)
                e.reduce_sum(smx(o + 0 + sl), buf[sl], axis=AX)
                e.drain()
                e.tensor_scalar(smx(o + 4 + sl), smx(o + 0 + sl),
                                1.0 / c.D, None, OP.mult)
                e.tensor_scalar(smx(o + 6 + sl), smx(o + 2 + sl),
                                1.0 / c.D, None, OP.mult)
                e.drain()
                e.tensor_tensor(smx(o + 8 + sl), smx(o + 4 + sl),
                                smx(o + 4 + sl), OP.mult)
                e.drain()
                e.tensor_tensor(smx(o + 6 + sl), smx(o + 6 + sl),
                                smx(o + 8 + sl), OP.subtract).then_inc(s_vst2.h, 1)
            em("vector", f_v1)
            s_vst2.n += 1

            def f_sc2(e, sl=sl, o=o, w=W((s_vst2, s_vst2.n))):
                w(e)
                e.activation(smx(o + 10 + sl), smx(o + 6 + sl),
                             AF.Sqrt, bias=epst[:, 0:1]).then_inc(s_sct2.h, 1)
            em("scalar", f_sc2)
            s_sct2.n += 1

            def f_v2(e, sl=sl, buf=buf, nbuf=nbuf, o=o,
                     w=W((s_sct2, s_sct2.n), (s_tr2, max(0, (i - 1) * 2 * c.H)))):
                w(e)
                e.reciprocal(smx(o + 12 + sl), smx(o + 10 + sl))
                e.drain()
                e.tensor_scalar(nbuf[sl], buf[sl], smx(o + 4 + sl),
                                smx(o + 12 + sl),
                                OP.subtract, OP.mult).then_inc(s_qn.h, 1)
            em("vector", f_v2)
            s_qn.n += 1

            for h in range(c.H):
                bnk = (h % 4) + (0 if sl == 0 else 4)

                def f_tr(e, sl=sl, nbuf=nbuf, h=h, bnk=bnk,
                         w=W((s_qn, s_qn.n), (s_ev2, max(0, s_ev2.n - 3)))):
                    w(e)
                    e.transpose(bank(bnk)[:, 0:128], nbuf[sl][:, h * 128:(h + 1) * 128],
                                identt).then_inc(s_tr2.h, 1)
                em("tensor", f_tr)
                s_tr2.n += 1

                s2 = suse2 % 8

                def f_ev(e, h=h, bnk=bnk, gw=gw, gb=gb, s2=s2,
                         w=W((s_tr2, s_tr2.n), (st2[s2], st2[s2].n))):
                    w(e)
                    e.tensor_scalar(stg2[s2], bank(bnk)[:, 0:128],
                                    gw[:, h:h + 1], gb[:, h:h + 1],
                                    OP.mult, OP.add).then_inc(s_ev2.h, 1)
                em("vector", f_ev)
                s_ev2.n += 1
                roff = ((h // HPN) * QKR + (0 if which == "q" else HPN * 128)
                        + (h % HPN) * 128)
                DMA("gpsimd", a2aqk_i[roff:roff + 128, i * 128:(i + 1) * 128],
                    stg2[s2], st2[s2], (s_ev2, s_ev2.n))
                suse2 += 1

    st2_finals = [(s, s.n) for s in st2]

    def f_ag(e, w=W(*st2_finals)):
        w(e)
        from concourse.aot_env import allow_ici_all_to_all
        with allow_ici_all_to_all():
            e.collective_compute("AllToAll", OP.bypass,
                                 replica_groups=[list(range(c.NC))],
                                 ins=[a2aqk_i.opt()],
                                 outs=[a2aqk_o.opt()]).then_inc(s_cc.h, 1)
    em("gpsimd", f_ag)
    s_cc.n += 1
    cc_qkv = s_cc.n
    quiesce("gpsimd", lqk + st2)
    run_phase()
    amark[0] = am2

    csb = alloc(c.NERC * c.TOK)
    csbh = alloc(c.NERC * c.TOK // 2)[:, :].bitcast(BF16)
    csbl = alloc(c.NERC * c.TOK // 2)[:, :].bitcast(BF16)
    clo = alloc(c.CHT)
    wsb3h = [alloc(256)[:, :].bitcast(BF16) for _ in range(6)]
    wsb3l = [alloc(256)[:, :].bitcast(BF16) for _ in range(6)]
    bstg = [alloc(512) for _ in range(4)]
    t1st = [alloc(512) for _ in range(4)]
    gstg = [alloc(512) for _ in range(4)]
    bst4 = [bst[0], bst[1], lqk[0], lqk[1]]
    st34 = [st3[0], st3[1], st2[0], st2[1]]
    prbf = alloc(64)
    prb2f = alloc(64)
    prb = prbf[:, 0:c.NE]
    prb2 = prb2f[:, 0:c.NE]

    for g in range(c.NT):
        lg = logits[:, g * c.NE:(g + 1) * c.NE]

        def f_v(e, lg=lg, w=W((s_ffs, s_ffs.n), (s_trc, s_trc.n))):
            w(e)
            e.reduce_max(smx(0), lg, axis=AX)
            e.drain()
            e.tensor_scalar(smx(1), smx(0), -1.0, None, OP.mult).then_inc(s_ffv.h, 1)
        em("vector", f_v)
        s_ffv.n += 1

        def f_s(e, lg=lg, w=W((s_ffv, s_ffv.n))):
            w(e)
            e.activation(prb, lg, AF.Exp, bias=smx(1),
                         accum_out=smx(2)).then_inc(s_ffs.h, 1)
        em("scalar", f_s)
        s_ffs.n += 1

        def f_v2(e, w=W((s_ffs, s_ffs.n))):
            w(e)
            e.reciprocal(smx(3), smx(2))
            e.drain()
            e.tensor_scalar(prb, prb, smx(3), None, OP.mult)
            e.drain()
            e.reduce_max(smx(4), prb, axis=AX)
            e.drain()
            e.tensor_scalar(prb2, prb, smx(4), None, OP.is_ge)
            e.drain()
            e.scalar_tensor_tensor(prb2, prb2, -2.0, prb, OP.mult, OP.add)
            e.drain()
            e.reduce_max(smx(5), prb2, axis=AX)
            e.drain()
            e.tensor_tensor(smx(6), smx(4), smx(5), OP.add)
            e.drain()
            e.tensor_scalar(smx(6), smx(6), 1e-6, None, OP.add)
            e.drain()
            e.reciprocal(smx(7), smx(6))
            e.drain()
            e.tensor_tensor(smx(8), smx(4), smx(7), OP.mult)
            e.tensor_tensor(smx(9), smx(5), smx(7), OP.mult)
            e.tensor_scalar(prb2, prb2, smx(5), None, OP.is_ge)
            e.tensor_scalar(prb, prb, smx(4), None, OP.is_ge)
            e.drain()
            e.tensor_scalar(prb, prb, smx(8), None, OP.mult)
            e.drain()
            e.scalar_tensor_tensor(prb, prb2, smx(9), prb, OP.mult,
                                   OP.add).then_inc(s_ffv.h, 1)
        em("vector", f_v2)
        s_ffv.n += 1

        def f_tr(e, g=g, w=W((s_ffv, s_ffv.n), (s_cbv, max(0, s_cbv.n - 1)))):
            w(e)
            e.transpose(bank(g % 2)[0:c.NE, 0:128], prb, identt).then_inc(s_trc.h, 1)
        em("tensor", f_tr)
        s_trc.n += 1

        def f_ev(e, g=g, w=W((s_trc, s_trc.n))):
            w(e)
            e.tensor_copy(combf[0:c.NE, g * 128:(g + 1) * 128],
                          bank(g % 2)[0:c.NE, 0:128]).then_inc(s_cbv.h, 1)
        em("vector", f_ev)
        s_cbv.n += 1

    def f_dup(e, w=W((s_ev1, s_ev1.n))):
        w(e)
        e.dma_start(out=rh2[c.NR:2 * c.NR, :], in_=rh2[0:c.NR, :]).then_inc(s_dup.h, 16)
    em("gpsimd", f_dup)
    s_dup.n += 16

    cbv_base = s_cbv.n
    csp_base = s_csp.n
    cb = 0
    for ecc in range(c.NERC):
        for tch in range(c.NTH):
            def f_cb(e, ecc=ecc, tch=tch, cb=cb,
                     w=W((s_cbv, cbv_base + max(0, cb - 1)), (s_dup, 16))):
                w(e)
                e.matmul(bank(2 + cb % 2)[:, 0:c.CHT],
                         selt[0:c.NE, ecc * 128:(ecc + 1) * 128],
                         combf[0:c.NE, tch * c.CHT:(tch + 1) * c.CHT],
                         start=True, stop=True).then_inc(s_cbm.h, 1)
            em("tensor", f_cb)
            s_cbm.n += 1

            def f_cv(e, ecc=ecc, tch=tch, cb=cb, w=W((s_cbm, s_cbm.n))):
                w(e)
                e.tensor_tensor(
                    csb[:, ecc * c.TOK + tch * c.CHT:ecc * c.TOK + (tch + 1) * c.CHT],
                    rh2[:, tch * c.CHT:(tch + 1) * c.CHT],
                    bank(2 + cb % 2)[:, 0:c.CHT], OP.mult).then_inc(s_cbv.h, 1)
            em("vector", f_cv)
            s_cbv.n += 1

            def f_ch(e, ecc=ecc, tch=tch, w=W((s_cbv, s_cbv.n), (s_csp, max(0, s_csp.n - 1)))):
                w(e)
                e.activation(csbh[:, ecc * c.TOK + tch * c.CHT:ecc * c.TOK + (tch + 1) * c.CHT],
                             csb[:, ecc * c.TOK + tch * c.CHT:ecc * c.TOK + (tch + 1) * c.CHT],
                             AF.Identity).then_inc(s_sphi.h, 1)
            em("scalar", f_ch)
            s_sphi.n += 1

            def f_cl(e, ecc=ecc, tch=tch, w=W((s_sphi, s_sphi.n))):
                w(e)
                e.scalar_tensor_tensor(
                    clo, csbh[:, ecc * c.TOK + tch * c.CHT:ecc * c.TOK + (tch + 1) * c.CHT],
                    -1.0, csb[:, ecc * c.TOK + tch * c.CHT:ecc * c.TOK + (tch + 1) * c.CHT],
                    OP.mult, OP.add).then_inc(s_splo.h, 1)
            em("vector", f_cl)
            s_splo.n += 1

            def f_cc2(e, ecc=ecc, tch=tch, w=W((s_splo, s_splo.n))):
                w(e)
                e.copy(csbl[:, ecc * c.TOK + tch * c.CHT:ecc * c.TOK + (tch + 1) * c.CHT],
                       clo).then_inc(s_csp.h, 1)
            em("scalar", f_cc2)
            s_csp.n += 1
            cb += 1
    csp_done = s_csp.n

    btile = 0
    bfp = [alloc(512), alloc(512)]
    blo_t = alloc(512)
    for oc in range(c.NBC):
        for ecc in range(c.NERC):
            sl_b = btile % 6
            sf = btile % 2
            DMA("sync", bfp[sf], bflat[ecc * 128:(ecc + 1) * 128, oc * 512:(oc + 1) * 512],
                spl[sf], (s_splo, max(0, s_splo.n - 1)))

            def f_bs(e, sf=sf, sl_b=sl_b,
                     w=W((spl[sf], spl[sf].n), (s_bfr, max(0, btile - 5)))):
                w(e)
                e.tensor_copy(wsb3h[sl_b], bfp[sf])
                e.drain()
                e.scalar_tensor_tensor(blo_t, wsb3h[sl_b], -1.0, bfp[sf], OP.mult, OP.add)
                e.drain()
                e.tensor_copy(wsb3l[sl_b], blo_t).then_inc(s_splo.h, 1)
            em("vector", f_bs)
            s_splo.n += 1
            bt_w = (s_splo, s_splo.n)
            btile += 1

            def f_dm(e, ecc=ecc, sl_b=sl_b, oc=oc,
                     w=W(bt_w,
                         (s_csp, csp_base + (ecc + 1) * c.NTH if oc == 0 else 0),
                         (s_dev, oc * c.NT if ecc == 0 else 0))):
                w(e)
                for tt in range(c.NT):
                    chs = csbh[:, ecc * c.TOK + tt * 128:ecc * c.TOK + (tt + 1) * 128]
                    cls = csbl[:, ecc * c.TOK + tt * 128:ecc * c.TOK + (tt + 1) * 128]
                    e.matmul(bank(tt), chs, wsb3h[sl_b], start=(ecc == 0), stop=False)
                    e.matmul(bank(tt), cls, wsb3h[sl_b], start=False, stop=False)
                    ins = e.matmul(bank(tt), chs, wsb3l[sl_b], start=False,
                                   stop=(ecc == c.NERC - 1))
                ins.then_inc(s_bfr.h, 1)
            em("tensor", f_dm)
            s_bfr.n += 1

        for tt in range(c.NT):
            dl = (oc * c.NT + tt) % 4
            extra = st1_finals if (oc == 0 and tt == 0) else []
            DMA("gpsimd", bstg[dl], base_s[tt * 128:(tt + 1) * 128, oc * 512:(oc + 1) * 512],
                bst4[dl], (s_dev, max(0, s_dev.n - 3)), *extra)

            def f_dv(e, tt=tt, dl=dl, w=W((s_bfr, s_bfr.n), (bst4[dl], bst4[dl].n),
                                          (s_sil, max(0, s_sil.n - 3)))):
                w(e)
                e.scalar_tensor_tensor(t1st[dl], bank(tt), 0.5, bstg[dl],
                                       OP.mult, OP.add).then_inc(s_dev.h, 1)
            em("vector", f_dv)
            s_dev.n += 1

            def f_sil(e, dl=dl, w=W((s_dev, s_dev.n), (st34[dl], st34[dl].n))):
                w(e)
                e.activation(gstg[dl], t1st[dl], AF.Silu).then_inc(s_sil.h, 1)
            em("scalar", f_sil)
            s_sil.n += 1
            DMA("gpsimd", g_s[tt * 128:(tt + 1) * 128, oc * 512:(oc + 1) * 512],
                gstg[dl], st34[dl], (s_sil, s_sil.n))

    quiesce("sync", ldb)
    quiesce("gpsimd", bst4 + st34 + [s_dup])
    run_phase()

    amark[0] = PERSIST0
    kf2 = [alloc(c.S), alloc(c.S)]
    vt2 = [alloc(c.S), alloc(c.S)]
    scb = [alloc(c.S), alloc(c.S)]
    pT = alloc(c.KB * 256)
    pT3 = pT.rearrange("p (b w) -> p b w", w=256)
    slopek = alloc(c.S)
    negqs = alloc(c.KB)
    qtl = [alloc(128), alloc(128)]
    astg = [alloc(256), alloc(256)]
    pmaxb = alloc(32)
    NTASK = c.B * HPN
    NTB = c.TOK // 128
    NCH = c.KB // 2
    scg = [0]
    sev_bank = [0, 0]
    tb_ctr = [0]
    TBANKS = (2, 3, 6, 7)
    ptv_bank = [0, 0, 0, 0]
    last_smm = {0: 0, 1: 0}
    ptm_scb = {0: 0, 1: 0}
    e1_after = {0: 0, 1: 0}
    rs_after = {0: 0, 1: 0}
    e2_after = {0: 0, 1: 0}
    e2b_after = {0: 0, 1: 0}
    aev_after = {0: 0, 1: 0}
    kfree = {0: 0, 1: 0}
    vfree = {0: 0, 1: 0}
    pv_chunk = [0]

    def emit_pt(t, m, ql, e2n):
        qt = 2 * m + ql
        qp = qt % 2
        P = qt + 1
        e2a_n, e2b_n, PB = e2n
        NG = (P + 3) // 4
        seen_a = [False]
        seen_b = [False]
        for gk in range(NG):
            nb = min(4, P - gk * 4)
            tb = tb_ctr[0] % 4
            tbk = TBANKS[tb]
            tb_ctr[0] += 1
            wa = wb = 0
            if gk * 4 < PB and not seen_a[0]:
                wa = e2a_n
                seen_a[0] = True
            if gk * 4 + nb > PB and not seen_b[0]:
                wb = e2b_n
                seen_b[0] = True

            def f_pt(e, qp=qp, gk=gk, nb=nb, tbk=tbk, tb=tb, wa=wa, wb=wb,
                     w=W((s_e2, wa), (s_cbm, wb),
                         (s_ptv, ptv_bank[tb]))):
                w(e)
                for q4 in range(nb):
                    kb = gk * 4 + q4
                    ins = e.transpose(bank(tbk)[:, q4 * 128:(q4 + 1) * 128],
                                      scb[qp][:, kb * 128:(kb + 1) * 128], identt)
                ins.then_inc(s_ptm.h, 1)
            em("tensor", f_pt)
            s_ptm.n += 1
            last_g = (ql == 1 and gk == NG - 1)

            def f_ptv(e, ql=ql, gk=gk, nb=nb, tbk=tbk, m=m, last_g=last_g,
                      w=W((s_ptm, s_ptm.n),
                          (s_pv, pv_chunk[0] if (ql == 0 and gk == 0) else 0))):
                w(e)
                ins = e.tensor_copy(
                    pT3[:, gk * 4:gk * 4 + nb, ql * 128:ql * 128 + 128],
                    bank(tbk)[:, 0:nb * 128].rearrange("p (b w) -> p b w",
                                                       w=128))
                if last_g:
                    ins = e.memset(pT[:, (2 * m + 1) * 256:(2 * m + 1) * 256
                                      + 128], 0.0)
                ins.then_inc(s_ptv.h, 1)
            em("vector", f_ptv)
            s_ptv.n += 1
            ptv_bank[tb] = s_ptv.n
        ptm_scb[qp] = s_ptm.n

    def emit_pvtail(t, tp, m, cp, kvw=0):
        NKB = 2 * m + 2

        def f_pv(e, tp=tp, cp=cp, NKB=NKB,
                 w=W((s_ptv, s_ptv.n), (s_aev, aev_after[cp]), (s_kv, kvw))):
            w(e)
            for kb in range(NKB):
                ins = e.matmul(bank(4 + cp)[:, 0:256],
                               vt2[tp][:, kb * 128:(kb + 1) * 128],
                               pT[:, kb * 256:(kb + 1) * 256],
                               start=(kb == 0), stop=(kb == NKB - 1))
            ins.then_inc(s_pv.h, 1)
        em("tensor", f_pv)
        s_pv.n += 1
        pv_chunk[0] = s_pv.n

        def f_aev(e, cp=cp, w=W((s_pv, s_pv.n), (ast[cp], ast[cp].n))):
            w(e)
            e.tensor_copy(astg[cp], bank(4 + cp)[:, 0:256]).then_inc(s_aev.h, 1)
        em("vector", f_aev)
        s_aev.n += 1
        aev_after[cp] = s_aev.n
        rpq = (2 * m * 128) // c.TOK
        qc2 = 2 * m * 128 - rpq * c.TOK
        aor = ((t // HPN) * c.R4 + rpq) * HPN * 128 + (t % HPN) * 128
        DMA("gpsimd", a2aao_i[aor:aor + 128, qc2:qc2 + 256], astg[cp],
            ast[cp], (s_aev, s_aev.n))

    sev_task = 0
    for t in range(NTASK):
        tp = t % 2
        hd = t % HPN
        bsel = t // HPN
        first = True
        for rp in range(c.R4):
            snd = bsel * c.R4 + rp
            gw_ = (((s_smm, kfree[tp]), (s_cc, cc_qkv)) if first else ())
            first = False
            krow = snd * QKR + HPN * 128 + hd * 128
            DMA("sync", kf2[tp][:, rp * c.TOK:(rp + 1) * c.TOK],
                a2aqk_o[krow:krow + 128, :], s_ini, *gw_)
        k4 = s_ini.n
        vw_ = ((s_pv, vfree[tp]),)
        for rp in range(c.R4):
            snd = bsel * c.R4 + rp
            for lb in range(NTB):
                vrow = snd * HPN * c.TOK + hd * c.TOK + lb * 128
                vcol = rp * c.TOK + lb * 128
                DMA("sync", vt2[tp][:, vcol:vcol + 128],
                    a2av_o[vrow:vrow + 128, :], s_kv, *vw_)
                vw_ = ()
        kv_here = s_kv.n

        DMA("sync", slopek, kpos_d, s_dbs, (s_sev, sev_task))

        def f_nqs(e, t=t, w=W((s_dbs, s_dbs.n))):
            w(e)
            e.tensor_scalar(slopek, slopek, slvt[:, t:t + 1], None, OP.mult)
            e.tensor_scalar(negqs, negqc, slvt[:, t:t + 1], None,
                            OP.mult).then_inc(s_dbv.h, 1)
        em("vector", f_nqs)
        s_dbv.n += 1
        dbv_task = s_dbv.n

        def emit_qt(m, ql, kv_first, t=t, tp=tp, hd=hd, bsel=bsel,
                    dbv_task=dbv_task, kv_here=k4):
            qt = 2 * m + ql
            qp = qt % 2
            PW = (qt + 1) * 128
            SCq = (PW + 511) // 512
            rp = (qt * 128) // c.TOK
            qcol = qt * 128 - rp * c.TOK
            qrow = (bsel * c.R4 + rp) * QKR + hd * 128
            DMA("sync", qtl[qp], a2aqk_o[qrow:qrow + 128, qcol:qcol + 128],
                qfl[qp], (s_smm, last_smm[qp]))
            dstart = qt * 128
            for sc in range(SCq):
                c0 = sc * 512
                wdt = min(512, PW - c0)
                bp = scg[0] % 2
                scg[0] += 1

                def f_smm(e, qp=qp, sc=sc, wdt=wdt, bp=bp, tp=tp,
                          kvw=(kv_here if (kv_first and sc == 0) else 0),
                          w=W((qfl[qp], qfl[qp].n if sc == 0 else 0),
                              (s_sev, sev_bank[bp]))):
                    if kvw:
                        e.wait_ge(s_ini.h, kvw)
                    w(e)
                    e.matmul(bank(bp)[:, 0:wdt], qtl[qp],
                             kf2[tp][:, sc * 512:sc * 512 + wdt],
                             start=True, stop=True).then_inc(s_smm.h, 1)
                em("tensor", f_smm)
                s_smm.n += 1
                w1 = wdt if c0 + wdt <= dstart else max(0, dstart - c0)

                def f_sev(e, qt=qt, qp=qp, t=t, c0=c0, wdt=wdt, w1=w1, bp=bp,
                          w=W((s_smm, s_smm.n), (s_dbv, dbv_task),
                              (s_ptm, ptm_scb[qp] if sc == 0 else 0))):
                    w(e)
                    ins = None
                    if w1 > 0:
                        ins = e.scalar_tensor_tensor(
                            scb[qp][:, c0:c0 + w1], slopek[:, c0:c0 + w1],
                            negqs[:, qt:qt + 1], bank(bp)[:, 0:w1],
                            OP.add, OP.add)
                    if w1 < wdt:
                        ins = e.scalar_tensor_tensor(
                            scb[qp][:, c0 + w1:c0 + wdt], dgt,
                            slvt[:, t:t + 1], bank(bp)[:, w1:wdt],
                            OP.mult, OP.add)
                    ins.then_inc(s_sev.h, 1)
                em("vector", f_sev)
                s_sev.n += 1
                sev_bank[bp] = s_sev.n

                def f_pm(e, qp=qp, sc=sc, c0=c0, wdt=wdt,
                         lastsc=(sc == SCq - 1), w=W((s_sev, s_sev.n))):
                    w(e)
                    ins = e.reduce_max(pmaxb[:, qp * 16 + sc:qp * 16 + sc + 1],
                                       scb[qp][:, c0:c0 + wdt], axis=AX)
                    if lastsc:
                        ins.then_inc(s_mx.h, 1)
                em("vector", f_pm)
            s_mx.n += 1
            last_smm[qp] = s_smm.n

            def f_mxf(e, qp=qp, SCq=SCq, w=W((s_mx, s_mx.n),
                                             (s_e1, e1_after[qp]))):
                w(e)
                e.reduce_max(smx(20 + qp), pmaxb[:, qp * 16:qp * 16 + SCq],
                             axis=AX).then_inc(s_mx.h, 1)
            em("vector", f_mxf)
            s_mx.n += 1

            def f_e1(e, qp=qp, PW=PW, w=W((s_mx, s_mx.n),
                                          (s_rs, rs_after[qp]))):
                w(e)
                e.activation(smx(22 + qp), smx(20 + qp), AF.Identity,
                             scale=-1.0)
                e.drain()
                e.activation(scb[qp][:, 0:PW], scb[qp][:, 0:PW], AF.Exp,
                             bias=smx(22 + qp),
                             accum_out=smx(24 + qp)).then_inc(s_e1.h, 1)
            em("scalar", f_e1)
            s_e1.n += 1
            e1_after[qp] = s_e1.n

            def f_rs(e, qp=qp, w=W((s_e1, s_e1.n), (s_e2, e2_after[qp]))):
                w(e)
                e.reciprocal(smx(26 + qp), smx(24 + qp)).then_inc(s_rs.h, 1)
            em("vector", f_rs)
            s_rs.n += 1
            rs_after[qp] = s_rs.n

            def f_e2(e, qp=qp, PW=PW, w=W((s_rs, s_rs.n))):
                w(e)
                e.activation(scb[qp][:, 0:PW], scb[qp][:, 0:PW], AF.Identity,
                             scale=smx(26 + qp)).then_inc(s_e2.h, 1)
            em("scalar", f_e2)
            s_e2.n += 1
            e2_after[qp] = s_e2.n
            return (s_e2.n, 0, qt + 1)

        pend = None
        for m in range(NCH):
            e2q0 = emit_qt(m, 0, kv_first=(m == 0))
            if pend is not None:
                emit_pt(t, pend[0], 1, pend[1])
                emit_pvtail(t, tp, pend[0], pend[0] % 2,
                            kvw=(kv_here if pend[0] == 0 else 0))
            e2q1 = emit_qt(m, 1, kv_first=False)
            emit_pt(t, m, 0, e2q0)
            pend = (m, e2q1)
        emit_pt(t, pend[0], 1, pend[1])
        emit_pvtail(t, tp, pend[0], pend[0] % 2)
        kfree[tp] = s_smm.n
        vfree[tp] = s_pv.n
        sev_task = s_sev.n

    def f_ao(e, w=W((ast[0], ast[0].n), (ast[1], ast[1].n))):
        w(e)
        from concourse.aot_env import allow_ici_all_to_all
        with allow_ici_all_to_all():
            e.collective_compute("AllToAll", OP.bypass,
                                 replica_groups=[list(range(c.NC))],
                                 ins=[a2aao_i.opt()],
                                 outs=[a2aao_o.opt()]).then_inc(s_cc.h, 1)
    em("gpsimd", f_ao)
    s_cc.n += 1
    cc_ao = s_cc.n
    quiesce("sync", [s_kv, s_ini, s_dbs] + qfl)
    run_phase()
    amark[0] = PERSIST0

    atnh = alloc(c.KD * c.CHT // 2)[:, :].bitcast(BF16)
    atnl = alloc(c.KD * c.CHT // 2)[:, :].bitcast(BF16)
    atmp = [alloc(c.CHT), alloc(c.CHT)]
    alo = [alloc(c.CHT), alloc(c.CHT)]
    wsb5h = [alloc(256)[:, :].bitcast(BF16) for _ in range(6)]
    wsb5l = [alloc(256)[:, :].bitcast(BF16) for _ in range(6)]
    gld5 = [alloc(512), alloc(512)]
    xld5 = [alloc(512), alloc(512)]
    t1s = [alloc(512), alloc(512)]
    ost = [alloc(512), alloc(512)]
    ast_finals = [(s, s.n) for s in ast]
    st3_finals = [(s, s.n) for s in st3]
    TPT = c.TPT
    wfr5_after = []

    wfp5 = [alloc(512), alloc(512)]
    wlo5 = alloc(512)
    w5_first = [True]
    for th in range(c.NTH):
        asp_base = s_asp.n
        for fc in range(c.KD):
            sx = fc % 2
            extra = ([(s_wfr, wfr5_after[th * c.NBC - 1] if th > 0 else 0),
                      (s_cc, cc_ao)]
                     if fc == 0 else [(s_asp, max(0, s_asp.n - 1))])
            DMA("gpsimd", atmp[sx],
                a2aao_o[fc * 128:(fc + 1) * 128, th * c.CHT:(th + 1) * c.CHT],
                lat[sx], *extra)

            def f_ah(e, sx=sx, fc=fc, w=W((lat[sx], lat[sx].n), (s_asp, max(0, s_asp.n - 1)))):
                w(e)
                e.activation(atnh[:, fc * c.CHT:(fc + 1) * c.CHT], atmp[sx],
                             AF.Identity).then_inc(s_sphi.h, 1)
            em("scalar", f_ah)
            s_sphi.n += 1

            def f_al(e, sx=sx, fc=fc, w=W((s_sphi, s_sphi.n))):
                w(e)
                e.scalar_tensor_tensor(alo[sx], atnh[:, fc * c.CHT:(fc + 1) * c.CHT],
                                       -1.0, atmp[sx], OP.mult, OP.add).then_inc(s_splo.h, 1)
            em("vector", f_al)
            s_splo.n += 1

            def f_ac(e, sx=sx, fc=fc, w=W((s_splo, s_splo.n))):
                w(e)
                e.copy(atnl[:, fc * c.CHT:(fc + 1) * c.CHT], alo[sx]).then_inc(s_asp.h, 1)
            em("scalar", f_ac)
            s_asp.n += 1
        lat_here = s_asp.n
        for oc in range(c.NBC):
            cgi5 = th * c.NBC + oc
            par = (cgi5 % 2) * 4
            for fc in range(c.KD):
                sl_w = (cgi5 * c.KD + fc) % 6
                if th == 0:
                    sf = fc % 2
                    DMA("sync", wfp5[sf],
                        woutT[fc * 128:(fc + 1) * 128, oc * 512:(oc + 1) * 512],
                        spl[sf], (s_splo, max(0, s_splo.n - 1)))

                    def f_w5(e, sf=sf, sl_w=sl_w,
                             w=W((spl[sf], spl[sf].n), (s_wfr, max(0, s_wfr.n - 5)),
                                 (sps[sl_w % 4], sps[sl_w % 4].n))):
                        w(e)
                        e.tensor_copy(wsb5h[sl_w], wfp5[sf])
                        e.drain()
                        e.scalar_tensor_tensor(wlo5, wsb5h[sl_w], -1.0, wfp5[sf],
                                               OP.mult, OP.add)
                        e.drain()
                        e.tensor_copy(wsb5l[sl_w], wlo5).then_inc(s_splo.h, 1)
                    em("vector", f_w5)
                    s_splo.n += 1
                    DMA("gpsimd", woutT_h[fc * 128:(fc + 1) * 128, oc * 512:(oc + 1) * 512],
                        wsb5h[sl_w], sps[sl_w % 4], (s_splo, s_splo.n))
                    DMA("gpsimd", woutT_l[fc * 128:(fc + 1) * 128, oc * 512:(oc + 1) * 512],
                        wsb5l[sl_w], sps[sl_w % 4], (s_splo, 0))
                    w5_w = ((s_splo, s_splo.n),)
                else:
                    extra5 = [(x, x.n) for x in sps] if w5_first[0] else []
                    w5_first[0] = False
                    DMA("sync", wsb5h[sl_w],
                        woutT_h[fc * 128:(fc + 1) * 128, oc * 512:(oc + 1) * 512], ldw[sl_w],
                        (s_wfr, max(0, s_wfr.n - 5)), *extra5)
                    DMA("sync", wsb5l[sl_w],
                        woutT_l[fc * 128:(fc + 1) * 128, oc * 512:(oc + 1) * 512],
                        ldwl[sl_w])
                    w5_w = ((ldw[sl_w], ldw[sl_w].n), (ldwl[sl_w], ldwl[sl_w].n))

                def f_wm(e, fc=fc, par=par, sl_w=sl_w,
                         w=W(*w5_w,
                             (s_asp, asp_base + fc + 1 if oc == 0 else 0),
                             (s_wv1, (cgi5 - 2 + 1) * TPT if (cgi5 >= 2 and fc == 0) else 0))):
                    w(e)
                    for tt in range(TPT):
                        ah_ = atnh[:, fc * c.CHT + tt * 128:fc * c.CHT + tt * 128 + 128]
                        al_ = atnl[:, fc * c.CHT + tt * 128:fc * c.CHT + tt * 128 + 128]
                        e.matmul(bank(par + tt), ah_, wsb5h[sl_w], start=(fc == 0), stop=False)
                        e.matmul(bank(par + tt), al_, wsb5h[sl_w], start=False, stop=False)
                        ins = e.matmul(bank(par + tt), ah_, wsb5l[sl_w], start=False,
                                       stop=(fc == c.KD - 1))
                    ins.then_inc(s_wfr.h, 1)
                em("tensor", f_wm)
                s_wfr.n += 1
            wfr5_after.append(s_wfr.n)

            for tt in range(TPT):
                dl = (cgi5 * TPT + tt) % 2
                trow = th * c.CHT + tt * 128
                extra = st3_finals if (cgi5 == 0 and tt == 0) else []
                DMA("gpsimd", gld5[dl], g_s[trow:trow + 128, oc * 512:(oc + 1) * 512],
                    gx[dl], (s_wv1, max(0, s_wv1.n - 1)), *extra)
                DMA("gpsimd", xld5[dl], xsh[trow:trow + 128, oc * 512:(oc + 1) * 512], gx[dl])

                def f_wv(e, par=par, tt=tt, dl=dl,
                         w=W((s_wfr, s_wfr.n), (gx[dl], gx[dl].n),
                             (yst[dl], yst[dl].n))):
                    w(e)
                    e.tensor_tensor(t1s[dl], bank(par + tt), gld5[dl], OP.mult)
                    e.drain()
                    e.tensor_tensor(ost[dl], t1s[dl], xld5[dl], OP.add).then_inc(s_wv1.h, 1)
                em("vector", f_wv)
                s_wv1.n += 1
                DMA("gpsimd", y[trow:trow + 128, oc * 512:(oc + 1) * 512], ost[dl],
                    yst[dl], (s_wv1, s_wv1.n))

    def f_fin(e, w=W((yst[0], yst[0].n), (yst[1], yst[1].n))):
        w(e)
    em("gpsimd", f_fin)
    quiesce("sync", ldw + ldwl)
    quiesce("gpsimd", gx + lat)
    run_phase()
    return nc


def shard_inputs(c, L0_attn_hidden_states, L0_attn_Wqkv, L0_attn_q_ln_w, L0_attn_q_ln_b,
                 L0_attn_k_ln_w, L0_attn_k_ln_b, L0_attn_Wout, L0_ffn_base_w,
                 L0_ffn_base_b, L0_ffn_A, L0_ffn_B, L0_ffn_router_w):
    x = np.asarray(L0_attn_hidden_states, dtype=np.float32)
    com = {
        "wqkvT": np.ascontiguousarray(np.asarray(L0_attn_Wqkv, np.float32).T),
        "woutT": np.ascontiguousarray(np.asarray(L0_attn_Wout, np.float32).T),
        "basewT": np.ascontiguousarray(np.asarray(L0_ffn_base_w, np.float32).T),
        "baseb": np.asarray(L0_ffn_base_b, np.float32).reshape(1, c.D),
        "aT": np.ascontiguousarray(np.asarray(L0_ffn_A, np.float32).T),
        "routerT": np.ascontiguousarray(np.asarray(L0_ffn_router_w, np.float32).T),
        "bflat": np.ascontiguousarray(
            np.asarray(L0_ffn_B, np.float32).transpose(0, 2, 1).reshape(c.ER, c.D)),
        "qlnw": np.asarray(L0_attn_q_ln_w, np.float32).reshape(1, c.D),
        "qlnb": np.asarray(L0_attn_q_ln_b, np.float32).reshape(1, c.D),
        "klnw": np.asarray(L0_attn_k_ln_w, np.float32).reshape(1, c.D),
        "klnb": np.asarray(L0_attn_k_ln_b, np.float32).reshape(1, c.D),
    }
    in_maps = []
    for core in range(c.NC):
        b, r = core // c.R4, core % c.R4
        sh = dict(com)
        sh["xsh"] = np.ascontiguousarray(x[b, r * c.TOK:(r + 1) * c.TOK, :])
        sh["qpos"] = np.arange(r * c.TOK, (r + 1) * c.TOK, dtype=np.float32).reshape(c.TOK, 1)
        hpn = c.H // c.NC
        sl = alibi_slopes(c.H)[core * hpn:(core + 1) * hpn]
        sh["slv"] = np.ascontiguousarray(np.broadcast_to(
            np.tile(sl, c.B).reshape(1, c.B * hpn), (128, c.B * hpn)))
        in_maps.append(sh)
    return in_maps


def gather_out(c, results):
    out = np.empty((c.B, c.S, c.D), np.float32)
    for core in range(c.NC):
        b, r = core // c.R4, core % c.R4
        out[b, r * c.TOK:(r + 1) * c.TOK, :] = results[core]["y"]
    return out


def kernel(**inputs):
    c = Cfg()
    in_maps = shard_inputs(c, **inputs)
    nc = build_program(c)
    res = run_bass_kernel_spmd(nc, in_maps, core_ids=list(range(c.NC)))
    return gather_out(c, res.results)


# version 16
# speedup vs baseline: 1.0448x; 1.0194x over previous
import math
import numpy as np
import concourse.bass as bass
import concourse.mybir as mybir
from concourse.bass_utils import run_bass_kernel_spmd

F32 = mybir.dt.float32
BF16 = mybir.dt.bfloat16
AX = mybir.AxisListType.X
OP = mybir.AluOpType
AF = mybir.ActivationFunctionType

LN_EPS = 1e-5
CLIP = 8.0
MASKMUL = -1e27
ARENA_COLS = 45056


def alibi_slopes(n_heads, bias_max=8.0):
    next_pow2 = 2 ** int(np.ceil(np.log2(n_heads)))
    m = np.arange(1, next_pow2 + 1, dtype=np.float32) * (bias_max / next_pow2)
    slopes = 1.0 / np.power(2.0, m)
    if next_pow2 != n_heads:
        slopes = np.concatenate([slopes[1::2], slopes[::2]])[:n_heads]
    return slopes.astype(np.float32)


class Cfg:
    def __init__(self, D=4096, S=4096, B=2, H=32, NE=16, NR=64):
        self.D, self.S, self.B, self.H = D, S, B, H
        self.DH = 128
        assert H * self.DH == D
        self.NC = 8
        self.R4 = self.NC // B
        self.TOK = B * S // self.NC
        self.NT = self.TOK // 128
        self.KD = D // 128
        self.CH = 512
        self.CHT = min(512, self.TOK)
        self.NTH = self.TOK // self.CHT
        self.TPT = self.CHT // 128
        self.NQC = 3 * D // self.CH
        self.NDC = D // self.CH
        self.NBC = D // self.CH
        self.NE, self.NR = NE, NR
        self.ER = NE * NR
        self.NERC = self.ER // 128
        self.SC = S // self.CH
        self.KB = S // 128
        self.HQ = min(2, H // self.R4)
        self.NHG = H // self.HQ


class Sem:
    def __init__(self, nc, name):
        self.h = nc.alloc_semaphore(name=name)
        self.n = 0


def build_program(c):
    nc = bass.Bass(trn_type="TRN2", target_bir_lowering=False)

    xsh = nc.dram_tensor("xsh", [c.TOK, c.D], F32, kind="ExternalInput").ap()
    qpos = nc.dram_tensor("qpos", [c.TOK, 1], F32, kind="ExternalInput").ap()
    slv = nc.dram_tensor("slv", [128, c.H // c.R4], F32, kind="ExternalInput").ap()
    wqkvT = nc.dram_tensor("wqkvT", [c.D, 3 * c.D], F32, kind="ExternalInput").ap()
    woutT = nc.dram_tensor("woutT", [c.D, c.D], F32, kind="ExternalInput").ap()
    basewT = nc.dram_tensor("basewT", [c.D, c.D], F32, kind="ExternalInput").ap()
    baseb = nc.dram_tensor("baseb", [1, c.D], F32, kind="ExternalInput").ap()
    aT = nc.dram_tensor("aT", [c.D, c.NR], F32, kind="ExternalInput").ap()
    routerT = nc.dram_tensor("routerT", [c.D, c.NE], F32, kind="ExternalInput").ap()
    bflat = nc.dram_tensor("bflat", [c.ER, c.D], F32, kind="ExternalInput").ap()
    qlnw = nc.dram_tensor("qlnw", [1, c.D], F32, kind="ExternalInput").ap()
    qlnb = nc.dram_tensor("qlnb", [1, c.D], F32, kind="ExternalInput").ap()
    klnw = nc.dram_tensor("klnw", [1, c.D], F32, kind="ExternalInput").ap()
    klnb = nc.dram_tensor("klnb", [1, c.D], F32, kind="ExternalInput").ap()
    y = nc.dram_tensor("y", [c.TOK, c.D], F32, kind="ExternalOutput").ap()

    xnf_s = nc.dram_tensor("xnf_s", [c.D, c.TOK], F32).ap()
    qs_s = nc.dram_tensor("qs_s", [c.TOK, c.D], F32).ap()
    ks_s = nc.dram_tensor("ks_s", [c.TOK, c.D], F32).ap()
    HPN = c.H // c.NC
    QKR = 2 * HPN * 128
    a2aqk_i = nc.dram_tensor("a2aqk_i", [c.NC * QKR, c.TOK], F32).ap()
    a2aqk_o = nc.dram_tensor("a2aqk_o", [c.NC * QKR, c.TOK], F32).ap()
    a2av_i = nc.dram_tensor("a2av_i", [c.NC * HPN * c.TOK, c.DH], F32).ap()
    a2av_o = nc.dram_tensor("a2av_o", [c.NC * HPN * c.TOK, c.DH], F32).ap()
    a2aao_i = nc.dram_tensor("a2aao_i", [c.NC * HPN * 128, c.TOK], F32).ap()
    a2aao_o = nc.dram_tensor("a2aao_o", [c.NC * HPN * 128, c.TOK], F32).ap()
    base_s = nc.dram_tensor("base_s", [c.TOK, c.D], F32).ap()
    g_s = nc.dram_tensor("g_s", [c.TOK, c.D], F32).ap()
    wqkvT_h = nc.dram_tensor("wqkvT_h", [c.D, 3 * c.D], BF16).ap()
    wqkvT_l = nc.dram_tensor("wqkvT_l", [c.D, 3 * c.D], BF16).ap()
    basewT_h = nc.dram_tensor("basewT_h", [c.D, c.D], BF16).ap()
    basewT_l = nc.dram_tensor("basewT_l", [c.D, c.D], BF16).ap()
    woutT_h = nc.dram_tensor("woutT_h", [c.D, c.D], BF16).ap()
    woutT_l = nc.dram_tensor("woutT_l", [c.D, c.D], BF16).ap()
    bflat_h = nc.dram_tensor("bflat_h", [c.ER, c.D], BF16).ap()
    bflat_l = nc.dram_tensor("bflat_l", [c.ER, c.D], BF16).ap()

    ident_d = nc.inline_tensor(np.eye(128, dtype=np.float32), name="identc").ap()
    kpos_d = nc.inline_tensor(
        np.broadcast_to(np.arange(c.S, dtype=np.float32), (128, c.S)).copy(),
        name="kposc").ap()
    sel_np = np.zeros((c.NE, c.NERC * 128), np.float32)
    for _ecc in range(c.NERC):
        for _m in range(128):
            sel_np[(_ecc * 128 + _m) // c.NR, _ecc * 128 + _m] = 1.0
    sel2_d = nc.inline_tensor(sel_np, name="sel2c").ap()
    ones1_d = nc.inline_tensor(np.ones((1, 128), np.float32), name="ones1c").ap()
    negq_np = -(np.arange(c.KB, dtype=np.float32)[None, :] * 128
                + np.arange(128, dtype=np.float32)[:, None])
    negq_d = nc.inline_tensor(np.ascontiguousarray(negq_np), name="negqc").ap()
    dg_np = (np.arange(128, dtype=np.float32)[None, :]
             - np.arange(128, dtype=np.float32)[:, None])
    dg_np = np.where(dg_np <= 0, dg_np, np.float32(-1e30))
    dg_d = nc.inline_tensor(np.ascontiguousarray(dg_np), name="dgmask").ap()
    slopes = alibi_slopes(c.H)

    psall = nc.alloc_psum_tensor("psall", [128, 8 * 512], F32).ap()

    def bank(i):
        return psall[:, i * 512:(i + 1) * 512]

    arena = nc.alloc_sbuf_tensor("arena", [128, ARENA_COLS], F32).ap()
    amark = [0]

    def alloc(cols):
        ap = arena[:, amark[0]:amark[0] + cols]
        amark[0] += cols
        assert amark[0] <= ARENA_COLS, f"arena overflow {amark[0]}"
        return ap

    identt = alloc(128)
    onest = alloc(128)
    dgt = alloc(128)
    selt = alloc(c.NERC * 128)
    basebt = alloc(c.D)
    gqs, gqb, gks, gkb = alloc(c.H), alloc(c.H), alloc(c.H), alloc(c.H)
    qpost = alloc(c.NT)
    negqp = alloc(c.NT)
    negqc = alloc(c.KB)
    slvt = alloc(c.H // c.R4)
    sm = alloc(16 * 36)
    epst = alloc(16)


    def smx(i):
        return sm[:, i * 16:i * 16 + 1]
    PERSIST0 = amark[0]
    logits = alloc(c.NT * c.NE)
    rh2 = alloc(c.TOK)
    combf = alloc(c.TOK)
    atc = alloc(c.KD * c.NR)
    rtc = alloc(c.KD * c.NE)

    def S_(nm):
        return Sem(nc, nm)

    def slots(nm, n):
        return [S_(f"{nm}{i}") for i in range(n)]

    s_sq, s_vst, s_sct, s_xn = S_("sq"), S_("vst"), S_("sct"), S_("xn")
    s_tr0, s_ev0, s_wfr, s_mm1, s_ev1 = (S_(n) for n in ("tr0", "ev0", "wfr", "mm1", "ev1"))
    s_sq2, s_vst2, s_sct2, s_qn, s_tr2, s_ev2 = s_sq, s_vst, s_sct, s_xn, s_tr0, s_ev0
    s_cc, s_ffv, s_ffs, s_trc, s_cbm, s_cbv = (S_(n) for n in
        ("cc", "ffv", "ffs", "trc", "cbm", "cbv"))
    s_bfr, s_dev, s_sil = (S_(n) for n in ("bfr", "dev", "sil"))
    s_dbs, s_dbv, s_smm, s_sev, s_mx, s_e1, s_rs, s_e2 = (S_(n) for n in
        ("dbs", "dbv", "smm", "sev", "mx", "e1", "rs", "e2"))
    s_ptm, s_ptv, s_pv, s_aev, s_wv1 = (S_(n) for n in
        ("ptm", "ptv", "pv", "aev", "wv1"))
    s_ini = S_("ini")
    ldx = slots("ldx", 2)
    st0 = slots("st0", 2)
    ldxn = slots("ldxn", 2)
    s_ldpre = S_("ldpre")
    ldw = slots("ldw", 6)
    st1 = slots("st1", 6)
    lqk = slots("lqk", 2)
    st2 = slots("st2", 8)
    ldb = ldw
    bst = slots("bst", 2)
    st3 = slots("st3", 2)
    s_dup = S_("dup")
    s_loc = S_("loc")
    s_kv = S_("kv")
    qfl = slots("qfl", 2)
    ast = slots("ast", 2)
    lat = slots("lat", 2)
    gx = slots("gx", 2)
    yst = slots("yst", 2)
    spl = slots("spl", 4)
    sps = slots("sps", 4)
    s_sphi = S_("sphi")
    s_splo = S_("splo")
    s_splc = S_("splc")
    ldwl = slots("ldwl", 6)
    s_xsp = S_("xsp")
    s_csp = s_xsp
    s_asp = s_xsp

    ENGS = ("sync", "gpsimd", "tensor", "vector", "scalar")
    ops = {e: [] for e in ENGS}

    def em(eng, fn):
        ops[eng].append(fn)

    def run_phase():
        nonlocal ops
        with nc.Block() as b:
            for e in ENGS:
                if ops[e]:
                    getattr(b, e)(lambda eng, fns=tuple(ops[e]): [f(eng) for f in fns])
        nc.all_engine_barrier()
        ops = {e: [] for e in ENGS}

    def W(*pairs):
        frozen = tuple((s.h, v) for (s, v) in pairs if v > 0)

        def apply(e):
            for h, v in frozen:
                e.wait_ge(h, v)
        return apply

    def DMA(eng, out, in_, sem, *waits, slow=False):
        w = W(*waits)

        def f(e, out=out, in_=in_):
            w(e)
            e.dma_start(out=out, in_=in_,
                        allow_slow_non_contiguous=slow).then_inc(sem.h, 16)
        em(eng, f)
        sem.n += 16
        return sem.n

    def quiesce(eng, sems):
        pairs = tuple((sx.h, sx.n) for sx in sems if sx.n > 0)

        def f(e):
            for h, v in pairs:
                e.wait_ge(h, v)
        em(eng, f)

    DMA("sync", identt, ident_d, s_ini)
    DMA("sync", dgt, dg_d, s_ini)
    DMA("sync", selt[0:c.NE, :], sel2_d, s_ini)
    DMA("sync", onest[0:1, :], ones1_d, s_ini)
    DMA("sync", basebt[0:1, :], baseb, s_ini)
    for dst, src in ((gqs, qlnw), (gqb, qlnb), (gks, klnw), (gkb, klnb)):
        DMA("sync", dst, src[0, :].rearrange("(h d) -> d h", d=128), s_ini, slow=True)
    DMA("sync", qpost, qpos[:, 0].rearrange("(t p) -> p t", p=128), s_ini, slow=True)
    DMA("sync", negqc, negq_d, s_ini)
    DMA("sync", slvt, slv, s_ini)

    def f_initv(e, w=W((s_ini, s_ini.n))):
        w(e)
        isq = 1.0 / math.sqrt(c.DH)
        e.tensor_scalar(gqs, gqs, isq, None, OP.mult)
        e.drain()
        e.tensor_scalar(gqb, gqb, isq, None, OP.mult)
        e.drain()
        e.tensor_scalar(negqp, qpost, -1.0, None, OP.mult)
        e.drain()
        e.memset(epst[:, 0:1], LN_EPS)
    em("vector", f_initv)
    quiesce("sync", [s_ini])
    run_phase()

    am0 = amark[0]
    xa = [alloc(c.D), alloc(c.D)]
    xnb = [alloc(c.D), alloc(c.D)]
    sqs = alloc(c.D)
    stg0 = [alloc(512), alloc(512)]
    NG0 = c.KD // 4

    for i in range(c.NT):
        sl = i % 2
        DMA("sync", xa[sl], xsh[i * 128:(i + 1) * 128, :], ldx[sl],
            (s_xn, max(0, s_xn.n - 1)), (s_sq, max(0, s_sq.n - 1)))

        def f_sc(e, sl=sl, w=W((ldx[sl], ldx[sl].n))):
            w(e)
            e.activation(sqs, xa[sl], AF.Square,
                         accum_out=smx(2 + sl)).then_inc(s_sq.h, 1)
        em("scalar", f_sc)
        s_sq.n += 1

        def f_v1(e, sl=sl, w=W((ldx[sl], ldx[sl].n), (s_sq, s_sq.n))):
            w(e)
            e.reduce_sum(smx(0 + sl), xa[sl], axis=AX)
            e.drain()
            e.tensor_scalar(smx(4 + sl), smx(0 + sl), 1.0 / c.D, None, OP.mult)
            e.tensor_scalar(smx(6 + sl), smx(2 + sl), 1.0 / c.D, None, OP.mult)
            e.drain()
            e.tensor_tensor(smx(8 + sl), smx(4 + sl), smx(4 + sl), OP.mult)
            e.drain()
            e.tensor_tensor(smx(6 + sl), smx(6 + sl), smx(8 + sl),
                            OP.subtract).then_inc(s_vst.h, 1)
        em("vector", f_v1)
        s_vst.n += 1

        def f_sc2(e, sl=sl, w=W((s_vst, s_vst.n))):
            w(e)
            e.activation(smx(10 + sl), smx(6 + sl), AF.Sqrt,
                         bias=epst[:, 0:1]).then_inc(s_sct.h, 1)
        em("scalar", f_sc2)
        s_sct.n += 1

        def f_v2(e, sl=sl, w=W((s_sct, s_sct.n), (s_tr0, max(0, (i - 1) * NG0)))):
            w(e)
            e.reciprocal(smx(12 + sl), smx(10 + sl))
            e.drain()
            e.tensor_scalar(xnb[sl], xa[sl], smx(4 + sl), smx(12 + sl),
                            OP.subtract, OP.mult).then_inc(s_xn.h, 1)
        em("vector", f_v2)
        s_xn.n += 1

        for g in range(NG0):
            def f_tr(e, sl=sl, g=g, w=W((s_xn, s_xn.n), (s_ev0, max(0, s_ev0.n - 1)))):
                w(e)
                for q in range(4):
                    kd = g * 4 + q
                    ins = e.transpose(bank(g % 2)[:, q * 128:(q + 1) * 128],
                                      xnb[sl][:, kd * 128:(kd + 1) * 128], identt)
                ins.then_inc(s_tr0.h, 1)
            em("tensor", f_tr)
            s_tr0.n += 1

            def f_ev(e, g=g, w=W((s_tr0, s_tr0.n), (st0[g % 2], st0[g % 2].n))):
                w(e)
                e.tensor_copy(stg0[g % 2], bank(g % 2)).then_inc(s_ev0.h, 1)
            em("vector", f_ev)
            s_ev0.n += 1

            for q in range(4):
                kd = g * 4 + q
                DMA("gpsimd", xnf_s[kd * 128:(kd + 1) * 128, i * 128:(i + 1) * 128],
                    stg0[g % 2][:, q * 128:(q + 1) * 128], st0[g % 2],
                    (s_ev0, s_ev0.n) if q == 0 else (s_ev0, 0))
    SPW = 512
    wtmp = [None] * 4
    lo_f = [None] * 4
    hi_b = [None] * 4
    lo_b = [None] * 4

    def alloc_split_bufs(n):
        for i in range(n):
            wtmp[i] = alloc(SPW)
            lo_f[i] = alloc(SPW)
            hi_b[i] = alloc(SPW // 2)[:, :].bitcast(BF16)
            lo_b[i] = alloc(SPW // 2)[:, :].bitcast(BF16)

    def emit_split(jobs, nsl):
        spc = [0]
        for srcT, hiT, loT, RR, CC in jobs:
            for r0 in range(RR // 128):
                for c0 in range(CC // SPW):
                    sl = spc[0] % nsl
                    spc[0] += 1
                    DMA("sync", wtmp[sl], srcT[r0 * 128:(r0 + 1) * 128,
                                               c0 * SPW:(c0 + 1) * SPW],
                        spl[sl], (s_splo, max(0, s_splo.n - (nsl - 1))))

                    def f_hi(e, sl=sl, w=W((spl[sl], spl[sl].n),
                                           (s_splo, max(0, s_splo.n - (nsl - 1))),
                                           (sps[sl], sps[sl].n))):
                        w(e)
                        e.tensor_copy(hi_b[sl], wtmp[sl])
                        e.drain()
                        e.scalar_tensor_tensor(lo_f[sl], hi_b[sl], -1.0, wtmp[sl],
                                               OP.mult, OP.add)
                        e.drain()
                        e.tensor_copy(lo_b[sl], lo_f[sl]).then_inc(s_splo.h, 1)
                    em("vector", f_hi)
                    s_splo.n += 1

                    DMA("gpsimd", hiT[r0 * 128:(r0 + 1) * 128, c0 * SPW:(c0 + 1) * SPW],
                        hi_b[sl], sps[sl], (s_splo, s_splo.n))
                    DMA("gpsimd", loT[r0 * 128:(r0 + 1) * 128, c0 * SPW:(c0 + 1) * SPW],
                        lo_b[sl], sps[sl], (s_splo, 0))

    sps_finals = [(x, x.n) for x in sps]
    quiesce("sync", ldx + spl)
    quiesce("gpsimd", st0 + sps)
    run_phase()
    amark[0] = am0

    am1 = amark[0]
    xnfh = alloc(c.KD * c.CHT // 2)[:, :].bitcast(BF16)
    xnfl = alloc(c.KD * c.CHT // 2)[:, :].bitcast(BF16)
    xtmp = [alloc(c.CHT), alloc(c.CHT)]
    xlo = [alloc(c.CHT), alloc(c.CHT)]
    atch = alloc(c.KD * c.NR // 2)[:, :].bitcast(BF16)
    atcl = alloc(c.KD * c.NR // 2)[:, :].bitcast(BF16)
    rtch = alloc(c.KD * c.NE // 2)[:, :].bitcast(BF16)
    rtcl = alloc(c.KD * c.NE // 2)[:, :].bitcast(BF16)
    wsbh = [alloc(256)[:, :].bitcast(BF16) for _ in range(6)]
    wsbl = [alloc(256)[:, :].bitcast(BF16) for _ in range(6)]
    stg1 = [alloc(512) for _ in range(6)]
    NSTG = 6
    TPT = c.TPT
    wfp = [alloc(512), alloc(512)]
    wlo_t = alloc(512)

    chunks = ([("qkv", i) for i in range(c.NQC)] + [("base", i) for i in range(c.NBC)]
              + [("arh", 0), ("rtr", 0)])
    NCHK = len(chunks)
    wtile = 0
    mm1_at_th_end = []
    wfr_at_th_end = []
    st0_finals = [(st0[0], st0[0].n), (st0[1], st0[1].n)]

    for kd in range(c.KD):
        DMA("sync", atc[:, kd * c.NR:(kd + 1) * c.NR], aT[kd * 128:(kd + 1) * 128, :], s_ldpre)
        DMA("sync", rtc[:, kd * c.NE:(kd + 1) * c.NE], routerT[kd * 128:(kd + 1) * 128, :],
            s_ldpre)
    ldpre_fin = s_ldpre.n

    def _chunks(src, hi, lo, total):
        cw = min(512, c.CHT, total)
        return [(src[:, i * cw:(i + 1) * cw], hi[:, i * cw:(i + 1) * cw],
                 lo[:, i * cw:(i + 1) * cw]) for i in range(total // cw)]
    spre_pairs = (_chunks(atc, atch, atcl, c.KD * c.NR)
                  + _chunks(rtc, rtch, rtcl, c.KD * c.NE))
    for fsrc, fhi, flo in spre_pairs:
        def f_spre(e, fsrc=fsrc, fhi=fhi, w=W((s_ldpre, ldpre_fin),
                                              (s_splo, max(0, s_splo.n)))):
            w(e)
            e.activation(fhi, fsrc, AF.Identity).then_inc(s_sphi.h, 1)
        em("scalar", f_spre)
        s_sphi.n += 1

        cw_ = fsrc.shape[1]

        def f_spre2(e, fsrc=fsrc, fhi=fhi, cw_=cw_, w=W((s_sphi, s_sphi.n),
                                                        (s_splc, max(0, s_splc.n)))):
            w(e)
            e.scalar_tensor_tensor(xlo[0][:, 0:cw_], fhi, -1.0, fsrc,
                                   OP.mult, OP.add).then_inc(s_splo.h, 1)
        em("vector", f_spre2)
        s_splo.n += 1

        def f_spre3(e, flo=flo, cw_=cw_, w=W((s_splo, s_splo.n))):
            w(e)
            e.copy(flo, xlo[0][:, 0:cw_]).then_inc(s_splc.h, 1)
        em("scalar", f_spre3)
        s_splc.n += 1
    spre_done = s_splc.n

    suse = 0
    xspc = 0
    wtile_th1_first = [True]
    for th in range(c.NTH):
        for kd in range(c.KD):
            sx = kd % 2
            extra = (st0_finals + [(s_mm1, mm1_at_th_end[th - 1] if th > 0 else 0),
                                   (s_wfr, wfr_at_th_end[th - 1] if th > 0 else 0),
                                   (s_splc, spre_done)]
                     if kd == 0 else [(s_xsp, max(0, s_xsp.n - 1))])
            DMA("gpsimd", xtmp[sx], xnf_s[kd * 128:(kd + 1) * 128, th * c.CHT:(th + 1) * c.CHT],
                ldxn[sx], *extra)

            def f_xh(e, sx=sx, kd=kd, w=W((ldxn[sx], ldxn[sx].n), (s_xsp, max(0, s_xsp.n - 1)))):
                w(e)
                e.activation(xnfh[:, kd * c.CHT:(kd + 1) * c.CHT], xtmp[sx],
                             AF.Identity).then_inc(s_sphi.h, 1)
            em("scalar", f_xh)
            s_sphi.n += 1

            def f_xl(e, sx=sx, kd=kd, w=W((s_sphi, s_sphi.n),
                                          (s_xsp, max(0, s_xsp.n - 1)))):
                w(e)
                e.scalar_tensor_tensor(xlo[sx], xnfh[:, kd * c.CHT:(kd + 1) * c.CHT],
                                       -1.0, xtmp[sx], OP.mult, OP.add).then_inc(s_splo.h, 1)
            em("vector", f_xl)
            s_splo.n += 1

            def f_xc(e, sx=sx, kd=kd, w=W((s_splo, s_splo.n))):
                w(e)
                e.copy(xnfl[:, kd * c.CHT:(kd + 1) * c.CHT], xlo[sx]).then_inc(s_xsp.h, 1)
            em("scalar", f_xc)
            s_xsp.n += 1
        ldxn_here = s_xsp.n

        for ci, (kind, ec) in enumerate(chunks):
            cgi = th * NCHK + ci
            par = (cgi % 2) * 4
            ev_guard = (cgi - 2 + 1) * TPT if cgi >= 2 else 0

            if kind in ("qkv", "base"):
                src0 = wqkvT if kind == "qkv" else basewT
                srch = wqkvT_h if kind == "qkv" else basewT_h
                srcl = wqkvT_l if kind == "qkv" else basewT_l
                for kd in range(c.KD):
                    sl_w = wtile % 6
                    if th == 0:
                        sf = wtile % 2
                        DMA("sync", wfp[sf],
                            src0[kd * 128:(kd + 1) * 128, ec * 512:ec * 512 + 512],
                            spl[sf], (s_splo, max(0, s_splo.n - 1)))

                        def f_ws(e, sf=sf, sl_w=sl_w,
                                 w=W((spl[sf], spl[sf].n), (s_wfr, max(0, wtile - 5)),
                                     (sps[sl_w % 4], sps[sl_w % 4].n))):
                            w(e)
                            e.tensor_copy(wsbh[sl_w], wfp[sf])
                            e.drain()
                            e.scalar_tensor_tensor(wlo_t, wsbh[sl_w], -1.0, wfp[sf],
                                                   OP.mult, OP.add)
                            e.drain()
                            e.tensor_copy(wsbl[sl_w], wlo_t).then_inc(s_splo.h, 1)
                        em("vector", f_ws)
                        s_splo.n += 1
                        DMA("gpsimd",
                            srch[kd * 128:(kd + 1) * 128, ec * 512:ec * 512 + 512],
                            wsbh[sl_w], sps[sl_w % 4], (s_splo, s_splo.n))
                        DMA("gpsimd",
                            srcl[kd * 128:(kd + 1) * 128, ec * 512:ec * 512 + 512],
                            wsbl[sl_w], sps[sl_w % 4], (s_splo, 0))
                        wt_w = ((s_splo, s_splo.n), (ldw[sl_w], 0))
                    else:
                        extra_w = [(x, x.n) for x in sps] if wtile_th1_first[0] else []
                        wtile_th1_first[0] = False
                        DMA("sync", wsbh[sl_w],
                            srch[kd * 128:(kd + 1) * 128, ec * 512:ec * 512 + 512],
                            ldw[sl_w], (s_wfr, max(0, wtile - 5)), *extra_w)
                        DMA("sync", wsbl[sl_w],
                            srcl[kd * 128:(kd + 1) * 128, ec * 512:ec * 512 + 512],
                            ldwl[sl_w])
                        wt_w = ((ldw[sl_w], ldw[sl_w].n), (ldwl[sl_w], ldwl[sl_w].n))
                    wtile += 1

                    def f_mm(e, kd=kd, par=par, sl_w=sl_w, kind=kind,
                             w=W(*wt_w,
                                 (s_xsp, ldxn_here if (kd == 0 and ci == 0) else 0),
                                 (s_ev1, ev_guard if kd == 0 else 0))):
                        w(e)
                        stop = (kd == c.KD - 1) and kind == "qkv"
                        for tt in range(TPT):
                            xh = xnfh[:, kd * c.CHT + tt * 128:kd * c.CHT + tt * 128 + 128]
                            xl = xnfl[:, kd * c.CHT + tt * 128:kd * c.CHT + tt * 128 + 128]
                            e.matmul(bank(par + tt), xh, wsbh[sl_w],
                                     start=(kd == 0 and tt == 0) if False else (kd == 0),
                                     stop=False)
                            e.matmul(bank(par + tt), xl, wsbh[sl_w], start=False, stop=False)
                            ins = e.matmul(bank(par + tt), xh, wsbl[sl_w], start=False,
                                           stop=stop)
                        ins.then_inc(s_wfr.h, 1)
                    em("tensor", f_mm)
                    s_wfr.n += 1

                if kind == "base":
                    def f_bias(e, par=par, ec=ec):
                        for tt in range(TPT):
                            ins = e.matmul(bank(par + tt), onest[0:1, 0:128],
                                           basebt[0:1, ec * 512:ec * 512 + 512],
                                           start=False, stop=True)
                        ins.then_inc(s_mm1.h, 1)
                    em("tensor", f_bias)
                    s_mm1.n += 1
                done_w = (s_mm1, s_mm1.n) if kind == "base" else (s_wfr, s_wfr.n)

                for tt in range(TPT):
                    sl1 = suse % NSTG

                    def f_ev1(e, par=par, tt=tt, kind=kind, sl1=sl1,
                              w=W(done_w, (st1[sl1], st1[sl1].n))):
                        w(e)
                        if kind == "qkv":
                            e.tensor_scalar(stg1[sl1], bank(par + tt), CLIP, -CLIP,
                                            OP.min, OP.max).then_inc(s_ev1.h, 1)
                        else:
                            e.tensor_copy(stg1[sl1], bank(par + tt)).then_inc(s_ev1.h, 1)
                    em("vector", f_ev1)
                    s_ev1.n += 1

                    trow = th * c.CHT + tt * 128
                    if kind == "qkv":
                        if ec < c.NDC:
                            DMA("gpsimd", qs_s[trow:trow + 128, ec * 512:(ec + 1) * 512],
                                stg1[sl1], st1[sl1], (s_ev1, s_ev1.n))
                        elif ec < 2 * c.NDC:
                            DMA("gpsimd",
                                ks_s[trow:trow + 128, (ec - c.NDC) * 512:(ec - c.NDC + 1) * 512],
                                stg1[sl1], st1[sl1], (s_ev1, s_ev1.n))
                        else:
                            for hb in range(4):
                                h = (ec - 2 * c.NDC) * 4 + hb
                                vrow = ((h // HPN) * HPN + (h % HPN)) * c.TOK + trow
                                DMA("gpsimd", a2av_i[vrow:vrow + 128, :],
                                    stg1[sl1][:, hb * 128:(hb + 1) * 128], st1[sl1],
                                    (s_ev1, s_ev1.n) if hb == 0 else (s_ev1, 0))
                    else:
                        DMA("gpsimd", base_s[trow:trow + 128, ec * 512:(ec + 1) * 512],
                            stg1[sl1], st1[sl1], (s_ev1, s_ev1.n))
                    suse += 1

            elif kind == "arh":
                def f_arh(e, par=par,
                          w=W((s_splc, spre_done), (s_xsp, ldxn_here), (s_ev1, ev_guard))):
                    w(e)
                    for kd in range(c.KD):
                        ah = atch[:, kd * c.NR:(kd + 1) * c.NR]
                        al = atcl[:, kd * c.NR:(kd + 1) * c.NR]
                        xh = xnfh[:, kd * c.CHT:(kd + 1) * c.CHT]
                        xl = xnfl[:, kd * c.CHT:(kd + 1) * c.CHT]
                        e.matmul(bank(par)[0:c.NR, 0:c.CHT], ah, xh,
                                 start=(kd == 0), stop=False)
                        e.matmul(bank(par)[0:c.NR, 0:c.CHT], al, xh, start=False, stop=False)
                        ins = e.matmul(bank(par)[0:c.NR, 0:c.CHT], ah, xl, start=False,
                                       stop=(kd == c.KD - 1))
                    ins.then_inc(s_mm1.h, 1)
                em("tensor", f_arh)
                s_mm1.n += 1

                def f_evarh(e, th=th, par=par, w=W((s_mm1, s_mm1.n))):
                    w(e)
                    e.tensor_copy(rh2[0:c.NR, th * c.CHT:(th + 1) * c.CHT],
                                  bank(par)[0:c.NR, 0:c.CHT]).then_inc(s_ev1.h, TPT)
                em("vector", f_evarh)
                s_ev1.n += TPT
                suse += TPT

            else:
                def f_rtr(e, par=par,
                          w=W((s_splc, spre_done), (s_xsp, ldxn_here), (s_ev1, ev_guard))):
                    w(e)
                    for tt in range(TPT):
                        for kd in range(c.KD):
                            xh = xnfh[:, kd * c.CHT + tt * 128:kd * c.CHT + tt * 128 + 128]
                            xl = xnfl[:, kd * c.CHT + tt * 128:kd * c.CHT + tt * 128 + 128]
                            rh_ = rtch[:, kd * c.NE:(kd + 1) * c.NE]
                            rl_ = rtcl[:, kd * c.NE:(kd + 1) * c.NE]
                            e.matmul(bank(par + tt)[:, 0:c.NE], xh, rh_,
                                     start=(kd == 0), stop=False)
                            e.matmul(bank(par + tt)[:, 0:c.NE], xl, rh_,
                                     start=False, stop=False)
                            ins = e.matmul(bank(par + tt)[:, 0:c.NE], xh, rl_,
                                           start=False, stop=(kd == c.KD - 1))
                    ins.then_inc(s_mm1.h, 1)
                em("tensor", f_rtr)
                s_mm1.n += 1

                def f_evrtr(e, th=th, par=par, w=W((s_mm1, s_mm1.n))):
                    w(e)
                    for tt in range(TPT):
                        g = th * TPT + tt
                        ins = e.tensor_copy(logits[:, g * c.NE:(g + 1) * c.NE],
                                            bank(par + tt)[:, 0:c.NE])
                    ins.then_inc(s_ev1.h, TPT)
                em("vector", f_evrtr)
                s_ev1.n += TPT
                suse += TPT
        mm1_at_th_end.append(s_mm1.n)
        wfr_at_th_end.append(s_wfr.n)
    quiesce("sync", [s_ldpre] + ldw + ldwl + spl)
    quiesce("gpsimd", ldxn + st1 + sps)
    run_phase()
    amark[0] = am1
    st1_finals = [(s, s.n) for s in st1]

    am2 = amark[0]
    qa = [alloc(c.D), alloc(c.D)]
    ka = [alloc(c.D), alloc(c.D)]
    qn = qa
    kn = ka
    sq2 = alloc(c.D)
    stg2 = [alloc(128) for _ in range(8)]
    suse2 = 0

    def f_agv(e, w=W(*st1_finals)):
        w(e)
        from concourse.aot_env import allow_ici_all_to_all
        with allow_ici_all_to_all():
            e.collective_compute("AllToAll", OP.bypass,
                                 replica_groups=[list(range(c.NC))],
                                 ins=[a2av_i.opt()],
                                 outs=[a2av_o.opt()]).then_inc(s_cc.h, 1)
    em("gpsimd", f_agv)
    s_cc.n += 1

    for i in range(c.NT):
        sl = i % 2
        extra = (st1_finals if i == 0 else []) + [(s_qn, max(0, s_qn.n - 2)),
                                                   (s_tr2, max(0, (i - 1) * 2 * c.H))]
        DMA("gpsimd", qa[sl], qs_s[i * 128:(i + 1) * 128, :], lqk[sl], *extra)
        DMA("gpsimd", ka[sl], ks_s[i * 128:(i + 1) * 128, :], lqk[sl])
        for which, buf, nbuf, gw, gb in (("q", qa, qn, gqs, gqb), ("k", ka, kn, gks, gkb)):
            o = 0 if which == "q" else 16

            def f_sc(e, sl=sl, buf=buf, o=o, w=W((lqk[sl], lqk[sl].n))):
                w(e)
                e.activation(sq2, buf[sl], AF.Square,
                             accum_out=smx(o + 2 + sl)).then_inc(s_sq2.h, 1)
            em("scalar", f_sc)
            s_sq2.n += 1

            def f_v1(e, sl=sl, buf=buf, o=o, w=W((lqk[sl], lqk[sl].n), (s_sq2, s_sq2.n))):
                w(e)
                e.reduce_sum(smx(o + 0 + sl), buf[sl], axis=AX)
                e.drain()
                e.tensor_scalar(smx(o + 4 + sl), smx(o + 0 + sl),
                                1.0 / c.D, None, OP.mult)
                e.tensor_scalar(smx(o + 6 + sl), smx(o + 2 + sl),
                                1.0 / c.D, None, OP.mult)
                e.drain()
                e.tensor_tensor(smx(o + 8 + sl), smx(o + 4 + sl),
                                smx(o + 4 + sl), OP.mult)
                e.drain()
                e.tensor_tensor(smx(o + 6 + sl), smx(o + 6 + sl),
                                smx(o + 8 + sl), OP.subtract).then_inc(s_vst2.h, 1)
            em("vector", f_v1)
            s_vst2.n += 1

            def f_sc2(e, sl=sl, o=o, w=W((s_vst2, s_vst2.n))):
                w(e)
                e.activation(smx(o + 10 + sl), smx(o + 6 + sl),
                             AF.Sqrt, bias=epst[:, 0:1]).then_inc(s_sct2.h, 1)
            em("scalar", f_sc2)
            s_sct2.n += 1

            def f_v2(e, sl=sl, buf=buf, nbuf=nbuf, o=o,
                     w=W((s_sct2, s_sct2.n), (s_tr2, max(0, (i - 1) * 2 * c.H)))):
                w(e)
                e.reciprocal(smx(o + 12 + sl), smx(o + 10 + sl))
                e.drain()
                e.tensor_scalar(nbuf[sl], buf[sl], smx(o + 4 + sl),
                                smx(o + 12 + sl),
                                OP.subtract, OP.mult).then_inc(s_qn.h, 1)
            em("vector", f_v2)
            s_qn.n += 1

            for h in range(c.H):
                bnk = (h % 4) + (0 if sl == 0 else 4)

                def f_tr(e, sl=sl, nbuf=nbuf, h=h, bnk=bnk,
                         w=W((s_qn, s_qn.n), (s_ev2, max(0, s_ev2.n - 3)))):
                    w(e)
                    e.transpose(bank(bnk)[:, 0:128], nbuf[sl][:, h * 128:(h + 1) * 128],
                                identt).then_inc(s_tr2.h, 1)
                em("tensor", f_tr)
                s_tr2.n += 1

                s2 = suse2 % 8

                def f_ev(e, h=h, bnk=bnk, gw=gw, gb=gb, s2=s2,
                         w=W((s_tr2, s_tr2.n), (st2[s2], st2[s2].n))):
                    w(e)
                    e.tensor_scalar(stg2[s2], bank(bnk)[:, 0:128],
                                    gw[:, h:h + 1], gb[:, h:h + 1],
                                    OP.mult, OP.add).then_inc(s_ev2.h, 1)
                em("vector", f_ev)
                s_ev2.n += 1
                roff = ((h // HPN) * QKR + (0 if which == "q" else HPN * 128)
                        + (h % HPN) * 128)
                DMA("gpsimd", a2aqk_i[roff:roff + 128, i * 128:(i + 1) * 128],
                    stg2[s2], st2[s2], (s_ev2, s_ev2.n))
                suse2 += 1

    st2_finals = [(s, s.n) for s in st2]

    def f_ag(e, w=W(*st2_finals)):
        w(e)
        from concourse.aot_env import allow_ici_all_to_all
        with allow_ici_all_to_all():
            e.collective_compute("AllToAll", OP.bypass,
                                 replica_groups=[list(range(c.NC))],
                                 ins=[a2aqk_i.opt()],
                                 outs=[a2aqk_o.opt()]).then_inc(s_cc.h, 1)
    em("gpsimd", f_ag)
    s_cc.n += 1
    cc_qkv = s_cc.n
    quiesce("gpsimd", lqk + st2)
    run_phase()
    amark[0] = am2

    csb = alloc(c.NERC * c.TOK)
    csbh = alloc(c.NERC * c.TOK // 2)[:, :].bitcast(BF16)
    csbl = alloc(c.NERC * c.TOK // 2)[:, :].bitcast(BF16)
    clo = alloc(c.CHT)
    wsb3h = [alloc(256)[:, :].bitcast(BF16) for _ in range(8)]
    wsb3l = [alloc(256)[:, :].bitcast(BF16) for _ in range(8)]
    bstg = [alloc(512) for _ in range(4)]
    t1st = [alloc(512) for _ in range(4)]
    gstg = [alloc(512) for _ in range(4)]
    bst4 = [bst[0], bst[1], lqk[0], lqk[1]]
    st34 = [st3[0], st3[1], st2[0], st2[1]]
    prbf = alloc(64)
    prb2f = alloc(64)
    prb = prbf[:, 0:c.NE]
    prb2 = prb2f[:, 0:c.NE]

    for g in range(c.NT):
        lg = logits[:, g * c.NE:(g + 1) * c.NE]

        def f_v(e, lg=lg, w=W((s_ffs, s_ffs.n), (s_trc, s_trc.n))):
            w(e)
            e.reduce_max(smx(0), lg, axis=AX)
            e.drain()
            e.tensor_scalar(smx(1), smx(0), -1.0, None, OP.mult).then_inc(s_ffv.h, 1)
        em("vector", f_v)
        s_ffv.n += 1

        def f_s(e, lg=lg, w=W((s_ffv, s_ffv.n))):
            w(e)
            e.activation(prb, lg, AF.Exp, bias=smx(1),
                         accum_out=smx(2)).then_inc(s_ffs.h, 1)
        em("scalar", f_s)
        s_ffs.n += 1

        def f_v2(e, w=W((s_ffs, s_ffs.n))):
            w(e)
            e.reciprocal(smx(3), smx(2))
            e.drain()
            e.tensor_scalar(prb, prb, smx(3), None, OP.mult)
            e.drain()
            e.reduce_max(smx(4), prb, axis=AX)
            e.drain()
            e.tensor_scalar(prb2, prb, smx(4), None, OP.is_ge)
            e.drain()
            e.scalar_tensor_tensor(prb2, prb2, -2.0, prb, OP.mult, OP.add)
            e.drain()
            e.reduce_max(smx(5), prb2, axis=AX)
            e.drain()
            e.tensor_tensor(smx(6), smx(4), smx(5), OP.add)
            e.drain()
            e.tensor_scalar(smx(6), smx(6), 1e-6, None, OP.add)
            e.drain()
            e.reciprocal(smx(7), smx(6))
            e.drain()
            e.tensor_tensor(smx(8), smx(4), smx(7), OP.mult)
            e.tensor_tensor(smx(9), smx(5), smx(7), OP.mult)
            e.tensor_scalar(prb2, prb2, smx(5), None, OP.is_ge)
            e.tensor_scalar(prb, prb, smx(4), None, OP.is_ge)
            e.drain()
            e.tensor_scalar(prb, prb, smx(8), None, OP.mult)
            e.drain()
            e.scalar_tensor_tensor(prb, prb2, smx(9), prb, OP.mult,
                                   OP.add).then_inc(s_ffv.h, 1)
        em("vector", f_v2)
        s_ffv.n += 1

        def f_tr(e, g=g, w=W((s_ffv, s_ffv.n), (s_cbv, max(0, s_cbv.n - 1)))):
            w(e)
            e.transpose(bank(g % 2)[0:c.NE, 0:128], prb, identt).then_inc(s_trc.h, 1)
        em("tensor", f_tr)
        s_trc.n += 1

        def f_ev(e, g=g, w=W((s_trc, s_trc.n))):
            w(e)
            e.tensor_copy(combf[0:c.NE, g * 128:(g + 1) * 128],
                          bank(g % 2)[0:c.NE, 0:128]).then_inc(s_cbv.h, 1)
        em("vector", f_ev)
        s_cbv.n += 1

    def f_dup(e, w=W((s_ev1, s_ev1.n))):
        w(e)
        e.dma_start(out=rh2[c.NR:2 * c.NR, :], in_=rh2[0:c.NR, :]).then_inc(s_dup.h, 16)
    em("gpsimd", f_dup)
    s_dup.n += 16

    cbv_base = s_cbv.n
    csp_base = s_csp.n
    cb = 0
    for ecc in range(c.NERC):
        for tch in range(c.NTH):
            def f_cb(e, ecc=ecc, tch=tch, cb=cb,
                     w=W((s_cbv, cbv_base + max(0, cb - 1)), (s_dup, 16))):
                w(e)
                e.matmul(bank(2 + cb % 2)[:, 0:c.CHT],
                         selt[0:c.NE, ecc * 128:(ecc + 1) * 128],
                         combf[0:c.NE, tch * c.CHT:(tch + 1) * c.CHT],
                         start=True, stop=True).then_inc(s_cbm.h, 1)
            em("tensor", f_cb)
            s_cbm.n += 1

            def f_cv(e, ecc=ecc, tch=tch, cb=cb, w=W((s_cbm, s_cbm.n))):
                w(e)
                e.tensor_tensor(
                    csb[:, ecc * c.TOK + tch * c.CHT:ecc * c.TOK + (tch + 1) * c.CHT],
                    rh2[:, tch * c.CHT:(tch + 1) * c.CHT],
                    bank(2 + cb % 2)[:, 0:c.CHT], OP.mult).then_inc(s_cbv.h, 1)
            em("vector", f_cv)
            s_cbv.n += 1

            def f_ch(e, ecc=ecc, tch=tch, w=W((s_cbv, s_cbv.n), (s_csp, max(0, s_csp.n - 1)))):
                w(e)
                e.activation(csbh[:, ecc * c.TOK + tch * c.CHT:ecc * c.TOK + (tch + 1) * c.CHT],
                             csb[:, ecc * c.TOK + tch * c.CHT:ecc * c.TOK + (tch + 1) * c.CHT],
                             AF.Identity).then_inc(s_sphi.h, 1)
            em("scalar", f_ch)
            s_sphi.n += 1

            def f_cl(e, ecc=ecc, tch=tch, w=W((s_sphi, s_sphi.n))):
                w(e)
                e.scalar_tensor_tensor(
                    clo, csbh[:, ecc * c.TOK + tch * c.CHT:ecc * c.TOK + (tch + 1) * c.CHT],
                    -1.0, csb[:, ecc * c.TOK + tch * c.CHT:ecc * c.TOK + (tch + 1) * c.CHT],
                    OP.mult, OP.add).then_inc(s_splo.h, 1)
            em("vector", f_cl)
            s_splo.n += 1

            def f_cc2(e, ecc=ecc, tch=tch, w=W((s_splo, s_splo.n))):
                w(e)
                e.copy(csbl[:, ecc * c.TOK + tch * c.CHT:ecc * c.TOK + (tch + 1) * c.CHT],
                       clo).then_inc(s_csp.h, 1)
            em("scalar", f_cc2)
            s_csp.n += 1
            cb += 1
    csp_done = s_csp.n

    btile = 0
    bfp = [alloc(512), alloc(512)]
    blo_t = alloc(512)
    for oc in range(c.NBC):
        btw = []
        for ecc in range(c.NERC):
            sf = btile % 2
            DMA("sync", bfp[sf], bflat[ecc * 128:(ecc + 1) * 128, oc * 512:(oc + 1) * 512],
                spl[sf], (s_splo, max(0, s_splo.n - 1)))

            def f_bs(e, sf=sf, ecc=ecc,
                     w=W((spl[sf], spl[sf].n),
                         (s_bfr, (oc - 1) * 2 * c.NERC + c.NERC + ecc + 1
                          if oc > 0 else 0))):
                w(e)
                e.tensor_copy(wsb3h[ecc], bfp[sf])
                e.drain()
                e.scalar_tensor_tensor(blo_t, wsb3h[ecc], -1.0, bfp[sf], OP.mult, OP.add)
                e.drain()
                e.tensor_copy(wsb3l[ecc], blo_t).then_inc(s_splo.h, 1)
            em("vector", f_bs)
            s_splo.n += 1
            btw.append((s_splo, s_splo.n))
            btile += 1

        HNT = c.NT // 2
        for half in range(2):
            for ecc in range(c.NERC):
                def f_dm(e, ecc=ecc, oc=oc, half=half, HNT=HNT,
                         w=W(btw[ecc],
                             (s_csp, csp_base + (ecc + 1) * c.NTH if oc == 0 else 0),
                             (s_dev, (oc - 1) * c.NT + (half + 1) * HNT
                              if (ecc == 0 and oc > 0) else 0))):
                    w(e)
                    for tt in range(half * HNT, (half + 1) * HNT):
                        chs = csbh[:, ecc * c.TOK + tt * 128:ecc * c.TOK + (tt + 1) * 128]
                        cls = csbl[:, ecc * c.TOK + tt * 128:ecc * c.TOK + (tt + 1) * 128]
                        e.matmul(bank(tt), chs, wsb3h[ecc], start=(ecc == 0), stop=False)
                        e.matmul(bank(tt), cls, wsb3h[ecc], start=False, stop=False)
                        ins = e.matmul(bank(tt), chs, wsb3l[ecc], start=False,
                                       stop=(ecc == c.NERC - 1))
                    ins.then_inc(s_bfr.h, 1)
                em("tensor", f_dm)
                s_bfr.n += 1

            for tt in range(half * HNT, (half + 1) * HNT):
                dl = (oc * c.NT + tt) % 4
                extra = st1_finals if (oc == 0 and half == 0 and tt == 0) else []
                DMA("gpsimd", bstg[dl], base_s[tt * 128:(tt + 1) * 128, oc * 512:(oc + 1) * 512],
                    bst4[dl], (s_dev, max(0, s_dev.n - 3)), *extra)

                def f_dv(e, tt=tt, dl=dl, w=W((s_bfr, s_bfr.n), (bst4[dl], bst4[dl].n),
                                              (s_sil, max(0, s_sil.n - 3)))):
                    w(e)
                    e.scalar_tensor_tensor(t1st[dl], bank(tt), 0.5, bstg[dl],
                                           OP.mult, OP.add).then_inc(s_dev.h, 1)
                em("vector", f_dv)
                s_dev.n += 1

                def f_sil(e, dl=dl, w=W((s_dev, s_dev.n), (st34[dl], st34[dl].n))):
                    w(e)
                    e.activation(gstg[dl], t1st[dl], AF.Silu).then_inc(s_sil.h, 1)
                em("scalar", f_sil)
                s_sil.n += 1
                DMA("gpsimd", g_s[tt * 128:(tt + 1) * 128, oc * 512:(oc + 1) * 512],
                    gstg[dl], st34[dl], (s_sil, s_sil.n))

    quiesce("sync", ldb)
    quiesce("gpsimd", bst4 + st34 + [s_dup])
    run_phase()

    amark[0] = PERSIST0
    kf2 = [alloc(c.S), alloc(c.S)]
    vt2 = [alloc(c.S), alloc(c.S)]
    scb = [alloc(c.S), alloc(c.S)]
    pT = alloc(c.KB * 256)
    pT3 = pT.rearrange("p (b w) -> p b w", w=256)
    slopek = alloc(c.S)
    negqs = alloc(c.KB)
    qtl = [alloc(128), alloc(128)]
    astg = [alloc(256), alloc(256)]
    pmaxb = alloc(32)
    NTASK = c.B * HPN
    NTB = c.TOK // 128
    NCH = c.KB // 2
    scg = [0]
    sev_bank = [0, 0]
    tb_ctr = [0]
    TBANKS = (2, 3, 6, 7)
    ptv_bank = [0, 0, 0, 0]
    last_smm = {0: 0, 1: 0}
    ptm_scb = {0: 0, 1: 0}
    e1_after = {0: 0, 1: 0}
    rs_after = {0: 0, 1: 0}
    e2_after = {0: 0, 1: 0}
    e2b_after = {0: 0, 1: 0}
    aev_after = {0: 0, 1: 0}
    kfree = {0: 0, 1: 0}
    vfree = {0: 0, 1: 0}
    pv_chunk = [0]

    def emit_pt(t, m, ql, e2n):
        qt = 2 * m + ql
        qp = qt % 2
        P = qt + 1
        e2a_n, e2b_n, PB = e2n
        NG = (P + 3) // 4
        seen_a = [False]
        seen_b = [False]
        for gk in range(NG):
            nb = min(4, P - gk * 4)
            tb = tb_ctr[0] % 4
            tbk = TBANKS[tb]
            tb_ctr[0] += 1
            wa = wb = 0
            if gk * 4 < PB and not seen_a[0]:
                wa = e2a_n
                seen_a[0] = True
            if gk * 4 + nb > PB and not seen_b[0]:
                wb = e2b_n
                seen_b[0] = True

            def f_pt(e, qp=qp, gk=gk, nb=nb, tbk=tbk, tb=tb, wa=wa, wb=wb,
                     w=W((s_e2, wa), (s_cbm, wb),
                         (s_ptv, ptv_bank[tb]))):
                w(e)
                for q4 in range(nb):
                    kb = gk * 4 + q4
                    ins = e.transpose(bank(tbk)[:, q4 * 128:(q4 + 1) * 128],
                                      scb[qp][:, kb * 128:(kb + 1) * 128], identt)
                ins.then_inc(s_ptm.h, 1)
            em("tensor", f_pt)
            s_ptm.n += 1
            last_g = (ql == 1 and gk == NG - 1)

            def f_ptv(e, ql=ql, gk=gk, nb=nb, tbk=tbk, m=m, last_g=last_g,
                      w=W((s_ptm, s_ptm.n),
                          (s_pv, pv_chunk[0] if (ql == 0 and gk == 0) else 0))):
                w(e)
                ins = e.tensor_copy(
                    pT3[:, gk * 4:gk * 4 + nb, ql * 128:ql * 128 + 128],
                    bank(tbk)[:, 0:nb * 128].rearrange("p (b w) -> p b w",
                                                       w=128))
                if last_g:
                    ins = e.memset(pT[:, (2 * m + 1) * 256:(2 * m + 1) * 256
                                      + 128], 0.0)
                ins.then_inc(s_ptv.h, 1)
            em("vector", f_ptv)
            s_ptv.n += 1
            ptv_bank[tb] = s_ptv.n
        ptm_scb[qp] = s_ptm.n

    def emit_pvtail(t, tp, m, cp, kvw=0):
        NKB = 2 * m + 2

        def f_pv(e, tp=tp, cp=cp, NKB=NKB,
                 w=W((s_ptv, s_ptv.n), (s_aev, aev_after[cp]), (s_kv, kvw))):
            w(e)
            for kb in range(NKB):
                ins = e.matmul(bank(4 + cp)[:, 0:256],
                               vt2[tp][:, kb * 128:(kb + 1) * 128],
                               pT[:, kb * 256:(kb + 1) * 256],
                               start=(kb == 0), stop=(kb == NKB - 1))
            ins.then_inc(s_pv.h, 1)
        em("tensor", f_pv)
        s_pv.n += 1
        pv_chunk[0] = s_pv.n

        def f_aev(e, cp=cp, w=W((s_pv, s_pv.n), (ast[cp], ast[cp].n))):
            w(e)
            e.tensor_copy(astg[cp], bank(4 + cp)[:, 0:256]).then_inc(s_aev.h, 1)
        em("vector", f_aev)
        s_aev.n += 1
        aev_after[cp] = s_aev.n
        rpq = (2 * m * 128) // c.TOK
        qc2 = 2 * m * 128 - rpq * c.TOK
        aor = ((t // HPN) * c.R4 + rpq) * HPN * 128 + (t % HPN) * 128
        DMA("gpsimd", a2aao_i[aor:aor + 128, qc2:qc2 + 256], astg[cp],
            ast[cp], (s_aev, s_aev.n))

    sev_task = 0
    for t in range(NTASK):
        tp = t % 2
        hd = t % HPN
        bsel = t // HPN
        first = True
        for rp in range(c.R4):
            snd = bsel * c.R4 + rp
            gw_ = (((s_smm, kfree[tp]), (s_cc, cc_qkv)) if first else ())
            first = False
            krow = snd * QKR + HPN * 128 + hd * 128
            DMA("sync", kf2[tp][:, rp * c.TOK:(rp + 1) * c.TOK],
                a2aqk_o[krow:krow + 128, :], s_ini, *gw_)
        k4 = s_ini.n
        vw_ = ((s_pv, vfree[tp]),)
        for rp in range(c.R4):
            snd = bsel * c.R4 + rp
            for lb in range(NTB):
                vrow = snd * HPN * c.TOK + hd * c.TOK + lb * 128
                vcol = rp * c.TOK + lb * 128
                DMA("sync", vt2[tp][:, vcol:vcol + 128],
                    a2av_o[vrow:vrow + 128, :], s_kv, *vw_)
                vw_ = ()
        kv_here = s_kv.n

        DMA("sync", slopek, kpos_d, s_dbs, (s_sev, sev_task))

        def f_nqs(e, t=t, w=W((s_dbs, s_dbs.n))):
            w(e)
            e.tensor_scalar(slopek, slopek, slvt[:, t:t + 1], None, OP.mult)
            e.tensor_scalar(negqs, negqc, slvt[:, t:t + 1], None,
                            OP.mult).then_inc(s_dbv.h, 1)
        em("vector", f_nqs)
        s_dbv.n += 1
        dbv_task = s_dbv.n

        def emit_qt(m, ql, kv_first, t=t, tp=tp, hd=hd, bsel=bsel,
                    dbv_task=dbv_task, kv_here=k4):
            qt = 2 * m + ql
            qp = qt % 2
            PW = (qt + 1) * 128
            SCq = (PW + 511) // 512
            rp = (qt * 128) // c.TOK
            qcol = qt * 128 - rp * c.TOK
            qrow = (bsel * c.R4 + rp) * QKR + hd * 128
            DMA("sync", qtl[qp], a2aqk_o[qrow:qrow + 128, qcol:qcol + 128],
                qfl[qp], (s_smm, last_smm[qp]))
            dstart = qt * 128
            for sc in range(SCq):
                c0 = sc * 512
                wdt = min(512, PW - c0)
                bp = scg[0] % 2
                scg[0] += 1

                def f_smm(e, qp=qp, sc=sc, wdt=wdt, bp=bp, tp=tp,
                          kvw=(kv_here if (kv_first and sc == 0) else 0),
                          w=W((qfl[qp], qfl[qp].n if sc == 0 else 0),
                              (s_sev, sev_bank[bp]))):
                    if kvw:
                        e.wait_ge(s_ini.h, kvw)
                    w(e)
                    e.matmul(bank(bp)[:, 0:wdt], qtl[qp],
                             kf2[tp][:, sc * 512:sc * 512 + wdt],
                             start=True, stop=True).then_inc(s_smm.h, 1)
                em("tensor", f_smm)
                s_smm.n += 1
                w1 = wdt if c0 + wdt <= dstart else max(0, dstart - c0)

                def f_sev(e, qt=qt, qp=qp, t=t, c0=c0, wdt=wdt, w1=w1, bp=bp,
                          w=W((s_smm, s_smm.n), (s_dbv, dbv_task),
                              (s_ptm, ptm_scb[qp] if sc == 0 else 0))):
                    w(e)
                    ins = None
                    if w1 > 0:
                        ins = e.scalar_tensor_tensor(
                            scb[qp][:, c0:c0 + w1], slopek[:, c0:c0 + w1],
                            negqs[:, qt:qt + 1], bank(bp)[:, 0:w1],
                            OP.add, OP.add)
                    if w1 < wdt:
                        ins = e.scalar_tensor_tensor(
                            scb[qp][:, c0 + w1:c0 + wdt], dgt,
                            slvt[:, t:t + 1], bank(bp)[:, w1:wdt],
                            OP.mult, OP.add)
                    ins.then_inc(s_sev.h, 1)
                em("vector", f_sev)
                s_sev.n += 1
                sev_bank[bp] = s_sev.n

                def f_pm(e, qp=qp, sc=sc, c0=c0, wdt=wdt,
                         lastsc=(sc == SCq - 1), w=W((s_sev, s_sev.n))):
                    w(e)
                    ins = e.reduce_max(pmaxb[:, qp * 16 + sc:qp * 16 + sc + 1],
                                       scb[qp][:, c0:c0 + wdt], axis=AX)
                    if lastsc:
                        ins.then_inc(s_mx.h, 1)
                em("vector", f_pm)
            s_mx.n += 1
            last_smm[qp] = s_smm.n

            def f_mxf(e, qp=qp, SCq=SCq, w=W((s_mx, s_mx.n),
                                             (s_e1, e1_after[qp]))):
                w(e)
                e.reduce_max(smx(20 + qp), pmaxb[:, qp * 16:qp * 16 + SCq],
                             axis=AX).then_inc(s_mx.h, 1)
            em("vector", f_mxf)
            s_mx.n += 1

            def f_e1(e, qp=qp, PW=PW, w=W((s_mx, s_mx.n),
                                          (s_rs, rs_after[qp]))):
                w(e)
                e.activation(smx(22 + qp), smx(20 + qp), AF.Identity,
                             scale=-1.0)
                e.drain()
                e.activation(scb[qp][:, 0:PW], scb[qp][:, 0:PW], AF.Exp,
                             bias=smx(22 + qp),
                             accum_out=smx(24 + qp)).then_inc(s_e1.h, 1)
            em("scalar", f_e1)
            s_e1.n += 1
            e1_after[qp] = s_e1.n

            def f_rs(e, qp=qp, w=W((s_e1, s_e1.n), (s_e2, e2_after[qp]))):
                w(e)
                e.reciprocal(smx(26 + qp), smx(24 + qp)).then_inc(s_rs.h, 1)
            em("vector", f_rs)
            s_rs.n += 1
            rs_after[qp] = s_rs.n

            def f_e2(e, qp=qp, PW=PW, w=W((s_rs, s_rs.n))):
                w(e)
                e.activation(scb[qp][:, 0:PW], scb[qp][:, 0:PW], AF.Identity,
                             scale=smx(26 + qp)).then_inc(s_e2.h, 1)
            em("scalar", f_e2)
            s_e2.n += 1
            e2_after[qp] = s_e2.n
            return (s_e2.n, 0, qt + 1)

        pend = None
        for m in range(NCH):
            e2q0 = emit_qt(m, 0, kv_first=(m == 0))
            if pend is not None:
                emit_pt(t, pend[0], 1, pend[1])
                emit_pvtail(t, tp, pend[0], pend[0] % 2,
                            kvw=(kv_here if pend[0] == 0 else 0))
            e2q1 = emit_qt(m, 1, kv_first=False)
            emit_pt(t, m, 0, e2q0)
            pend = (m, e2q1)
        emit_pt(t, pend[0], 1, pend[1])
        emit_pvtail(t, tp, pend[0], pend[0] % 2)
        kfree[tp] = s_smm.n
        vfree[tp] = s_pv.n
        sev_task = s_sev.n

    def f_ao(e, w=W((ast[0], ast[0].n), (ast[1], ast[1].n))):
        w(e)
        from concourse.aot_env import allow_ici_all_to_all
        with allow_ici_all_to_all():
            e.collective_compute("AllToAll", OP.bypass,
                                 replica_groups=[list(range(c.NC))],
                                 ins=[a2aao_i.opt()],
                                 outs=[a2aao_o.opt()]).then_inc(s_cc.h, 1)
    em("gpsimd", f_ao)
    s_cc.n += 1
    cc_ao = s_cc.n
    quiesce("sync", [s_kv, s_ini, s_dbs] + qfl)
    run_phase()
    amark[0] = PERSIST0

    atnh = alloc(c.KD * c.CHT // 2)[:, :].bitcast(BF16)
    atnl = alloc(c.KD * c.CHT // 2)[:, :].bitcast(BF16)
    atmp = [alloc(c.CHT), alloc(c.CHT)]
    alo = [alloc(c.CHT), alloc(c.CHT)]
    wsb5h = [alloc(256)[:, :].bitcast(BF16) for _ in range(6)]
    wsb5l = [alloc(256)[:, :].bitcast(BF16) for _ in range(6)]
    gld5 = [alloc(512), alloc(512)]
    xld5 = [alloc(512), alloc(512)]
    t1s = [alloc(512), alloc(512)]
    ost = [alloc(512), alloc(512)]
    ast_finals = [(s, s.n) for s in ast]
    st3_finals = [(s, s.n) for s in st3]
    TPT = c.TPT
    wfr5_after = []

    wfp5 = [alloc(512), alloc(512)]
    wlo5 = alloc(512)
    w5_first = [True]
    for th in range(c.NTH):
        asp_base = s_asp.n
        for fc in range(c.KD):
            sx = fc % 2
            extra = ([(s_wfr, wfr5_after[th * c.NBC - 1] if th > 0 else 0),
                      (s_cc, cc_ao)]
                     if fc == 0 else [(s_asp, max(0, s_asp.n - 1))])
            DMA("gpsimd", atmp[sx],
                a2aao_o[fc * 128:(fc + 1) * 128, th * c.CHT:(th + 1) * c.CHT],
                lat[sx], *extra)

            def f_ah(e, sx=sx, fc=fc, w=W((lat[sx], lat[sx].n), (s_asp, max(0, s_asp.n - 1)))):
                w(e)
                e.activation(atnh[:, fc * c.CHT:(fc + 1) * c.CHT], atmp[sx],
                             AF.Identity).then_inc(s_sphi.h, 1)
            em("scalar", f_ah)
            s_sphi.n += 1

            def f_al(e, sx=sx, fc=fc, w=W((s_sphi, s_sphi.n))):
                w(e)
                e.scalar_tensor_tensor(alo[sx], atnh[:, fc * c.CHT:(fc + 1) * c.CHT],
                                       -1.0, atmp[sx], OP.mult, OP.add).then_inc(s_splo.h, 1)
            em("vector", f_al)
            s_splo.n += 1

            def f_ac(e, sx=sx, fc=fc, w=W((s_splo, s_splo.n))):
                w(e)
                e.copy(atnl[:, fc * c.CHT:(fc + 1) * c.CHT], alo[sx]).then_inc(s_asp.h, 1)
            em("scalar", f_ac)
            s_asp.n += 1
        lat_here = s_asp.n
        for oc in range(c.NBC):
            cgi5 = th * c.NBC + oc
            par = (cgi5 % 2) * 4
            for fc in range(c.KD):
                sl_w = (cgi5 * c.KD + fc) % 6
                if th == 0:
                    sf = fc % 2
                    DMA("sync", wfp5[sf],
                        woutT[fc * 128:(fc + 1) * 128, oc * 512:(oc + 1) * 512],
                        spl[sf], (s_splo, max(0, s_splo.n - 1)))

                    def f_w5(e, sf=sf, sl_w=sl_w,
                             w=W((spl[sf], spl[sf].n), (s_wfr, max(0, s_wfr.n - 5)),
                                 (sps[sl_w % 4], sps[sl_w % 4].n))):
                        w(e)
                        e.tensor_copy(wsb5h[sl_w], wfp5[sf])
                        e.drain()
                        e.scalar_tensor_tensor(wlo5, wsb5h[sl_w], -1.0, wfp5[sf],
                                               OP.mult, OP.add)
                        e.drain()
                        e.tensor_copy(wsb5l[sl_w], wlo5).then_inc(s_splo.h, 1)
                    em("vector", f_w5)
                    s_splo.n += 1
                    DMA("gpsimd", woutT_h[fc * 128:(fc + 1) * 128, oc * 512:(oc + 1) * 512],
                        wsb5h[sl_w], sps[sl_w % 4], (s_splo, s_splo.n))
                    DMA("gpsimd", woutT_l[fc * 128:(fc + 1) * 128, oc * 512:(oc + 1) * 512],
                        wsb5l[sl_w], sps[sl_w % 4], (s_splo, 0))
                    w5_w = ((s_splo, s_splo.n),)
                else:
                    extra5 = [(x, x.n) for x in sps] if w5_first[0] else []
                    w5_first[0] = False
                    DMA("sync", wsb5h[sl_w],
                        woutT_h[fc * 128:(fc + 1) * 128, oc * 512:(oc + 1) * 512], ldw[sl_w],
                        (s_wfr, max(0, s_wfr.n - 5)), *extra5)
                    DMA("sync", wsb5l[sl_w],
                        woutT_l[fc * 128:(fc + 1) * 128, oc * 512:(oc + 1) * 512],
                        ldwl[sl_w])
                    w5_w = ((ldw[sl_w], ldw[sl_w].n), (ldwl[sl_w], ldwl[sl_w].n))

                def f_wm(e, fc=fc, par=par, sl_w=sl_w,
                         w=W(*w5_w,
                             (s_asp, asp_base + fc + 1 if oc == 0 else 0),
                             (s_wv1, (cgi5 - 2 + 1) * TPT if (cgi5 >= 2 and fc == 0) else 0))):
                    w(e)
                    for tt in range(TPT):
                        ah_ = atnh[:, fc * c.CHT + tt * 128:fc * c.CHT + tt * 128 + 128]
                        al_ = atnl[:, fc * c.CHT + tt * 128:fc * c.CHT + tt * 128 + 128]
                        e.matmul(bank(par + tt), ah_, wsb5h[sl_w], start=(fc == 0), stop=False)
                        e.matmul(bank(par + tt), al_, wsb5h[sl_w], start=False, stop=False)
                        ins = e.matmul(bank(par + tt), ah_, wsb5l[sl_w], start=False,
                                       stop=(fc == c.KD - 1))
                    ins.then_inc(s_wfr.h, 1)
                em("tensor", f_wm)
                s_wfr.n += 1
            wfr5_after.append(s_wfr.n)

            for tt in range(TPT):
                dl = (cgi5 * TPT + tt) % 2
                trow = th * c.CHT + tt * 128
                extra = st3_finals if (cgi5 == 0 and tt == 0) else []
                DMA("gpsimd", gld5[dl], g_s[trow:trow + 128, oc * 512:(oc + 1) * 512],
                    gx[dl], (s_wv1, max(0, s_wv1.n - 1)), *extra)
                DMA("gpsimd", xld5[dl], xsh[trow:trow + 128, oc * 512:(oc + 1) * 512], gx[dl])

                def f_wv(e, par=par, tt=tt, dl=dl,
                         w=W((s_wfr, s_wfr.n), (gx[dl], gx[dl].n),
                             (yst[dl], yst[dl].n))):
                    w(e)
                    e.tensor_tensor(t1s[dl], bank(par + tt), gld5[dl], OP.mult)
                    e.drain()
                    e.tensor_tensor(ost[dl], t1s[dl], xld5[dl], OP.add).then_inc(s_wv1.h, 1)
                em("vector", f_wv)
                s_wv1.n += 1
                DMA("gpsimd", y[trow:trow + 128, oc * 512:(oc + 1) * 512], ost[dl],
                    yst[dl], (s_wv1, s_wv1.n))

    def f_fin(e, w=W((yst[0], yst[0].n), (yst[1], yst[1].n))):
        w(e)
    em("gpsimd", f_fin)
    quiesce("sync", ldw + ldwl)
    quiesce("gpsimd", gx + lat)
    run_phase()
    return nc


def shard_inputs(c, L0_attn_hidden_states, L0_attn_Wqkv, L0_attn_q_ln_w, L0_attn_q_ln_b,
                 L0_attn_k_ln_w, L0_attn_k_ln_b, L0_attn_Wout, L0_ffn_base_w,
                 L0_ffn_base_b, L0_ffn_A, L0_ffn_B, L0_ffn_router_w):
    x = np.asarray(L0_attn_hidden_states, dtype=np.float32)
    com = {
        "wqkvT": np.ascontiguousarray(np.asarray(L0_attn_Wqkv, np.float32).T),
        "woutT": np.ascontiguousarray(np.asarray(L0_attn_Wout, np.float32).T),
        "basewT": np.ascontiguousarray(np.asarray(L0_ffn_base_w, np.float32).T),
        "baseb": np.asarray(L0_ffn_base_b, np.float32).reshape(1, c.D),
        "aT": np.ascontiguousarray(np.asarray(L0_ffn_A, np.float32).T),
        "routerT": np.ascontiguousarray(np.asarray(L0_ffn_router_w, np.float32).T),
        "bflat": np.ascontiguousarray(
            np.asarray(L0_ffn_B, np.float32).transpose(0, 2, 1).reshape(c.ER, c.D)),
        "qlnw": np.asarray(L0_attn_q_ln_w, np.float32).reshape(1, c.D),
        "qlnb": np.asarray(L0_attn_q_ln_b, np.float32).reshape(1, c.D),
        "klnw": np.asarray(L0_attn_k_ln_w, np.float32).reshape(1, c.D),
        "klnb": np.asarray(L0_attn_k_ln_b, np.float32).reshape(1, c.D),
    }
    in_maps = []
    for core in range(c.NC):
        b, r = core // c.R4, core % c.R4
        sh = dict(com)
        sh["xsh"] = np.ascontiguousarray(x[b, r * c.TOK:(r + 1) * c.TOK, :])
        sh["qpos"] = np.arange(r * c.TOK, (r + 1) * c.TOK, dtype=np.float32).reshape(c.TOK, 1)
        hpn = c.H // c.NC
        sl = alibi_slopes(c.H)[core * hpn:(core + 1) * hpn]
        sh["slv"] = np.ascontiguousarray(np.broadcast_to(
            np.tile(sl, c.B).reshape(1, c.B * hpn), (128, c.B * hpn)))
        in_maps.append(sh)
    return in_maps


def gather_out(c, results):
    out = np.empty((c.B, c.S, c.D), np.float32)
    for core in range(c.NC):
        b, r = core // c.R4, core % c.R4
        out[b, r * c.TOK:(r + 1) * c.TOK, :] = results[core]["y"]
    return out


def kernel(**inputs):
    c = Cfg()
    in_maps = shard_inputs(c, **inputs)
    nc = build_program(c)
    res = run_bass_kernel_spmd(nc, in_maps, core_ids=list(range(c.NC)))
    return gather_out(c, res.results)
